# version 07
# speedup vs baseline: 1.2522x; 1.0350x over previous
import numpy as np
import ml_dtypes
from contextlib import ExitStack
import concourse.bass as bass
import concourse.mybir as mybir
from concourse.alu_op_type import AluOpType as OP
from concourse.bass_utils import run_bass_kernel_spmd

F32 = mybir.dt.float32
BF16 = mybir.dt.bfloat16
I32 = mybir.dt.int32
U32 = mybir.dt.uint32
AF = mybir.ActivationFunctionType
AX = mybir.AxisListType

NCORES = 8
SEQ = 16384
D = 1024
TOK = SEQ // NCORES
NT = TOK // 128
BF = ml_dtypes.bfloat16

ENGS = ("t", "v", "s", "g", "q")
SEMS = ("t", "v", "s", "g", "qd", "gd", "cc", "od")


class Ctx:

    def __init__(self, nc):
        self.nc = nc
        self.eng = {"t": nc.tensor, "v": nc.vector, "s": nc.scalar,
                    "g": nc.gpsimd, "q": nc.sync}
        self.progs = {k: [] for k in ENGS}
        self.cnt = {k: 0 for k in SEMS}
        self.seen = {e: {} for e in ENGS}
        self.sem = {}
        self.stack = None

    def alloc_sems(self, stack):
        for k in SEMS:
            self.sem[k] = stack.enter_context(self.nc.semaphore(f"sem_{k}"))
        self.scope = ""

    def sb(self, name, shape, dtype, **kw):
        return self.nc.sbuf_tensor(self.scope + name, shape, dtype, **kw)

    def ps(self, name, shape, dtype, **kw):
        return self.nc.psum_tensor(self.scope + name, shape, dtype, **kw)

    def dr(self, name, *a, **kw):
        return self.nc.dram_tensor(self.scope + name, *a, **kw)

    def run(self, eng, f, inc=None, n=1):
        if eng in ("v", "s", "g"):
            self.wait(eng, eng)
        if inc is not None:
            self.cnt[inc] += n
            tgt = self.cnt[inc]
            sem = self.sem

            def thunk(E, f=f, inc=inc, n=n):
                inst = f(E)
                inst.then_inc(sem[inc], n)
        else:
            def thunk(E, f=f):
                f(E)
        self.progs[eng].append(thunk)
        return self.cnt[inc] if inc is not None else None

    def wait(self, eng, key, val=None):
        val = self.cnt[key] if val is None else val
        if val <= 0 or self.seen[eng].get(key, 0) >= val:
            return
        self.seen[eng][key] = val
        sem = self.sem
        self.progs[eng].append(lambda E, key=key, val=val: E.wait_ge(sem[key], val))

    def barrier_from(self, eng, keys):
        for k in keys:
            self.wait(eng, k)

    def flush(self):
        names = {"t": "tensor", "v": "vector", "s": "scalar", "g": "gpsimd", "q": "sync"}
        with self.nc.Block() as block:
            for e in ENGS:
                prog = self.progs[e]
                if not prog:
                    continue

                def section(E, prog=prog):
                    for th in prog:
                        th(E)
                getattr(block, names[e])(section)

    def dma(self, queue, out, in_, inc=None):
        inc = inc or ("qd" if queue == "q" else "gd")
        return self.run(queue, lambda E: E.dma_start(out=out, in_=in_), inc=inc, n=16)

    def ts(self, out, in0, s1, op0, s2=None, op1=None, accum=None, eng="v"):
        kw = {}
        if op1 is not None:
            kw["op1"] = op1
        return self.run(eng, lambda E: E.tensor_scalar(
            out=out, in0=in0, scalar1=s1, scalar2=s2, op0=op0,
            accum_out=accum, **kw), inc=eng)

    def tt(self, out, in0, in1, op, eng="v"):
        return self.run(eng, lambda E: E.tensor_tensor(out=out, in0=in0, in1=in1, op=op),
                        inc=eng)

    def stt(self, out, in0, scalar, in1, op0, op1, accum=None, eng="v"):
        return self.run(eng, lambda E: E.scalar_tensor_tensor(
            out=out, in0=in0, scalar=scalar, in1=in1, op0=op0, op1=op1,
            accum_out=accum), inc=eng)

    def cp(self, out, in_, eng="v"):
        if eng == "s":
            return self.run("s", lambda E: E.copy(out=out, in_=in_), inc="s")
        return self.run(eng, lambda E: E.tensor_copy(out=out, in_=in_), inc=eng)

    def act(self, out, in_, func, bias=None, scale=None, accum=None):
        kw = {}
        if bias is not None:
            kw["bias"] = bias
        if scale is not None:
            kw["scale"] = scale
        return self.run("s", lambda E: E.activation(out=out, in_=in_, func=func,
                                                    accum_out=accum, **kw), inc="s")

    def red(self, out, in_, op, absval=False, negate=False, ax=None):
        kw = {}
        if absval:
            kw["apply_absolute_value"] = True
        if negate:
            kw["negate"] = True
        ax = ax or AX.X
        return self.run("v", lambda E: E.tensor_reduce(out=out, in_=in_, axis=ax,
                                                       op=op, **kw), inc="v")

    def mm(self, out, lhsT, rhs, start, stop, inc=False):
        self.wait("t", "g")
        return self.run("t", lambda E: E.matmul(out, lhsT, rhs, start=start, stop=stop),
                        inc="t" if inc else None)

    def transpose(self, out_psum, in_, ident, inc=True):
        self.wait("t", "g")
        return self.run("t", lambda E: E.transpose(out_psum, in_, ident),
                        inc="t" if inc else None)



def emit_prelude(ctx, stack):
    nc = ctx.nc
    ident = stack.enter_context(nc.sbuf_tensor("c_ident", [128, 128], BF16))
    ones = stack.enter_context(nc.sbuf_tensor("c_ones", [1, 128], BF16))
    ii = stack.enter_context(nc.sbuf_tensor("c_iota", [128, 128], I32))
    ctx.run("g", lambda E: E.iota(ii[:], pattern=[[-1, 128]], base=0,
                                  channel_multiplier=1), inc="g")
    ctx.run("g", lambda E: E.memset(ones[:], 1.0), inc="g")
    ctx.wait("v", "g")
    ctx.ts(ident[:], ii[:], 0.0, OP.is_equal)
    return ident, ones


def emit_ln(ctx, x_ap, out_bf, scr_fp32, mcol, eps=1e-5, out_f32=None,
            gain_bc=None, bias_bc=None):
    n = x_ap.shape[-1]
    ctx.red(mcol[:, 0:1], x_ap, OP.add)
    ctx.ts(mcol[:, 0:1], mcol[:, 0:1], 1.0 / n, OP.mult)
    ctx.ts(scr_fp32, x_ap, mcol[:, 0:1], OP.subtract)
    ctx.wait("s", "v")
    ctx.act(out_bf, scr_fp32, AF.Square, accum=mcol[:, 1:2])
    ctx.wait("v", "s")
    ctx.ts(mcol[:, 1:2], mcol[:, 1:2], 1.0 / n, OP.mult, eps, OP.add)
    ctx.wait("s", "v")
    ctx.act(mcol[:, 2:3], mcol[:, 1:2], AF.Sqrt)
    ctx.wait("v", "s")
    ctx.run("v", lambda E: E.reciprocal(out=mcol[:, 3:4], in_=mcol[:, 2:3]), inc="v")
    if gain_bc is None:
        ctx.ts(out_bf, scr_fp32, mcol[:, 3:4], OP.mult)
        if out_f32 is not None:
            ctx.ts(out_f32, scr_fp32, mcol[:, 3:4], OP.mult)
    else:
        tgt = out_f32 if out_f32 is not None else scr_fp32
        ctx.ts(scr_fp32, scr_fp32, mcol[:, 3:4], OP.mult)
        if bias_bc is not None:
            ctx.tt(scr_fp32, scr_fp32, gain_bc, OP.mult)
            ctx.tt(tgt if out_f32 is not None else scr_fp32, scr_fp32, bias_bc, OP.add)
            src = tgt if out_f32 is not None else scr_fp32
            ctx.cp(out_bf, src)
        else:
            ctx.tt(tgt, scr_fp32, gain_bc, OP.mult)
            ctx.cp(out_bf, tgt)
    return ctx.cnt["v"]


_SQUARE_DUMP = {}


def emit_transpose(ctx, src_ap, dst_ap, ident, psum_bf):
    p_out, f_out = src_ap.shape[-1], src_ap.shape[0]
    ctx.wait("t", "v")
    ctx.wait("t", "s")
    t_at = ctx.transpose(psum_bf[0:p_out, 0:f_out], src_ap,
                         ident[0:src_ap.shape[0], 0:src_ap.shape[0]])
    ctx.wait("v", "t", t_at)
    ctx.cp(dst_ap, psum_bf[0:p_out, 0:f_out])


def emit_bcast_row(ctx, row_ap, out_ap, ones, psum_f32, n=None):
    F = row_ap.shape[-1]
    assert F <= 512
    t_at = ctx.mm(psum_f32[:, 0:F], ones[:], row_ap, start=True, stop=True, inc=True)
    ctx.wait("v", "t", t_at)
    ctx.cp(out_ap, psum_f32[:, 0:F])


def emit_quantize_tile(ctx, X, XD, scr):
    y, b1, b2, t1 = scr["y"], scr["b1"], scr["b2"], scr["t1"]
    ints = scr["ints"]
    amax, s, inv = scr["amax"], scr["s"], scr["inv"]
    ctx.wait("v", "g")
    ctx.wait("v", "s")
    x3 = X.rearrange("p (b j) -> p b j", j=32)
    ctx.red(amax[:], x3, OP.max, absval=True)
    ctx.ts(s[:], amax[:], 1.0 / 7.5, OP.mult, 1e-30, OP.max)
    ctx.run("v", lambda E: E.reciprocal(out=inv[:], in_=s[:]), inc="v")
    for b in range(32):
        ctx.ts(y[:, 32 * b:32 * b + 32], X[:, 32 * b:32 * b + 32],
               inv[:, b:b + 1], OP.mult)
    ctx.tt(t1[:], y[:], y[:], OP.mult)
    ctx.ts(b1[:], t1[:], 4.0, OP.is_lt)
    ctx.ts(b2[:], t1[:], 16.0, OP.is_lt)
    ctx.ts(t1[:], b1[:], 1.0, OP.add, 2.0, OP.mult)
    ctx.stt(b2[:], b2[:], 1.0, t1[:], OP.add, OP.mult)
    ctx.tt(y[:], y[:], b2[:], OP.mult)
    ctx.cp(ints[:], y[:])
    ctx.cp(t1[:], ints[:])
    ctx.tt(b1[:], y[:], t1[:], OP.subtract)
    ctx.ts(y[:], b1[:], 0.5, OP.is_ge)
    ctx.stt(t1[:], y[:], 1.0, t1[:], OP.mult, OP.add)
    ctx.ts(y[:], b1[:], -0.5, OP.is_le)
    ctx.stt(t1[:], y[:], -1.0, t1[:], OP.mult, OP.add)
    ctx.run("v", lambda E: E.reciprocal(out=b2[:], in_=b2[:]), inc="v")
    ctx.wait("v", "s")
    ctx.tt(t1[:], t1[:], b2[:], OP.mult)
    ctx.ts(t1[:], t1[:], 7.5, OP.min, -7.5, OP.max)
    for b in range(32):
        ctx.ts(XD[:, 32 * b:32 * b + 32], t1[:, 32 * b:32 * b + 32],
               s[:, b:b + 1], OP.mult)
    return ctx.cnt["v"]



def emit_L0f(ctx, stack, x_in, y_out):
    nc = ctx.nc
    E = stack.enter_context
    ident, ones = ctx.ident, ctx.ones
    w_in = ctx.dr("w", [D, D], F32, kind="ExternalInput").ap()
    b_in = ctx.dr("b", [1, D], F32, kind="ExternalInput").ap()

    xs = E(ctx.sb("xs", [128, NT * D], F32))
    wf = E(ctx.sb("wf", [128, 8 * D], F32))
    wd = E(ctx.sb("wd", [128, 8 * D], BF16))
    wdT = E(ctx.sb("wdT", [128, 8 * D], BF16))
    brow = E(ctx.sb("brow", [1, D], BF16))
    h = E(ctx.sb("h", [128, D], F32))
    xd = E(ctx.sb("xd", [128, D], BF16))
    xdT = E(ctx.sb("xdT", [128, D], BF16))
    mcol = E(ctx.sb("mcol", [128, 4], F32))
    scr = {
        "y": E(ctx.sb("q_y", [128, D], F32)),
        "b1": E(ctx.sb("q_b1", [128, D], F32)),
        "b2": E(ctx.sb("q_b2", [128, D], F32)),
        "t1": E(ctx.sb("q_t1", [128, D], F32)),
        "ints": E(ctx.sb("q_i", [128, D], I32)),
        "amax": E(ctx.sb("q_am", [128, 32], F32)),
        "s": E(ctx.sb("q_s", [128, 32], F32)),
        "inv": E(ctx.sb("q_iv", [128, 32], F32)),
    }
    ptp = E(ctx.ps("ptp", [128, 128], BF16))
    pmm = E(ctx.ps("pmm", [128, 1024], F32))

    x_t = x_in.rearrange("(n p) d -> n p d", p=128)
    w_t = w_in.rearrange("(c p) d -> c p d", p=128)
    for i in range(NT):
        ctx.dma("q", xs[:, i * D:(i + 1) * D], x_t[i])
    for c in range(8):
        ctx.dma("q", wf[:, c * D:(c + 1) * D], w_t[c])
    ctx.dma("g", brow[:], b_in)
    ld_at = ctx.cnt["qd"]

    ctx.wait("v", "qd", ld_at)
    for c in range(8):
        ctx.wait("g", "qd", ld_at)
        ctx.wait("s", "qd", ld_at)
        emit_quantize_tile(ctx, wf[:, c * D:(c + 1) * D], wd[:, c * D:(c + 1) * D], scr)
    for r in range(8):
        for c in range(8):
            ctx.wait("t", "v")
            emit_transpose(ctx, wd[:, r * D + 128 * c: r * D + 128 * (c + 1)],
                           wdT[:, c * D + 128 * r: c * D + 128 * (r + 1)], ident, ptp)

    for i in range(NT):
        xt = xs[:, i * D:(i + 1) * D]
        emit_ln(ctx, xt, xd[:], h[:], mcol)
        v_ln = ctx.cnt["v"]
        ctx.ts(scr["t1"][:], h[:], mcol[:, 3:4], OP.mult)
        ctx.cp(h[:], scr["t1"][:])
        emit_quantize_tile(ctx, h[:], xd[:], scr)
        v_q = ctx.cnt["v"]
        for c in range(8):
            ctx.wait("t", "v", v_q)
            emit_transpose(ctx, xd[:, 128 * c:128 * (c + 1)],
                           xdT[:, 128 * c:128 * (c + 1)], ident, ptp)
        v_tp = ctx.cnt["v"]
        ctx.wait("t", "gd")
        ctx.wait("t", "v", v_tp)
        for f in range(2):
            ctx.mm(pmm[:, f * 512:(f + 1) * 512], ones[:], brow[:, f * 512:(f + 1) * 512],
                   start=True, stop=False)
            for c in range(8):
                ctx.mm(pmm[:, f * 512:(f + 1) * 512], xdT[:, 128 * c:128 * (c + 1)],
                       wdT[:, c * D + f * 512: c * D + (f + 1) * 512],
                       start=False, stop=(c == 7), inc=(c == 7))
        t_mm = ctx.cnt["t"]
        ctx.wait("v", "t", t_mm)
        ctx.tt(xt, xt, pmm[:], OP.add)
        ctx.wait("g", "v")
        ctx.dma("g", y_out.rearrange("(n p) d -> n p d", p=128)[i, :, :], xt, inc="od")





def build_L0f():
    nc = bass.Bass("TRN2", target_bir_lowering=False, debug=False)
    x_in = nc.dram_tensor("x", [TOK, D], F32, kind="ExternalInput").ap()
    y_out = nc.dram_tensor("y", [TOK, D], F32, kind="ExternalOutput").ap()
    ctx = Ctx(nc)
    with ExitStack() as stack:
        ctx.alloc_sems(stack)
        ctx.ident, ctx.ones = emit_prelude(ctx, stack)
        ctx.scope = "f0_"
        emit_L0f(ctx, stack, x_in, y_out)
        ctx.wait("q", "od", ctx.cnt["od"])
        ctx.flush()
    return nc

def emit_transpose_tile8(ctx, src_bf, dst_bf, ident, ptp, n=8):
    grp = max(1, min(4, ptp[:].shape[-1] // 128))
    c = 0
    while c < n:
        g = min(grp, n - c)
        ctx.wait("t", "v")
        ctx.wait("t", "s")
        for j in range(g):
            t_at = ctx.transpose(ptp[:, 128 * j:128 * (j + 1)],
                                 src_bf[:, 128 * (c + j):128 * (c + j + 1)],
                                 ident[0:128, 0:128])
        ctx.wait("v", "t", t_at)
        ctx.cp(dst_bf[:, 128 * c:128 * (c + g)], ptp[:, 0:128 * g])
        c += g


def emit_proj(ctx, xT, W, psum, f_sizes, nc_=8, bias=None, ones=None, xoff=0,
              close=True):
    ctx.wait("t", "v")
    ctx.wait("t", "s")
    f0 = 0
    for fs in f_sizes:
        if bias is not None:
            ctx.mm(psum[:, f0:f0 + fs], ones[:], bias[:, f0:f0 + fs],
                   start=True, stop=False)
        for c in range(nc_):
            F = W.shape[-1] // nc_
            ctx.mm(psum[:, f0:f0 + fs], xT[:, xoff + 128 * c:xoff + 128 * (c + 1)],
                   W[:, c * F + f0:c * F + f0 + fs],
                   start=(c == 0 and bias is None),
                   stop=(c == nc_ - 1 and close),
                   inc=(c == nc_ - 1 and close))
        f0 += fs


def emit_L3a(ctx, stack, x_in, y_out, dbg=False):
    nc = ctx.nc
    E = stack.enter_context
    ident, ones = ctx.ident, ctx.ones
    if dbg:
        d_h = ctx.dr("d_h", [128, D], F32, kind="ExternalOutput").ap()
        d_v = ctx.dr("d_v", [128, D], F32, kind="ExternalOutput").ap()
        d_t2 = ctx.dr("d_t2", [128, 64], F32, kind="ExternalOutput").ap()
        d_gate = ctx.dr("d_gate", [128, 16], F32, kind="ExternalOutput").ap()
        d_vg = ctx.dr("d_vg", [128, D], F32, kind="ExternalOutput").ap()
    wv_in = ctx.dr("wvT", [D, D], F32, kind="ExternalInput").ap()
    w1_in = ctx.dr("w1T", [D, D], F32, kind="ExternalInput").ap()
    w2_in = ctx.dr("w2T", [D, 64], F32, kind="ExternalInput").ap()
    wo_in = ctx.dr("woT", [D, D], F32, kind="ExternalInput").ap()
    sig_in = ctx.dr("sigT", [64, 16], F32, kind="ExternalInput").ap()
    bv_in = ctx.dr("bv", [1, D], F32, kind="ExternalInput").ap()
    b1_in = ctx.dr("b1", [1, D], F32, kind="ExternalInput").ap()
    b2_in = ctx.dr("b2", [1, 64], F32, kind="ExternalInput").ap()
    bo_in = ctx.dr("bo", [1, D], F32, kind="ExternalInput").ap()

    xs = E(ctx.sb("xs", [128, NT * D], F32))
    wv = E(ctx.sb("wv", [128, 8 * D], BF16))
    w1 = E(ctx.sb("w1", [128, 8 * D], BF16))
    w2 = E(ctx.sb("w2", [128, 8 * 64], BF16))
    wo = E(ctx.sb("wo", [128, 8 * D], BF16))
    sig = E(ctx.sb("sig", [64, 16], BF16))
    brows = E(ctx.sb("brows", [1, 3 * D + 64], BF16))
    h = E(ctx.sb("h", [128, D], BF16))
    hf = E(ctx.sb("hf", [128, D], F32))
    hT = E(ctx.sb("hT", [128, D], BF16))
    vt = E(ctx.sb("vt", [128, D], F32))
    t1 = E(ctx.sb("t1", [128, D], BF16))
    t1T = E(ctx.sb("t1T", [128, D], BF16))
    t2 = E(ctx.sb("t2", [128, 64], BF16))
    t2T = E(ctx.sb("t2T", [64, 128], BF16))
    gate = E(ctx.sb("gate", [128, 16], F32))
    vg = E(ctx.sb("vg", [128, D], BF16))
    vgT = E(ctx.sb("vgT", [128, D], BF16))
    mcol = E(ctx.sb("mcol", [128, 8], F32))
    ptp = E(ctx.ps("ptp", [128, 128], BF16))
    pmm = E(ctx.ps("pmm", [128, 1024], F32))
    pmm2 = E(ctx.ps("pmm2", [128, 1024], F32))
    psm = E(ctx.ps("psm", [128, 128], F32))

    x_t = x_in.rearrange("(n p) d -> n p d", p=128)
    y_t = y_out.rearrange("(n p) d -> n p d", p=128)
    for i in range(NT):
        ctx.dma("q", xs[:, i * D:(i + 1) * D], x_t[i])
    for w_sb, w_ap in [(wv, wv_in), (w1, w1_in), (wo, wo_in)]:
        wt = w_ap.rearrange("(c p) d -> c p d", p=128)
        for c in range(8):
            ctx.dma("g", w_sb[:, c * D:(c + 1) * D], wt[c])
    w2t = w2_in.rearrange("(c p) d -> c p d", p=128)
    for c in range(8):
        ctx.dma("g", w2[:, c * 64:(c + 1) * 64], w2t[c])
    ctx.dma("g", sig[:], sig_in)
    ctx.dma("g", brows[:, 0:D], bv_in)
    ctx.dma("g", brows[:, D:2 * D], b1_in)
    ctx.dma("g", brows[:, 2 * D:2 * D + 64], b2_in)
    ctx.dma("g", brows[:, 2 * D + 64:3 * D + 64], bo_in)
    ld_q, ld_g = ctx.cnt["qd"], ctx.cnt["gd"]
    ctx.wait("v", "qd", ld_q)
    ctx.wait("t", "gd", ld_g)
    bv_r = brows[:, 0:D]
    b1_r = brows[:, D:2 * D]
    b2_r = brows[:, 2 * D:2 * D + 64]
    bo_r = brows[:, 2 * D + 64:3 * D + 64]

    for i in range(NT):
        xt = xs[:, i * D:(i + 1) * D]
        emit_ln(ctx, xt, h[:], hf[:], mcol)
        ctx.wait("t", "v")
        emit_transpose_tile8(ctx, h[:], hT[:], ident, ptp)
        emit_proj(ctx, hT[:], wv[:], pmm, [512, 512], bias=bv_r, ones=ones)
        ctx.wait("v", "t")
        ctx.cp(vt[:], pmm[:])
        emit_proj(ctx, hT[:], w1[:], pmm2, [512, 512], bias=b1_r, ones=ones)
        ctx.wait("s", "t")
        ctx.act(t1[:], pmm2[:], AF.Relu)
        ctx.wait("t", "s")
        emit_transpose_tile8(ctx, t1[:], t1T[:], ident, ptp)
        emit_proj(ctx, t1T[:], w2[:], psm, [64], bias=b2_r, ones=ones)
        ctx.wait("v", "t")
        ctx.cp(t2[:], psm[:, 0:64])
        ctx.wait("t", "v")
        emit_transpose(ctx, t2[:], t2T[:], ident, ptp)
        ctx.wait("t", "v")
        lg_at = ctx.mm(psm[:, 0:16], t2T[:], sig[:], start=True, stop=True, inc=True)
        ctx.wait("v", "t", lg_at)
        ctx.red(mcol[:, 4:5], psm[:, 0:16], OP.max)
        ctx.ts(mcol[:, 4:5], mcol[:, 4:5], -1.0, OP.mult)
        ctx.wait("s", "v")
        ctx.act(gate[:], psm[:, 0:16], AF.Exp, bias=mcol[:, 4:5], accum=mcol[:, 5:6])
        ctx.wait("v", "s")
        ctx.run("v", lambda E: E.reciprocal(out=mcol[:, 6:7], in_=mcol[:, 5:6]), inc="v")
        ctx.ts(gate[:], gate[:], mcol[:, 6:7], OP.mult)
        for hh in range(16):
            ctx.ts(vg[:, 64 * hh:64 * (hh + 1)], vt[:, 64 * hh:64 * (hh + 1)],
                   gate[:, hh:hh + 1], OP.mult)
        if dbg and i == 0:
            ctx.wait("g", "v")
            ctx.wait("g", "s")
            ctx.cp(hf[:], h[:])
            ctx.wait("g", "v")
            ctx.dma("g", d_h, hf[:], inc="gd")
            ctx.dma("g", d_v, vt[:], inc="gd")
            ctx.cp(hf[:, 0:64], t2[:])
            ctx.wait("g", "v")
            ctx.dma("g", d_t2, hf[:, 0:64], inc="gd")
            ctx.dma("g", d_gate, gate[:], inc="gd")
            ctx.cp(hf[:], vg[:])
            ctx.wait("g", "v")
            ctx.dma("g", d_vg, hf[:], inc="gd")
            ctx.wait("v", "gd")
        ctx.wait("t", "v")
        emit_transpose_tile8(ctx, vg[:], vgT[:], ident, ptp)
        emit_proj(ctx, vgT[:], wo[:], pmm, [512, 512], bias=bo_r, ones=ones)
        ctx.wait("v", "t")
        ctx.tt(xt, xt, pmm[:], OP.add)
        ctx.wait("g", "v")
        ctx.dma("g", y_t[i, :, :], xt, inc="od")




def build_L3a(dbg=False):
    nc = bass.Bass("TRN2", target_bir_lowering=False, debug=False)
    x_in = nc.dram_tensor("x", [TOK, D], F32, kind="ExternalInput").ap()
    y_out = nc.dram_tensor("y", [TOK, D], F32, kind="ExternalOutput").ap()
    ctx = Ctx(nc)
    with ExitStack() as stack:
        ctx.alloc_sems(stack)
        ctx.ident, ctx.ones = emit_prelude(ctx, stack)
        ctx.scope = "a3_"
        emit_L3a(ctx, stack, x_in, y_out, dbg)
        ctx.wait("q", "od", ctx.cnt["od"])
        ctx.flush()
    return nc

def run_L3a(x_full, INP):
    nc = build_L3a()
    xs = _shard_x(x_full)
    T = lambda a: np.ascontiguousarray(np.asarray(a, np.float32).T)
    R = lambda a, s: np.ascontiguousarray(np.asarray(a, np.float32).reshape(s))
    common = dict(
        wvT=T(INP["L3_attn_Wv"]), w1T=T(INP["L3_attn_W1"]),
        w2T=T(INP["L3_attn_W2"]), woT=T(INP["L3_attn_Wo"]),
        sigT=T(INP["L3_attn_head_signatures"]),
        bv=R(INP["L3_attn_bv"], (1, D)), b1=R(INP["L3_attn_b1"], (1, D)),
        b2=R(INP["L3_attn_b2"], (1, 64)), bo=R(INP["L3_attn_bo"], (1, D)),
    )
    maps = [dict(x=xs[r], **common) for r in range(NCORES)]
    res = run_bass_kernel_spmd(nc, _prefix_maps(maps, "a3_"), core_ids=list(range(NCORES)))
    return np.concatenate([res.results[r]["y"] for r in range(NCORES)], 0)



DBGSER = False

def emit_L1f(ctx, stack, x_in, y_out, dbg=False):
    nc = ctx.nc
    E = stack.enter_context
    ident, ones = ctx.ident, ctx.ones
    rmsw_in = ctx.dr("rmsw", [1, D], F32, kind="ExternalInput").ap()
    gw_in = ctx.dr("gwT", [D, 8], F32, kind="ExternalInput").ap()
    w1_in = ctx.dr("w1", [8, D, 4096], F32, kind="ExternalInput").ap()
    b1_in = ctx.dr("b1", [8, 4096], F32, kind="ExternalInput").ap()
    w2_in = ctx.dr("w2", [8, 4096, D], F32, kind="ExternalInput").ap()
    b2_in = ctx.dr("b2", [8, D], F32, kind="ExternalInput").ap()
    if dbg:
        d_S = ctx.dr("d_S", [128, NT * 8], F32, kind="ExternalOutput").ap()
        d_xn = ctx.dr("d_xn", [128, D], F32, kind="ExternalOutput").ap()

    xs = E(ctx.sb("xs", [128, NT * D], F32))
    xnT = E(ctx.sb("xnT", [128, NT * D], BF16))
    S = E(ctx.sb("S", [128, NT * 8], F32))
    gw = E(ctx.sb("gw", [128, 8 * 8], BF16))
    gwf = E(ctx.sb("gwf", [128, 8 * 8], F32))
    gwlo = E(ctx.sb("gwlo", [128, 8 * 8], BF16))
    xnlo = E(ctx.sb("xnlo", [128, D], BF16))
    xnloT = E(ctx.sb("xnloT", [128, D], BF16))
    rms_bc = E(ctx.sb("rms_bc", [128, D], F32))
    rmsrow = E(ctx.sb("rmsrow", [1, D], BF16))
    ones8 = E(ctx.sb("ones8", [128, 8], F32))
    w1sb = [E(nc.sbuf_tensor(f"w1sb{b}", [128, 8 * D], BF16)) for b in range(2)]
    w2sb = [E(nc.sbuf_tensor(f"w2sb{b}", [128, 8 * D], BF16)) for b in range(2)]
    b1row = [E(nc.sbuf_tensor(f"b1row{b}", [1, D], BF16)) for b in range(2)]
    b2row = [E(nc.sbuf_tensor(f"b2row{b}", [1, D], BF16)) for b in range(2)]
    hf = E(ctx.sb("hf", [128, D], F32))
    xnbf = E(ctx.sb("xnbf", [128, D], BF16))
    hbf = E(ctx.sb("hbf", [128, D], BF16))
    hT = E(ctx.sb("hT", [128, D], BF16))
    sel = E(ctx.sb("sel", [128, 8], F32))
    csum = E(ctx.sb("csum", [128, 8], F32))
    mcol = E(ctx.sb("mcol", [128, 8], F32))
    ptp = E(ctx.ps("ptp", [128, 128], BF16))
    ph = E(ctx.ps("ph", [128, 1024], F32))
    py = E(ctx.ps("py", [128, 1024], F32))
    plog = E(ctx.ps("plog", [128, 16], F32))

    x_t = x_in.rearrange("(n p) d -> n p d", p=128)
    y_t = y_out.rearrange("(n p) d -> n p d", p=128)
    for i in range(NT):
        ctx.dma("q", xs[:, i * D:(i + 1) * D], x_t[i])
    gw_t = gw_in.rearrange("(c p) e -> c p e", p=128)
    for c in range(8):
        ctx.dma("g", gwf[:, c * 8:(c + 1) * 8], gw_t[c])
    ctx.wait("v", "gd")
    ctx.cp(gw[:], gwf[:])
    ctx.tt(gwlo[:], gwf[:], gw[:], OP.subtract)
    ctx.dma("g", rmsrow[:], rmsw_in)
    ctx.run("g", lambda E_: E_.memset(ones8[:], 1.0), inc="g")
    ld_q, ld_g = ctx.cnt["qd"], ctx.cnt["gd"]
    ctx.wait("t", "gd", ld_g)
    for f in range(2):
        bc_at = ctx.mm(ph[:, 512 * f:512 * (f + 1)], ones[:],
                       rmsrow[:, 512 * f:512 * (f + 1)], start=True, stop=True,
                       inc=True)
    ctx.wait("v", "t")
    ctx.cp(rms_bc[:], ph[:])
    ctx.wait("v", "qd", ld_q)

    for i in range(NT):
        xt = xs[:, i * D:(i + 1) * D]
        emit_ln(ctx, xt, xnbf[:], hf[:], mcol)
        ctx.ts(hf[:], hf[:], mcol[:, 3:4], OP.mult)
        ctx.wait("s", "v")
        ctx.act(xnbf[:], hf[:], AF.Square, accum=mcol[:, 4:5])
        ctx.wait("v", "s")
        ctx.ts(mcol[:, 4:5], mcol[:, 4:5], 1.0 / D, OP.mult, 1e-6, OP.add,
               eng="v")
        ctx.wait("s", "v")
        ctx.act(mcol[:, 5:6], mcol[:, 4:5], AF.Sqrt)
        ctx.wait("v", "s")
        ctx.run("v", lambda E_: E_.reciprocal(out=mcol[:, 6:7], in_=mcol[:, 5:6]),
                inc="v")
        ctx.ts(hf[:], hf[:], mcol[:, 6:7], OP.mult)
        ctx.tt(hf[:], hf[:], rms_bc[:], OP.mult)
        ctx.cp(xnbf[:], hf[:])
        ctx.tt(xnlo[:], hf[:], xnbf[:], OP.subtract)
        ctx.wait("t", "v")
        emit_transpose_tile8(ctx, xnbf[:], xnT[:, i * D:(i + 1) * D], ident, ptp)
        emit_transpose_tile8(ctx, xnlo[:], xnloT[:], ident, ptp)
        ctx.wait("t", "v")
        for c in range(8):
            ctx.mm(plog[:, 0:8], xnT[:, i * D + 128 * c:i * D + 128 * (c + 1)],
                   gw[:, c * 8:(c + 1) * 8], start=(c == 0), stop=False)
        for c in range(8):
            ctx.mm(plog[:, 0:8], xnloT[:, 128 * c:128 * (c + 1)],
                   gw[:, c * 8:(c + 1) * 8], start=False, stop=False)
        for c in range(8):
            ctx.mm(plog[:, 0:8], xnT[:, i * D + 128 * c:i * D + 128 * (c + 1)],
                   gwlo[:, c * 8:(c + 1) * 8], start=False, stop=(c == 7),
                   inc=(c == 7))
        ctx.wait("v", "t")
        ctx.red(mcol[:, 0:1], plog[:, 0:8], OP.max)
        ctx.ts(sel[:], plog[:, 0:8], mcol[:, 0:1], OP.is_equal)
        ctx.ts(mcol[:, 1:2], mcol[:, 0:1], -1.0, OP.mult)
        ctx.wait("s", "v")
        ctx.act(csum[:], plog[:, 0:8], AF.Exp, bias=mcol[:, 1:2],
                accum=mcol[:, 2:3])
        ctx.wait("v", "s")
        ctx.run("v", lambda E_: E_.reciprocal(out=mcol[:, 3:4], in_=mcol[:, 2:3]),
                inc="v")
        ctx.run("v", lambda E_: E_.tensor_tensor_scan(
            out=csum[:], data0=ones8[:], data1=sel[:], initial=0.0,
            op0=OP.mult, op1=OP.add), inc="v")
        ctx.ts(csum[:], csum[:], 1.0, OP.is_le)
        ctx.tt(sel[:], sel[:], csum[:], OP.mult)
        ctx.ts(S[:, i * 8:(i + 1) * 8], sel[:], mcol[:, 3:4], OP.mult)

    if dbg:
        ctx.wait("g", "v")
        ctx.dma("g", d_S, S[:], inc="gd")
        ctx.cp(hf[:], xnT[:, 0:D])
        ctx.wait("g", "v")
        ctx.dma("g", d_xn, hf[:], inc="gd")
        ctx.wait("v", "gd")

    sets = [(e, fc) for e in range(8) for fc in range(4)]
    t_done = {}

    def load_set(k):
        e, fc = sets[k]
        b = k % 2
        if DBGSER:
            ctx.wait("g", "t")
        elif k >= 2:
            ctx.wait("g", "t", t_done[k - 2])
        for c in range(8):
            ctx.dma("g", w1sb[b][:, c * D:(c + 1) * D],
                    w1_in[e, c * 128:(c + 1) * 128, fc * D:(fc + 1) * D])
            ctx.dma("g", w2sb[b][:, c * D:(c + 1) * D],
                    w2_in[e, fc * 8 * 128 + c * 128: fc * 8 * 128 + (c + 1) * 128, :])
        ctx.dma("g", b1row[b][:], b1_in[e:e + 1, fc * D:(fc + 1) * D])
        ctx.dma("g", b2row[b][:], b2_in[e:e + 1, :])
        return ctx.cnt["gd"]

    if DBGSER:
        ld_at = {}
    else:
        ld_at = {0: load_set(0), 1: load_set(1)}
    for k, (e, fc) in enumerate(sets):
        b = k % 2
        if DBGSER:
            ld_at[k] = load_set(k)
        ctx.wait("t", "gd", ld_at[k])
        for i in range(NT):
            scol = S[:, i * 8 + e: i * 8 + e + 1]
            emit_proj(ctx, xnT[:, i * D:(i + 1) * D], w1sb[b], ph, [512, 512],
                      bias=b1row[b], ones=ones)
            ctx.wait("s", "t")
            ctx.act(hbf[:], ph[:], AF.Silu)
            ctx.wait("t", "s")
            emit_transpose_tile8(ctx, hbf[:], hT[:], ident, ptp)
            emit_proj(ctx, hT[:], w2sb[b], py, [512, 512],
                      bias=(b2row[b] if fc == 0 else None), ones=ones)
            t_at = ctx.cnt["t"]
            ctx.wait("v", "t", t_at)
            ctx.stt(xs[:, i * D:(i + 1) * D], py[:], scol,
                    xs[:, i * D:(i + 1) * D], OP.mult, OP.add)
        t_done[k] = ctx.cnt["t"]
        if not DBGSER and k + 2 < len(sets):
            ld_at[k + 2] = load_set(k + 2)

    for i in range(NT):
        ctx.wait("g", "v")
        ctx.dma("g", y_t[i, :, :], xs[:, i * D:(i + 1) * D], inc="od")



def build_L1f(dbg=False):
    nc = bass.Bass("TRN2", target_bir_lowering=False, debug=False)
    x_in = nc.dram_tensor("x", [TOK, D], F32, kind="ExternalInput").ap()
    y_out = nc.dram_tensor("y", [TOK, D], F32, kind="ExternalOutput").ap()
    ctx = Ctx(nc)
    with ExitStack() as stack:
        ctx.alloc_sems(stack)
        ctx.ident, ctx.ones = emit_prelude(ctx, stack)
        ctx.scope = "f1_"
        emit_L1f(ctx, stack, x_in, y_out, dbg)
        ctx.wait("q", "od", ctx.cnt["od"])
        ctx.flush()
    return nc

def run_L1f(x_full, INP):
    nc = build_L1f()
    xs = _shard_x(x_full)
    T = lambda a: np.ascontiguousarray(np.asarray(a, np.float32).T)
    common = dict(
        rmsw=np.ascontiguousarray(INP["L1_ffn_rms_w"].reshape(1, D), np.float32),
        gwT=T(INP["L1_ffn_gate_w"]),
        w1=np.ascontiguousarray(INP["L1_ffn_W1"], np.float32),
        b1=np.ascontiguousarray(INP["L1_ffn_b1"], np.float32),
        w2=np.ascontiguousarray(INP["L1_ffn_W2"], np.float32),
        b2=np.ascontiguousarray(INP["L1_ffn_b2"], np.float32),
    )
    maps = [dict(x=xs[r], **common) for r in range(NCORES)]
    res = run_bass_kernel_spmd(nc, _prefix_maps(maps, "f1_"), core_ids=list(range(NCORES)))
    return np.concatenate([res.results[r]["y"] for r in range(NCORES)], 0)



def emit_L2f(ctx, stack, x_in, y_out):
    nc = ctx.nc
    E = stack.enter_context
    ident, ones = ctx.ident, ctx.ones
    w1_in = ctx.dr("w1T", [D, 4096], F32, kind="ExternalInput").ap()
    w2_in = ctx.dr("w2T", [D, 4096], F32, kind="ExternalInput").ap()
    h1_in = ctx.dr("h1T", [4096, 4096], F32, kind="ExternalInput").ap()
    h2_in = ctx.dr("h2T", [4096, 4096], F32, kind="ExternalInput").ap()
    w3_in = ctx.dr("w3T", [4096, D], F32, kind="ExternalInput").ap()
    s_in = ctx.dr("scl", [128, 32 * 4 + 8], F32, kind="ExternalInput").ap()

    HTOK = 1024
    scl = E(ctx.sb("scl_sb", [128, 32 * 4 + 8], F32))
    xT = E(ctx.sb("xT", [128, 8 * HTOK], BF16))
    hsb = E(ctx.sb("hsb", [128, 32 * HTOK], BF16))
    hnsb = E(ctx.sb("hnsb", [128, 32 * HTOK], BF16))
    wba = [E(nc.sbuf_tensor(f"wba{b}", [128, 8 * 128], BF16)) for b in range(2)]
    wbb = [E(nc.sbuf_tensor(f"wbb{b}", [128, 8 * 128], BF16)) for b in range(2)]
    wbig = [E(nc.sbuf_tensor(f"wbig{b}", [128, 32 * 128], BF16)) for b in range(2)]
    xtile = E(ctx.sb("xtile", [128, D], F32))
    xn = E(ctx.sb("xn", [128, D], BF16))
    tf = E(ctx.sb("tf", [128, 512], F32))
    ofb = E(ctx.sb("ofb", [128, 512], BF16))
    mcol = E(ctx.sb("mcol", [128, 8], F32))
    ptp = E(ctx.ps("ptp", [128, 128], BF16))
    pz1 = E(ctx.ps("pz1", [128, 512], F32))
    pz2 = E(ctx.ps("pz2", [128, 512], F32))

    s1c = lambda oc: scl[:, oc:oc + 1]
    s2c = lambda oc: scl[:, 32 + oc:32 + oc + 1]
    h1c = lambda oc: scl[:, 64 + oc:64 + oc + 1]
    h2c = lambda oc: scl[:, 96 + oc:96 + oc + 1]
    w3c = lambda oc: scl[:, 128 + oc:128 + oc + 1]
    ctx.dma("q", scl[:], s_in)

    x_t = x_in.rearrange("(n p) d -> n p d", p=128)
    y_t = y_out.rearrange("(n p) d -> n p d", p=128)
    xs2 = hnsb.ap().bitcast(F32)

    for half in range(2):
        t0 = half * 8
        for i in range(8):
            ctx.wait("q", "v")
            ctx.dma("q", xtile[:], x_t[t0 + i])
            ctx.wait("v", "qd")
            ctx.wait("s", "qd")
            emit_ln(ctx, xtile[:], xn[:], xtile[:], mcol)
            ctx.wait("t", "v")
            for c in range(8):
                emit_transpose(ctx, xn[:, 128 * c:128 * (c + 1)],
                               xT[:, c * HTOK + i * 128:c * HTOK + (i + 1) * 128],
                               ident, ptp)

        t_doneA = {}
        ld_atA = {}

        def loadA(k):
            if k - 2 in t_doneA:
                ctx.wait("g", "t", t_doneA[k - 2])
            else:
                ctx.wait("g", "t")
            wt1 = w1_in.rearrange("(c p) o -> c p o", p=128)
            wt2 = w2_in.rearrange("(c p) o -> c p o", p=128)
            b = k % 2
            for c in range(8):
                ctx.dma("g", wba[b][:, 128 * c:128 * (c + 1)],
                        wt1[c, :, 128 * k:128 * (k + 1)])
                ctx.dma("g", wbb[b][:, 128 * c:128 * (c + 1)],
                        wt2[c, :, 128 * k:128 * (k + 1)])
            return ctx.cnt["gd"]

        ld_atA[0] = loadA(0)
        ld_atA[1] = loadA(1)
        for oc in range(32):
            b = oc % 2
            ctx.wait("t", "gd", ld_atA[oc])
            for g in range(2):
                ctx.wait("t", "s")
                ctx.wait("t", "v")
                for c in range(8):
                    ctx.mm(pz1[:], wba[b][:, 128 * c:128 * (c + 1)],
                           xT[:, c * HTOK + g * 512:c * HTOK + (g + 1) * 512],
                           start=(c == 0), stop=(c == 7))
                    ctx.mm(pz2[:], wbb[b][:, 128 * c:128 * (c + 1)],
                           xT[:, c * HTOK + g * 512:c * HTOK + (g + 1) * 512],
                           start=(c == 0), stop=(c == 7), inc=(c == 7))
                ctx.wait("s", "t")
                ctx.act(tf[:], pz1[:], AF.Silu, scale=s1c(oc))
                ctx.wait("v", "s")
                ctx.stt(hsb[:, oc * HTOK + g * 512:oc * HTOK + (g + 1) * 512],
                        pz2[:], s2c(oc), tf[:], OP.mult, OP.mult)
            t_doneA[oc] = ctx.cnt["t"]
            if oc + 2 < 32:
                ld_atA[oc + 2] = loadA(oc + 2)

        for (w_in_ap, sc, src_sb, dst_sb) in [(h1_in, h1c, hsb, hnsb),
                                              (h2_in, h2c, hnsb, hsb)]:
            t_doneB = {}
            ld_atB = {}

            def loadB(k, w_in_ap=w_in_ap, t_doneB=t_doneB):
                if k - 2 in t_doneB:
                    ctx.wait("g", "t", t_doneB[k - 2])
                else:
                    ctx.wait("g", "t")
                wt = w_in_ap.rearrange("(c p) o -> c p o", p=128)
                b = k % 2
                for c4 in range(4):
                    ctx.dma("g", wbig[b][:, 128 * 8 * c4:128 * 8 * (c4 + 1)].rearrange(
                        "p (c o) -> p c o", o=128),
                        wt[8 * c4:8 * (c4 + 1), :, 128 * k:128 * (k + 1)].rearrange(
                        "c p o -> p c o"))
                return ctx.cnt["gd"]

            ld_atB[0] = loadB(0)
            ld_atB[1] = loadB(1)
            for oc in range(32):
                b = oc % 2
                ctx.wait("t", "gd", ld_atB[oc])
                for g in range(2):
                    ctx.wait("t", "s")
                    for c in range(32):
                        ctx.mm(pz1[:], wbig[b][:, 128 * c:128 * (c + 1)],
                               src_sb[:, c * HTOK + g * 512:c * HTOK + (g + 1) * 512],
                               start=(c == 0), stop=(c == 31), inc=(c == 31))
                    ctx.wait("s", "t")
                    ctx.act(dst_sb[:, oc * HTOK + g * 512:oc * HTOK + (g + 1) * 512],
                            pz1[:], AF.Gelu, scale=sc(oc))
                t_doneB[oc] = ctx.cnt["t"]
                if oc + 2 < 32:
                    ld_atB[oc + 2] = loadB(oc + 2)

        ctx.wait("q", "t")
        ctx.wait("q", "s")
        for i in range(8):
            ctx.dma("q", xs2[:, i * D:(i + 1) * D], x_t[t0 + i])
        xld = ctx.cnt["qd"]
        ctx.wait("v", "qd", xld)
        t_doneD = {}
        ld_atD = {}

        def loadD(k):
            if k - 2 in t_doneD:
                ctx.wait("g", "t", t_doneD[k - 2])
            else:
                ctx.wait("g", "t")
            wt = w3_in.rearrange("(c p) o -> c p o", p=128)
            b = k % 2
            for c4 in range(4):
                ctx.dma("g", wbig[b][:, 128 * 8 * c4:128 * 8 * (c4 + 1)].rearrange(
                    "p (c o) -> p c o", o=128),
                    wt[8 * c4:8 * (c4 + 1), :, 128 * k:128 * (k + 1)].rearrange(
                    "c p o -> p c o"))
            return ctx.cnt["gd"]

        ld_atD[0] = loadD(0)
        ld_atD[1] = loadD(1)
        for oc in range(8):
            b = oc % 2
            ctx.wait("t", "gd", ld_atD[oc])
            for g in range(2):
                ctx.wait("t", "v")
                for c in range(32):
                    ctx.mm(pz1[:], wbig[b][:, 128 * c:128 * (c + 1)],
                           hsb[:, c * HTOK + g * 512:c * HTOK + (g + 1) * 512],
                           start=(c == 0), stop=(c == 31), inc=(c == 31))
                ctx.wait("v", "t")
                ctx.ts(ofb[:], pz1[:], w3c(oc), OP.mult)
                for c4 in range(4):
                    i = g * 4 + c4
                    ctx.wait("t", "v")
                    tat = ctx.transpose(ptp[:], ofb[:, 128 * c4:128 * (c4 + 1)],
                                        ident[:])
                    ctx.wait("v", "t", tat)
                    ctx.tt(xs2[:, i * D + oc * 128:i * D + (oc + 1) * 128],
                           xs2[:, i * D + oc * 128:i * D + (oc + 1) * 128],
                           ptp[:], OP.add)
            t_doneD[oc] = ctx.cnt["t"]
            if oc + 2 < 8:
                ld_atD[oc + 2] = loadD(oc + 2)
        for i in range(8):
            ctx.wait("g", "v")
            ctx.dma("g", y_t[t0 + i, :, :], xs2[:, i * D:(i + 1) * D], inc="od")
        ctx.wait("v", "od", ctx.cnt["od"])
        ctx.wait("s", "od", ctx.cnt["od"])
        ctx.wait("t", "od", ctx.cnt["od"])




def build_L2f():
    nc = bass.Bass("TRN2", target_bir_lowering=False, debug=False)
    x_in = nc.dram_tensor("x", [TOK, D], F32, kind="ExternalInput").ap()
    y_out = nc.dram_tensor("y", [TOK, D], F32, kind="ExternalOutput").ap()
    ctx = Ctx(nc)
    with ExitStack() as stack:
        ctx.alloc_sems(stack)
        ctx.ident, ctx.ones = emit_prelude(ctx, stack)
        ctx.scope = "f2_"
        emit_L2f(ctx, stack, x_in, y_out)
        ctx.wait("q", "od", ctx.cnt["od"])
        ctx.flush()
    return nc

def run_L2f(x_full, INP):
    nc = build_L2f()
    xs = _shard_x(x_full)
    T = lambda a: np.ascontiguousarray(np.asarray(a, np.float32).T)
    SC = lambda a, n: np.ascontiguousarray(
        np.asarray(a, np.float32).reshape(n, 128).T)
    scl = np.concatenate([SC(INP["L2_ffn_w1_s"], 32), SC(INP["L2_ffn_w2_s"], 32),
                          SC(INP["L2_ffn_h1_s"], 32), SC(INP["L2_ffn_h2_s"], 32),
                          SC(INP["L2_ffn_w3_s"], 8)], axis=1)
    common = dict(
        w1T=T(INP["L2_ffn_w1_w"]), w2T=T(INP["L2_ffn_w2_w"]),
        h1T=T(INP["L2_ffn_h1_w"]), h2T=T(INP["L2_ffn_h2_w"]),
        w3T=T(INP["L2_ffn_w3_w"]), scl=np.ascontiguousarray(scl),
    )
    maps = [dict(x=xs[r], **common) for r in range(NCORES)]
    res = run_bass_kernel_spmd(nc, _prefix_maps(maps, "f2_"), core_ids=list(range(NCORES)))
    return np.concatenate([res.results[r]["y"] for r in range(NCORES)], 0)



def emit_allgather(ctx, in_ap, out_ap):
    return ctx.run("g", lambda E: E.collective_compute(
        "AllGather", OP.bypass, replica_groups=[list(range(NCORES))],
        ins=[in_ap.opt()], outs=[out_ap.opt()]), inc="cc", n=1)


def emit_bcast_f32(ctx, row_bf, out_f32, ones, psum_f32):
    F = row_bf.shape[-1]
    ctx.wait("t", "v")
    ctx.wait("t", "s")
    for f in range(0, F, 512):
        w = min(512, F - f)
        at = ctx.mm(psum_f32[:, f:f + w], ones[:], row_bf[:, f:f + w],
                    start=True, stop=True, inc=True)
    ctx.wait("v", "t", at)
    ctx.cp(out_f32, psum_f32[:, 0:F])


SKIP_CC = False

def emit_L1a(ctx, stack, x_in, y_out):
    nc = ctx.nc
    E = stack.enter_context
    ident, ones = ctx.ident, ctx.ones
    wk_in = ctx.dr("wkT", [D, D], F32, kind="ExternalInput").ap()
    wv_in = ctx.dr("wvT", [D, D], F32, kind="ExternalInput").ap()
    wq_in = ctx.dr("wqT", [D, D], F32, kind="ExternalInput").ap()
    wg_in = ctx.dr("wgT", [D, D], F32, kind="ExternalInput").ap()
    wo_in = ctx.dr("woT", [D, D], F32, kind="ExternalInput").ap()
    w0_in = ctx.dr("w0", [D, D], F32, kind="ExternalInput").ap()
    a_in = ctx.dr("ainit", [D, 8], F32, kind="ExternalInput").ap()
    b_in = ctx.dr("binit", [8, D], F32, kind="ExternalInput").ap()
    lng_in = ctx.dr("lng", [1, D], F32, kind="ExternalInput").ap()
    lnb_in = ctx.dr("lnb", [1, D], F32, kind="ExternalInput").ap()
    loglr_in = ctx.dr("loglr", [1, D], F32, kind="ExternalInput").ap()
    lsg_in = ctx.dr("lsg", [1, D], F32, kind="ExternalInput").ap()
    strict_in = ctx.dr("strictLT", [128, 128], F32, kind="ExternalInput").ap()
    maskp_in = ctx.dr("maskprev", [8, 1], F32, kind="ExternalInput").ap()
    pos_in = ctx.dr("poscol", [128, NT], F32, kind="ExternalInput").ap()
    cs_pub = ctx.dr("cs_pub", [1, D], F32)
    cs_ag = ctx.dr("cs_ag", [NCORES, D], F32)
    xmid = ctx.dr("xmid", [TOK, D], F32)

    wk = E(ctx.sb("wk", [128, 8 * D], BF16))
    wv = E(ctx.sb("wv", [128, 8 * D], BF16))
    wq = E(ctx.sb("wq", [128, 8 * D], BF16))
    wg = E(ctx.sb("wg", [128, 8 * D], BF16))
    wo = E(ctx.sb("wo", [128, 8 * D], BF16))
    w0 = E(ctx.sb("w0_sb", [128, 8 * D], BF16))
    asb = E(ctx.sb("asb", [128, 8 * 8], BF16))
    bsb = E(ctx.sb("bsb", [8, D], BF16))
    strict = E(ctx.sb("strict", [128, 128], BF16))
    maskp = E(ctx.sb("maskp", [8, 1], BF16))
    poss = E(ctx.sb("poss", [128, NT], F32))
    rows = E(ctx.sb("rows", [1, D], F32))
    rowsbf = E(ctx.sb("rowsbf", [1, 2 * D], BF16))
    g_bc = E(ctx.sb("g_bc", [128, D], BF16))
    b_bc = E(ctx.sb("b_bc", [128, D], BF16))
    lr_bc = E(ctx.sb("lr_bc", [128, D], BF16))
    lsg_bc = E(ctx.sb("lsg_bc", [128, D], BF16))
    Qg = E(ctx.sb("Qg", [128, NT * D], BF16))
    xtile = E(ctx.sb("xtile", [128, D], F32))
    hf = E(ctx.sb("hf", [128, D], F32))
    scr2 = E(ctx.sb("scr2", [128, D], F32))
    hbf = E(ctx.sb("hbf", [128, D], BF16))
    hnbf = E(ctx.sb("hnbf", [128, D], BF16))
    hT = E(ctx.sb("hT", [128, D], BF16))
    hnT = E(ctx.sb("hnT", [128, D], BF16))
    kf = E(ctx.sb("kf", [128, D], F32))
    kbf = E(ctx.sb("kbf", [128, D], BF16))
    kT = E(ctx.sb("kT", [128, D], BF16))
    vf = E(ctx.sb("vf", [128, D], F32))
    qf = E(ctx.sb("qf", [128, D], F32))
    gatef = E(ctx.sb("gatef", [128, D], F32))
    gradbf = E(ctx.sb("gradbf", [128, D], BF16))
    y0bf = E(ctx.sb("y0bf", [128, D], BF16))
    t1 = E(ctx.sb("t1", [128, D], F32))
    ob = E(ctx.sb("ob", [128, D], BF16))
    obT = E(ctx.sb("obT", [128, D], BF16))
    kabf = E(ctx.sb("kabf", [128, 8], BF16))
    kaT = E(ctx.sb("kaT", [8, 128], BF16))
    onescol = E(ctx.sb("onescol", [128, 1], BF16))
    carry = E(ctx.sb("carry", [1, D], F32))
    ag8 = E(ctx.sb("ag8", [8, D], BF16))
    mcol = E(ctx.sb("mcol", [128, 8], F32))
    ptp = E(ctx.ps("ptp", [128, 128], BF16))
    pa = E(ctx.ps("pa", [128, 1024], F32))
    pb = E(ctx.ps("pb", [128, 1024], F32))
    pka = E(ctx.ps("pka", [128, 16], F32))
    pc1 = E(ctx.ps("pc1", [1, 1024], F32))

    x_t = x_in.rearrange("(n p) d -> n p d", p=128)
    y_t = y_out.rearrange("(n p) d -> n p d", p=128)
    xm_t = xmid.ap().rearrange("(n p) d -> n p d", p=128)
    for w_sb, w_ap in [(wk, wk_in), (wv, wv_in), (wq, wq_in), (wg, wg_in),
                       (wo, wo_in), (w0, w0_in)]:
        wt = w_ap.rearrange("(c p) d -> c p d", p=128)
        for c in range(8):
            ctx.dma("g", w_sb[:, c * D:(c + 1) * D], wt[c])
    at_ = a_in.rearrange("(c p) r -> c p r", p=128)
    for c in range(8):
        ctx.dma("g", asb[:, c * 8:(c + 1) * 8], at_[c])
    ctx.dma("g", bsb[:], b_in)
    ctx.dma("g", strict[:], strict_in)
    ctx.dma("g", maskp[:], maskp_in)
    ctx.dma("q", poss[:], pos_in)
    ctx.run("g", lambda E_: E_.memset(onescol[:], 1.0), inc="g")
    rb1 = rowsbf[:, 0:D]
    for (src_ap, dst_bc, is_lr) in [(lng_in, g_bc, False), (lnb_in, b_bc, False),
                                    (loglr_in, lr_bc, True), (lsg_in, lsg_bc, False)]:
        ctx.wait("q", "v")
        ctx.wait("q", "s")
        ctx.dma("q", rows[:], src_ap)
        ctx.wait("v", "qd")
        ctx.wait("s", "qd")
        if is_lr:
            ctx.act(rows[:], rows[:], AF.Exp)
            ctx.wait("v", "s")
            ctx.ts(rb1, rows[:], 1.0, OP.min)
        else:
            ctx.cp(rb1, rows[:])
        emit_bcast_f32(ctx, rb1, dst_bc[:], ones, pa)

    ctx.wait("t", "gd")
    ctx.run("v", lambda E_: E_.memset(rowsbf[:, D:2 * D], 0.0), inc="v")
    for i in range(NT):
        ctx.wait("q", "v")
        ctx.wait("q", "gd")
        ctx.dma("q", xtile[:], x_t[i])
        ctx.wait("v", "qd")
        ctx.wait("s", "qd")
        emit_ln(ctx, xtile[:], hbf[:], hf[:], mcol)
        ctx.ts(hf[:], hf[:], mcol[:, 3:4], OP.mult)
        ctx.cp(hbf[:], hf[:])
        emit_ln(ctx, hf[:], hnbf[:], scr2[:], mcol[:, 4:8],
                gain_bc=g_bc[:], bias_bc=b_bc[:])
        ctx.wait("t", "v")
        emit_transpose_tile8(ctx, hnbf[:], hnT[:], ident, ptp)
        emit_transpose_tile8(ctx, hbf[:], hT[:], ident, ptp)
        emit_proj(ctx, hnT[:], wk[:], pa, [512, 512])
        ctx.wait("s", "t")
        ctx.cp(kf[:], pa[:], eng="s")
        ctx.wait("v", "s")
        ctx.cp(kbf[:], kf[:])
        ctx.wait("t", "v")
        emit_proj(ctx, hnT[:], wv[:], pb, [512, 512])
        ctx.wait("s", "t")
        ctx.cp(vf[:], pb[:], eng="s")
        ctx.wait("t", "v")
        emit_transpose_tile8(ctx, kbf[:], kT[:], ident, ptp)
        ctx.wait("t", "v")
        emit_proj(ctx, kT[:], asb[:], pka, [8])
        ctx.wait("v", "t")
        ctx.cp(kabf[:], pka[:, 0:8])
        emit_transpose(ctx, kabf[:], kaT[:], ident, ptp)
        ctx.wait("t", "s")
        emit_proj(ctx, kT[:], w0[:], pb, [512, 512], close=False)
        ctx.wait("t", "v")
        for f in range(2):
            ctx.mm(pb[:, f * 512:(f + 1) * 512], kaT[:],
                   bsb[:, f * 512:(f + 1) * 512], start=False, stop=True,
                   inc=True)
        y0_at = ctx.cnt["t"]
        ctx.wait("v", "t", y0_at)
        ctx.tt(vf[:], pb[:], vf[:], OP.subtract)
        ctx.tt(t1[:], kf[:], vf[:], OP.mult)
        ctx.cp(gradbf[:], t1[:])
        ctx.cp(y0bf[:], pb[:])
        ctx.wait("t", "v")
        emit_proj(ctx, hnT[:], wq[:], pa, [512, 512])
        ctx.wait("s", "t")
        ctx.cp(qf[:], pa[:], eng="s")
        ctx.wait("t", "s")
        ctx.wait("t", "v")
        emit_proj(ctx, hT[:], wg[:], pb, [512, 512])
        ctx.wait("s", "t")
        ctx.act(gatef[:], pb[:], AF.Sigmoid)
        ctx.wait("t", "s")
        ctx.wait("t", "v")
        for f in range(2):
            ctx.mm(pa[:, f * 512:(f + 1) * 512], strict[:],
                   gradbf[:, f * 512:(f + 1) * 512], start=True, stop=False)
            if i > 0:
                ctx.mm(pa[:, f * 512:(f + 1) * 512], ones[:],
                       rowsbf[:, D + f * 512:D + (f + 1) * 512],
                       start=False, stop=True, inc=True)
            else:
                ctx.mm(pa[:, f * 512:(f + 1) * 512], ones[:],
                       rowsbf[:, D + f * 512:D + (f + 1) * 512],
                       start=False, stop=True, inc=True)
        sh_at = ctx.cnt["t"]
        ctx.wait("v", "t", sh_at)
        ctx.wait("v", "s")
        ctx.tt(t1[:], qf[:], lr_bc[:], OP.mult)
        ctx.ts(t1[:], t1[:], poss[:, i:i + 1], OP.mult)
        ctx.tt(Qg[:, i * D:(i + 1) * D], t1[:], gatef[:], OP.mult)
        ctx.tt(t1[:], t1[:], pa[:], OP.mult)
        ctx.tt(t1[:], y0bf[:], t1[:], OP.subtract)
        ctx.tt(ob[:], t1[:], gatef[:], OP.mult)
        ctx.wait("t", "v")
        for f in range(2):
            ctx.mm(pc1[0:1, f * 512:(f + 1) * 512], onescol[:],
                   gradbf[:, f * 512:(f + 1) * 512], start=True, stop=True,
                   inc=(f == 1))
        cs_at = ctx.cnt["t"]
        ctx.wait("v", "t", cs_at)
        if i == 0:
            ctx.cp(carry[:], pc1[0:1, :])
        else:
            ctx.tt(carry[:], carry[:], pc1[0:1, :], OP.add)
        ctx.cp(rowsbf[0:1, D:2 * D], carry[:])
        ctx.wait("t", "v")
        emit_transpose_tile8(ctx, ob[:], obT[:], ident, ptp)
        emit_proj(ctx, obT[:], wo[:], pb, [512, 512])
        ctx.wait("v", "t")
        ctx.tt(t1[:], pb[:], lsg_bc[:], OP.mult)
        ctx.tt(xtile[:], xtile[:], hf[:], OP.add)
        ctx.tt(xtile[:], xtile[:], t1[:], OP.add)
        ctx.wait("g", "v")
        ctx.dma("g", xm_t[i, :, :], xtile[:], inc="gd")

    ctx.wait("g", "v")
    ctx.dma("g", cs_pub.ap(), carry[:], inc="gd")
    ctx.wait("g", "gd")
    if not SKIP_CC:
        cc_at = emit_allgather(ctx, cs_pub.ap(), cs_ag.ap())
        ctx.wait("g", "cc", cc_at)
        ctx.dma("g", ag8[:], cs_ag.ap(), inc="gd")
    else:
        ctx.run("g", lambda E_: E_.memset(ag8[:], 0.0), inc="g")
    agd = ctx.cnt["gd"]
    ctx.wait("t", "gd", agd)
    for f in range(2):
        ctx.mm(pc1[0:1, f * 512:(f + 1) * 512], maskp[:],
               ag8[:, f * 512:(f + 1) * 512], start=True, stop=True,
               inc=(f == 1))
    ci_at = ctx.cnt["t"]
    ctx.wait("v", "t", ci_at)
    ctx.cp(rowsbf[0:1, D:2 * D], pc1[0:1, :])
    ctx.wait("t", "v")
    for f in range(2):
        cb_at = ctx.mm(pb[:, f * 512:(f + 1) * 512], ones[:],
                       rowsbf[:, D + f * 512:D + (f + 1) * 512],
                       start=True, stop=True, inc=True)

    ctx.wait("q", "gd")
    for i in range(NT):
        ctx.wait("q", "v")
        ctx.wait("q", "od")
        ctx.dma("q", xtile[:], xm_t[i, :, :])
        ctx.wait("v", "qd")
        ctx.wait("v", "t", cb_at)
        ctx.tt(ob[:], Qg[:, i * D:(i + 1) * D], pb[:], OP.mult)
        ctx.wait("t", "v")
        emit_transpose_tile8(ctx, ob[:], obT[:], ident, ptp)
        emit_proj(ctx, obT[:], wo[:], pa, [512, 512])
        ctx.wait("v", "t")
        ctx.tt(t1[:], pa[:], lsg_bc[:], OP.mult)
        ctx.tt(xtile[:], xtile[:], t1[:], OP.subtract)
        ctx.wait("g", "v")
        ctx.dma("g", y_t[i, :, :], xtile[:], inc="od")




def build_L1a():
    nc = bass.Bass("TRN2", target_bir_lowering=False, debug=False)
    x_in = nc.dram_tensor("x", [TOK, D], F32, kind="ExternalInput").ap()
    y_out = nc.dram_tensor("y", [TOK, D], F32, kind="ExternalOutput").ap()
    ctx = Ctx(nc)
    with ExitStack() as stack:
        ctx.alloc_sems(stack)
        ctx.ident, ctx.ones = emit_prelude(ctx, stack)
        ctx.scope = "a1_"
        emit_L1a(ctx, stack, x_in, y_out)
        ctx.wait("q", "od", ctx.cnt["od"])
        ctx.flush()
    return nc

def run_L1a(x_full, INP):
    nc = build_L1a()
    xs = _shard_x(x_full)
    T = lambda a: np.ascontiguousarray(np.asarray(a, np.float32).T)
    R1 = lambda a: np.ascontiguousarray(np.asarray(a, np.float32).reshape(1, -1))
    common = dict(
        wkT=T(INP["L1_attn_Wk"]), wvT=T(INP["L1_attn_Wv"]), wqT=T(INP["L1_attn_Wq"]),
        wgT=T(INP["L1_attn_Wg"]), woT=T(INP["L1_attn_Wo"]),
        w0=np.ascontiguousarray(INP["L1_attn_W0"], np.float32),
        ainit=np.ascontiguousarray(INP["L1_attn_A_init"], np.float32),
        binit=np.ascontiguousarray(INP["L1_attn_B_init"], np.float32),
        lng=R1(INP["L1_attn_ln_g"]), lnb=R1(INP["L1_attn_ln_b"]),
        loglr=R1(INP["L1_attn_log_lr"]), lsg=R1(INP["L1_attn_ls_g"]),
        strictLT=np.ascontiguousarray(np.triu(np.ones((128, 128)), 1), np.float32),
    )
    maps = []
    for r in range(NCORES):
        pos = (np.arange(TOK) + r * TOK + 1).astype(np.float64)
        ps = (1.0 / (1.0 + 0.1 * np.log(pos))).astype(np.float32)
        poscol = np.ascontiguousarray(ps.reshape(NT, 128).T)
        maskp = np.zeros((8, 1), np.float32)
        maskp[:r] = 1.0
        maps.append(dict(x=xs[r], poscol=poscol, maskprev=maskp, **common))
    res = run_bass_kernel_spmd(nc, _prefix_maps(maps, "a1_"), core_ids=list(range(NCORES)))
    return np.concatenate([res.results[r]["y"] for r in range(NCORES)], 0)



def emit_L0a(ctx, stack, x_in, y_out, dbg=False):
    nc = ctx.nc
    E = stack.enter_context
    ident, ones = ctx.ident, ctx.ones
    if dbg:
        d_ag = ctx.dr("d_ag", [NCORES * 128, 2048], F32, kind="ExternalOutput").ap()
        d_M = ctx.dr("d_M", [128, 2048], F32, kind="ExternalOutput").ap()
        d_sco = ctx.dr("d_sco", [128, 2048], F32, kind="ExternalOutput").ap()
    wq_in = ctx.dr("qkvuT", [D, 4 * D], F32, kind="ExternalInput").ap()
    qb_in = ctx.dr("qkvub", [1, 4 * D], F32, kind="ExternalInput").ap()
    wo_in = ctx.dr("outwT", [D, D], F32, kind="ExternalInput").ap()
    ob_in = ctx.dr("outb", [1, D], F32, kind="ExternalInput").ap()
    sl_in = ctx.dr("slope", [1, 8], F32, kind="ExternalInput").ap()
    th_in = ctx.dr("theta", [1, D], F32, kind="ExternalInput").ap()
    tk_in = ctx.dr("tokcol", [128, NT], F32, kind="ExternalInput").ap()
    trl_in = ctx.dr("trilT", [128, 128], F32, kind="ExternalInput").ap()
    wp_in = ctx.dr("wprev", [128, 8], F32, kind="ExternalInput").ap()
    s_pubh = [ctx.dr(f"s_pub{h}", [128, 256], F32) for h in range(8)]
    s_agh = [ctx.dr(f"s_ag{h}", [NCORES * 128, 256], F32) for h in range(8)]

    TWOPI = float(2 * np.pi)
    wq = E(ctx.sb("wq", [128, 8 * 4 * D], BF16))
    wo = E(ctx.sb("wo", [128, 8 * D], BF16))
    brow = E(ctx.sb("brow", [1, 4 * D], BF16))
    obrow = E(ctx.sb("obrow", [1, D], BF16))
    slr = E(ctx.sb("slr", [1, 8], F32))
    thr = E(ctx.sb("thr", [1, D], BF16))
    th_bc = E(ctx.sb("th_bc", [128, D], F32))
    es1_bc = E(ctx.sb("es1_bc", [128, 8], F32))
    tokc = E(ctx.sb("tokc", [128, NT], F32))
    tril = E(ctx.sb("tril", [128, 128], BF16))
    wprev = E(ctx.sb("wprev_sb", [128, 8], F32))
    hstore = E(ctx.sb("hstore", [128, NT * D], BF16))
    Sco = E(ctx.sb("Sco", [128, 2 * D], F32))
    Spv = E(ctx.sb("Spv", [128, 2 * D], F32))
    Sto = Sco
    Mbf = E(ctx.sb("Mbf", [128, 2 * D], BF16))
    blk = E(ctx.sb("blk", [128, 2 * D], BF16))
    xtile = E(ctx.sb("xtile", [128, D], F32))
    hf = E(ctx.sb("hf", [128, D], F32))
    hT = E(ctx.sb("hT", [128, D], BF16))
    ksil = E(ctx.sb("ksil", [128, D], F32))
    qsil = E(ctx.sb("qsil", [128, D], F32))
    uf = E(ctx.sb("uf", [128, D], F32))
    vbf = E(ctx.sb("vbf", [128, D], BF16))
    sint = E(ctx.sb("sint", [128, D], I32))
    tww = E(ctx.sb("tww", [128, D], F32))
    rr = E(ctx.sb("rr", [128, D], F32))
    sin_t, cos_t = rr, tww
    kc = E(ctx.sb("kc", [128, D], BF16))
    ks = E(ctx.sb("ks", [128, D], BF16))
    qc = E(ctx.sb("qc", [128, D], BF16))
    qs = E(ctx.sb("qs", [128, D], BF16))
    fT = E(ctx.sb("fT", [128, 4 * 128], BF16))
    sd = E(ctx.sb("sd", [128, 128], BF16))
    ao = E(ctx.sb("ao", [128, D], F32))
    obuf = E(ctx.sb("obuf", [128, D], BF16))
    obT = E(ctx.sb("obT", [128, D], BF16))
    mcol = E(ctx.sb("mcol", [128, 8], F32))
    ptp = E(ctx.ps("ptp", [128, 128], BF16))
    pq = E(ctx.ps("pq", [128, 2048], F32))
    pA = E(ctx.ps("pA", [128, 128], F32))
    pB = E(ctx.ps("pB", [128, 128], F32))

    x_t = x_in.rearrange("(n p) d -> n p d", p=128)
    y_t = y_out.rearrange("(n p) d -> n p d", p=128)
    wt = wq_in.rearrange("(c p) d -> c p d", p=128)
    for c in range(8):
        ctx.dma("g", wq[:, c * 4 * D:(c + 1) * 4 * D], wt[c])
    wot = wo_in.rearrange("(c p) d -> c p d", p=128)
    for c in range(8):
        ctx.dma("g", wo[:, c * D:(c + 1) * D], wot[c])
    ctx.dma("g", brow[:], qb_in)
    ctx.dma("g", obrow[:], ob_in)
    ctx.dma("g", thr[:], th_in)
    ctx.dma("q", slr[:], sl_in)
    ctx.dma("q", tokc[:], tk_in)
    ctx.dma("g", tril[:], trl_in)
    ctx.dma("q", wprev[:], wp_in)
    ctx.wait("s", "qd")
    ctx.wait("v", "qd")
    ctx.wait("v", "gd")
    ctx.act(slr[:], slr[:], AF.Exp)
    ctx.wait("v", "s")
    rb = E(ctx.sb("rb", [1, 128], BF16))
    ctx.ts(rb[:, 0:8], slr[:], -1.0, OP.add)
    emit_bcast_f32(ctx, rb[:, 0:8], es1_bc[:], ones, pq)
    emit_bcast_f32(ctx, thr[:], th_bc[:], ones, pq)
    ctx.run("v", lambda E_: E_.memset(Sco[:], 0.0), inc="v")
    ctx.run("v", lambda E_: E_.memset(Spv[:], 0.0), inc="v")

    def emit_trig(i):
        ctx.wait("v", "s")
        ctx.ts(tww[:], th_bc[:], tokc[:, i:i + 1], OP.mult)
        ctx.ts(rr[:], tww[:], 1.0 / TWOPI, OP.mult)
        ctx.cp(sint[:], rr[:])
        ctx.cp(rr[:], sint[:])
        ctx.stt(rr[:], rr[:], -TWOPI, tww[:], OP.mult, OP.add)
        ctx.ts(hf[:], rr[:], float(np.pi), OP.is_gt)
        ctx.stt(rr[:], hf[:], -TWOPI, rr[:], OP.mult, OP.add)
        ctx.ts(hf[:], rr[:], float(-np.pi), OP.is_lt)
        ctx.stt(rr[:], hf[:], TWOPI, rr[:], OP.mult, OP.add)
        ctx.ts(tww[:], rr[:], float(np.pi / 2), OP.add)
        ctx.ts(hf[:], tww[:], float(np.pi), OP.is_gt)
        ctx.stt(tww[:], hf[:], -TWOPI, tww[:], OP.mult, OP.add)
        ctx.wait("s", "v")
        ctx.act(sin_t[:], rr[:], AF.Sin)
        ctx.act(cos_t[:], tww[:], AF.Sin)
        ctx.wait("v", "s")

    def emit_qkvu(i, want_q):
        ctx.wait("t", "v")
        emit_transpose_tile8(ctx, hstore[:, i * D:(i + 1) * D], hT[:], ident, ptp)
        ctx.wait("t", "v")
        ctx.wait("t", "s")
        for f in range(4):
            ctx.mm(pq[:, f * 512:(f + 1) * 512], ones[:],
                   brow[:, D + f * 512:D + (f + 1) * 512], start=True, stop=False)
            for c in range(8):
                ctx.mm(pq[:, f * 512:(f + 1) * 512], hT[:, 128 * c:128 * (c + 1)],
                       wq[:, c * 4 * D + D + f * 512:c * 4 * D + D + (f + 1) * 512],
                       start=False, stop=(c == 7), inc=(c == 7))
        ctx.wait("s", "t")
        ctx.act(ksil[:], pq[:, 0:D], AF.Silu)
        ctx.wait("v", "s")
        ctx.cp(vbf[:], pq[:, D:2 * D])
        if want_q:
            ctx.wait("t", "v")
            ctx.wait("t", "s")
            for f in range(2):
                ctx.mm(pq[:, f * 512:(f + 1) * 512], ones[:],
                       brow[:, f * 512:(f + 1) * 512], start=True, stop=False)
                ctx.mm(pq[:, D + f * 512:D + (f + 1) * 512], ones[:],
                       brow[:, 3 * D + f * 512:3 * D + (f + 1) * 512],
                       start=True, stop=False)
                for c in range(8):
                    ctx.mm(pq[:, f * 512:(f + 1) * 512], hT[:, 128 * c:128 * (c + 1)],
                           wq[:, c * 4 * D + f * 512:c * 4 * D + (f + 1) * 512],
                           start=False, stop=(c == 7), inc=(c == 7))
                    ctx.mm(pq[:, D + f * 512:D + (f + 1) * 512],
                           hT[:, 128 * c:128 * (c + 1)],
                           wq[:, c * 4 * D + 3 * D + f * 512:c * 4 * D + 3 * D + (f + 1) * 512],
                           start=False, stop=(c == 7), inc=(c == 7))
            ctx.wait("s", "t")
            ctx.act(qsil[:], pq[:, 0:D], AF.Silu)
            ctx.cp(uf[:], pq[:, D:2 * D], eng="s")

    for i in range(NT):
        ctx.wait("q", "v")
        ctx.wait("q", "s")
        ctx.dma("q", xtile[:], x_t[i])
        ctx.wait("v", "qd")
        ctx.wait("s", "qd")
        emit_ln(ctx, xtile[:], hstore[:, i * D:(i + 1) * D], hf[:], mcol)
        emit_qkvu(i, want_q=False)
        emit_trig(i)
        ctx.tt(kc[:], ksil[:], cos_t[:], OP.mult)
        ctx.tt(ks[:], ksil[:], sin_t[:], OP.mult)
        for h in range(8):
            hs = slice(128 * h, 128 * (h + 1))
            ctx.wait("t", "v")
            ctx.mm(pA[:], kc[:, hs], vbf[:, hs], start=True, stop=True, inc=True)
            ctx.mm(pB[:], ks[:, hs], vbf[:, hs], start=True, stop=True, inc=True)
            ctx.wait("v", "t")
            ctx.tt(Sco[:, 256 * h:256 * h + 128], Sco[:, 256 * h:256 * h + 128],
                   pA[:], OP.add)
            ctx.tt(Sco[:, 256 * h + 128:256 * h + 256],
                   Sco[:, 256 * h + 128:256 * h + 256], pB[:], OP.add)

    ctx.wait("g", "v")
    for h in range(8):
        ctx.dma("g", s_pubh[h].ap(), Sco[:, 256 * h:256 * (h + 1)], inc="gd")
    ctx.wait("g", "gd")
    for h in range(8):
        cc_at = emit_allgather(ctx, s_pubh[h].ap(), s_agh[h].ap())
    ctx.wait("g", "cc", cc_at)
    ctx.wait("v", "gd")
    ctx.run("v", lambda E_: E_.memset(Sto[:], 0.0), inc="v")
    for r in range(NCORES):
        ctx.wait("g", "v")
        for h in range(8):
            ctx.dma("g", blk[:, 256 * h:256 * (h + 1)],
                    s_agh[h].ap().rearrange("(r p) d -> r p d", p=128)[r])
        ctx.wait("v", "gd")
        ctx.stt(Spv[:], blk[:], wprev[:, r:r + 1], Spv[:], OP.mult, OP.add)
        ctx.tt(Sto[:], Sto[:], blk[:], OP.add)
    if dbg:
        ctx.wait("g", "v")
        ctx.dma("g", d_sco, Sto[:], inc="gd")
    for h in range(8):
        ctx.stt(Spv[:, 256 * h:256 * (h + 1)], Spv[:, 256 * h:256 * (h + 1)],
                es1_bc[:, h:h + 1], Sto[:, 256 * h:256 * (h + 1)],
                OP.mult, OP.add)
    ctx.cp(Mbf[:], Spv[:])
    if dbg:
        ctx.wait("g", "v")
        ctx.dma("g", d_M, Spv[:], inc="gd")
        ctx.dma("g", d_ag, s_ag.ap(), inc="gd")
        ctx.wait("v", "gd")

    for i in range(NT):
        emit_qkvu(i, want_q=True)
        emit_trig(i)
        ctx.wait("v", "s")
        ctx.tt(kc[:], ksil[:], cos_t[:], OP.mult)
        ctx.tt(ks[:], ksil[:], sin_t[:], OP.mult)
        ctx.tt(qc[:], qsil[:], cos_t[:], OP.mult)
        ctx.tt(qs[:], qsil[:], sin_t[:], OP.mult)
        for h in range(8):
            hs = slice(128 * h, 128 * (h + 1))
            ctx.wait("t", "v")
            for j, src in enumerate((qc, qs, kc, ks)):
                emit_transpose(ctx, src[:, hs], fT[:, 128 * j:128 * (j + 1)],
                               ident, ptp)
            ctx.wait("t", "v")
            ctx.mm(pA[:], fT[:, 256:384], fT[:, 0:128], start=True, stop=False)
            ctx.mm(pA[:], fT[:, 384:512], fT[:, 128:256], start=False, stop=True,
                   inc=True)
            ctx.wait("v", "t")
            ctx.stt(sd[:], pA[:], es1_bc[:, h:h + 1], tril[:], OP.mult, OP.mult)
            ctx.wait("t", "v")
            ctx.mm(pB[:], sd[:], vbf[:, hs], start=True, stop=False)
            ctx.mm(pB[:], fT[:, 0:128], Mbf[:, 256 * h:256 * h + 128],
                   start=False, stop=False)
            ctx.mm(pB[:], fT[:, 128:256], Mbf[:, 256 * h + 128:256 * h + 256],
                   start=False, stop=True, inc=True)
            ctx.wait("v", "t")
            ctx.cp(ao[:, hs], pB[:])
            ctx.wait("t", "v")
            ctx.mm(pA[:], kc[:, hs], vbf[:, hs], start=True, stop=True, inc=True)
            ctx.wait("v", "t")
            ctx.stt(Spv[:, 256 * h:256 * h + 128], pA[:], es1_bc[:, h:h + 1],
                    Spv[:, 256 * h:256 * h + 128], OP.mult, OP.add)
            ctx.cp(Mbf[:, 256 * h:256 * h + 128], Spv[:, 256 * h:256 * h + 128])
            ctx.wait("t", "v")
            ctx.mm(pA[:], ks[:, hs], vbf[:, hs], start=True, stop=True, inc=True)
            ctx.wait("v", "t")
            ctx.stt(Spv[:, 256 * h + 128:256 * h + 256], pA[:], es1_bc[:, h:h + 1],
                    Spv[:, 256 * h + 128:256 * h + 256], OP.mult, OP.add)
            ctx.cp(Mbf[:, 256 * h + 128:256 * h + 256],
                   Spv[:, 256 * h + 128:256 * h + 256])
        ctx.wait("s", "v")
        ctx.act(obuf[:], ao[:], AF.Square, accum=mcol[:, 4:5])
        ctx.wait("v", "s")
        ctx.ts(mcol[:, 4:5], mcol[:, 4:5], 1.0 / D, OP.mult, 1e-6, OP.add)
        ctx.wait("s", "v")
        ctx.act(mcol[:, 5:6], mcol[:, 4:5], AF.Sqrt)
        ctx.wait("v", "s")
        ctx.run("v", lambda E_: E_.reciprocal(out=mcol[:, 6:7], in_=mcol[:, 5:6]),
                inc="v")
        ctx.ts(ao[:], ao[:], mcol[:, 6:7], OP.mult)
        ctx.tt(obuf[:], ao[:], uf[:], OP.mult)
        ctx.wait("t", "v")
        emit_transpose_tile8(ctx, obuf[:], obT[:], ident, ptp)
        ctx.wait("q", "v")
        ctx.wait("q", "od")
        ctx.dma("q", xtile[:], x_t[i])
        emit_proj(ctx, obT[:], wo[:], pq, [512, 512], bias=obrow, ones=ones)
        ctx.wait("v", "qd")
        ctx.wait("v", "t")
        ctx.tt(xtile[:], xtile[:], pq[:, 0:D], OP.add)
        ctx.wait("g", "v")
        ctx.dma("g", y_t[i, :, :], xtile[:], inc="od")




def build_L0a():
    nc = bass.Bass("TRN2", target_bir_lowering=False, debug=False)
    x_in = nc.dram_tensor("x", [TOK, D], F32, kind="ExternalInput").ap()
    y_out = nc.dram_tensor("y", [TOK, D], F32, kind="ExternalOutput").ap()
    ctx = Ctx(nc)
    with ExitStack() as stack:
        ctx.alloc_sems(stack)
        ctx.ident, ctx.ones = emit_prelude(ctx, stack)
        ctx.scope = "a0_"
        emit_L0a(ctx, stack, x_in, y_out)
        ctx.wait("q", "od", ctx.cnt["od"])
        ctx.flush()
    return nc

def run_L0a(x_full, INP):
    nc = build_L0a()
    xs = _shard_x(x_full)
    T = lambda a: np.ascontiguousarray(np.asarray(a, np.float32).T)
    R1 = lambda a: np.ascontiguousarray(np.asarray(a, np.float32).reshape(1, -1))
    common = dict(
        qkvuT=T(INP["L0_attn_qkvu_w"]), qkvub=R1(INP["L0_attn_qkvu_b"]),
        outwT=T(INP["L0_attn_out_w"]), outb=R1(INP["L0_attn_out_b"]),
        slope=R1(INP["L0_attn_slope_rate"]), theta=R1(INP["L0_attn_theta"]),
        trilT=np.ascontiguousarray(np.triu(np.ones((128, 128)), 0), np.float32),
    )
    maps = []
    for r in range(NCORES):
        tok = np.arange(TOK, dtype=np.float32) + r * TOK
        tokcol = np.ascontiguousarray(tok.reshape(NT, 128).T)
        wprev = np.zeros((128, 8), np.float32)
        wprev[:, :r] = 1.0
        maps.append(dict(x=xs[r], tokcol=tokcol, wprev=wprev, **common))
    res = run_bass_kernel_spmd(nc, _prefix_maps(maps, "a0_"), core_ids=list(range(NCORES)))
    return np.concatenate([res.results[r]["y"] for r in range(NCORES)], 0)



def emit_L2a(ctx, stack, x_in, y_out, dbg=False):
    nc = ctx.nc
    E = stack.enter_context
    ident, ones = ctx.ident, ctx.ones
    if dbg:
        d_q32 = ctx.dr("d_q32", [32, SEQ], F32, kind="ExternalOutput").ap()
        d_klat = ctx.dr("d_klat", [32, SEQ], F32, kind="ExternalOutput").ap()
        d_v65 = ctx.dr("d_v65", [128, 128 * 33], F32, kind="ExternalOutput").ap()
        d_part = ctx.dr("d_part", [128, D], F32, kind="ExternalOutput").ap()
        d_rs = ctx.dr("d_rs", [256, D], F32, kind="ExternalOutput").ap()
        d_pub = ctx.dr("d_pub", [NCORES * 256, D], F32, kind="ExternalOutput").ap()
        d_l = ctx.dr("d_l", [128, 1], F32, kind="ExternalOutput").ap()
    wqs_in = ctx.dr("wq_slice", [D, 128], F32, kind="ExternalInput").ap()
    wkd_in = ctx.dr("wkd_slice", [D, 32], F32, kind="ExternalInput").ap()
    wvd_in = ctx.dr("wvd_slice", [D, 32], F32, kind="ExternalInput").ap()
    wkup_in = ctx.dr("wkupT", [128, 32], F32, kind="ExternalInput").ap()
    wvup_in = ctx.dr("wvup", [32, 128], F32, kind="ExternalInput").ap()
    wos_in = ctx.dr("wo_slice", [128, D], F32, kind="ExternalInput").ap()
    m4_in = ctx.dr("mask4", [128, 4 * 512], F32, kind="ExternalInput").ap()
    trl_in = ctx.dr("trilT2", [128, 128], F32, kind="ExternalInput").ap()
    h_pub = [ctx.dr(f"h_pub{j}", [256, D], BF16) for j in range(8)]
    h_ag = [ctx.dr(f"h_ag{j}", [NCORES * 256, D], BF16) for j in range(8)]
    p_pub = [ctx.dr(f"p_pub{b}", [NCORES * 256, D], BF16) for b in range(8)]
    p_rs = [ctx.dr(f"p_rs{b}", [256, D], BF16) for b in range(8)]

    SCALE = float(128 ** -0.5)
    NGT = 128
    wqs = E(ctx.sb("wqs", [128, 8 * 128], BF16))
    wkd = E(ctx.sb("wkd", [128, 8 * 32], BF16))
    wvd = E(ctx.sb("wvd", [128, 8 * 32], BF16))
    wkup = E(ctx.sb("wkup_sb", [128, 32], BF16))
    wvup = E(ctx.sb("wvup_sb", [32, 128], BF16))
    wos = E(ctx.sb("wos", [128, D], BF16))
    wq32 = E(ctx.sb("wq32", [128, 8 * 32], BF16))
    mask4 = E(ctx.sb("mask4_sb", [128, 4 * 512], F32))
    tril2 = E(ctx.sb("tril2", [128, 128], F32))
    q33 = E(ctx.sb("q33", [33, SEQ], BF16))
    klat = E(ctx.sb("klat", [33, SEQ], BF16))

    v65 = E(ctx.sb("v65", [128, NGT * 33], BF16))
    xtile = E(ctx.sb("xtile", [128, D], F32))
    hf = E(ctx.sb("hf", [128, D], F32))
    hbf = E(ctx.sb("hbf", [128, D], BF16))
    hT = E(ctx.sb("hT", [128, D], BF16))
    vl = E(ctx.sb("vl", [32, 128], BF16))
    mrow = E(ctx.sb("mrow", [128, 8], F32))
    mbcol = E(ctx.sb("mbcol", [128, 1], BF16))
    Pbf = E(ctx.sb("Pbf", [128, 1024], BF16))
    y32 = E(ctx.sb("y32", [32, 128], BF16))
    y128 = E(ctx.sb("y128", [128, 128], BF16))
    lrow = E(ctx.sb("lrow", [1, 128], BF16))
    rcol = E(ctx.sb("rcol", [128, 1], F32))
    pout = E(ctx.sb("pout", [128, D], BF16))
    mcol = E(ctx.sb("mcol", [128, 8], F32))
    ptp = E(ctx.ps("ptp", [128, 128], BF16))
    pq3 = E(ctx.ps("pq3", [128, 1024], F32))
    psT = E(ctx.ps("psT", [128, 2048], F32))
    pov = E(ctx.ps("pov", [33, 128], F32))
    pwo = psT

    x_t = x_in.rearrange("(n p) d -> n p d", p=128)
    y_t = y_out.rearrange("(n p) d -> n p d", p=128)
    for w_sb, w_ap, wdt in [(wqs, wqs_in, 128), (wkd, wkd_in, 32), (wvd, wvd_in, 32)]:
        wt = w_ap.rearrange("(c p) d -> c p d", p=128)
        for c in range(8):
            ctx.dma("g", w_sb[:, c * wdt:(c + 1) * wdt], wt[c])
    ctx.dma("g", wkup[:], wkup_in)
    ctx.dma("g", wvup[:], wvup_in)
    ctx.dma("g", wos[:], wos_in)
    ctx.dma("q", mask4[:], m4_in)
    ctx.dma("q", tril2[:], trl_in)
    ctx.run("g", lambda E_: E_.memset(v65[:], 1.0), inc="g")
    ctx.run("g", lambda E_: E_.memset(klat[32:33, :], 1.0), inc="g")
    ctx.wait("v", "gd")
    ctx.wait("v", "qd")
    ctx.wait("t", "gd")
    for c in range(8):
        emit_transpose(ctx, wqs[:, 128 * c:128 * (c + 1)], hT[:, 0:128], ident, ptp)
        ctx.wait("t", "v")
        at = ctx.mm(pq3[:, 0:32], hT[:, 0:128], wkup[:], start=True, stop=True,
                    inc=True)
        ctx.wait("v", "t", at)
        ctx.cp(wq32[:, 32 * c:32 * (c + 1)], pq3[:, 0:32])

    for i in range(NT):
        ctx.wait("q", "v")
        ctx.wait("q", "s")
        ctx.dma("q", xtile[:], x_t[i])
        ctx.wait("v", "qd")
        ctx.wait("s", "qd")
        emit_ln(ctx, xtile[:], hbf[:], hf[:], mcol)
        ctx.wait("g", "v")
        ctx.dma("g", h_pub[i // 2].ap().rearrange("(a p) d -> a p d", p=128)[i % 2],
                hbf[:], inc="gd")
    ctx.wait("g", "gd")
    for j in range(8):
        cc_at = emit_allgather(ctx, h_pub[j].ap(), h_ag[j].ap())
    ctx.wait("q", "cc", cc_at)

    def hag_tile(n):
        r, j, a = n // 16, (n % 16) // 2, n % 2
        return h_ag[j].ap().rearrange("(b p) d -> b p d", p=128)[r * 2 + a]

    for n in range(NGT):
        ctx.wait("q", "v")
        ctx.wait("q", "t")
        ctx.dma("q", hbf[:], hag_tile(n))
        ctx.wait("t", "qd")
        ctx.wait("t", "v")
        emit_transpose_tile8(ctx, hbf[:], hT[:], ident, ptp)
        for (wsb, dst, scale) in [(wq32, q33, SCALE), (wkd, klat, None),
                                  (wvd, None, None)]:
            ctx.wait("t", "v")
            ctx.wait("t", "s")
            for c in range(8):
                ctx.mm(pq3[0:32, 0:128], wsb[:, 32 * c:32 * (c + 1)],
                       hT[:, 128 * c:128 * (c + 1)],
                       start=(c == 0), stop=(c == 7), inc=(c == 7))
            at = ctx.cnt["t"]
            if dst is not None:
                ctx.wait("s", "t", at)
                if scale is not None:
                    ctx.act(dst[0:32, 128 * n:128 * (n + 1)], pq3[0:32, 0:128],
                            AF.Copy, scale=scale)
                else:
                    ctx.cp(dst[0:32, 128 * n:128 * (n + 1)], pq3[0:32, 0:128],
                           eng="s")
            else:
                ctx.wait("v", "t", at)
                ctx.cp(vl[:], pq3[0:32, 0:128])
                ctx.wait("t", "v")
                tat = ctx.transpose(ptp[0:128, 0:32], vl[:], ident[0:32, 0:32])
                ctx.wait("v", "t", tat)
                ctx.cp(v65[:, 33 * n:33 * n + 32], ptp[0:128, 0:32])

    if dbg:
        ctx.wait("g", "s")
        ctx.wait("g", "v")
        ctx.dma("g", d_q32, q33[0:32, :], inc="gd")
        ctx.dma("g", d_klat, klat[0:32, :], inc="gd")
        ctx.dma("g", d_v65, v65[:], inc="gd")
    qrec = {}
    for qt in range(NGT):
        qsl = q33[0:32, 128 * qt:128 * (qt + 1)]
        nb = (qt + 4) // 4
        for b in range(nb):
            last = (b == nb - 1)
            pb = (b % 2) * 512
            if b >= 2:
                ctx.wait("t", "v", qrec["p1v", b - 2])
            elif qt > 0:
                ctx.wait("t", "v")
            at = ctx.mm(pq3[:, pb:pb + 512], qsl,
                        klat[0:32, 512 * b:512 * b + 512],
                        start=True, stop=True, inc=True)
            ctx.wait("v", "t", at)
            if last:
                off = qt - 4 * (nb - 1)
                ctx.tt(pq3[:, pb:pb + 512], pq3[:, pb:pb + 512],
                       mask4[:, 512 * off:512 * (off + 1)], OP.add)
            if b == 0:
                ctx.red(mrow[:, 1:2], pq3[:, pb:pb + 512], OP.max)
            else:
                ctx.red(mrow[:, 0:1], pq3[:, pb:pb + 512], OP.max)
                ctx.tt(mrow[:, 1:2], mrow[:, 1:2], mrow[:, 0:1], OP.max)
            qrec["p1v", b] = ctx.cnt["v"]
        ctx.ts(mrow[:, 2:3], mrow[:, 1:2], -1.0, OP.mult)
        ctx.cp(mbcol[:], mrow[:, 2:3])
        ctx.wait("t", "v")
        tat = ctx.transpose(ptp[0:1, 0:128], mbcol[:], ident[:])
        ctx.wait("v", "t", tat)
        ctx.cp(q33[32:33, 128 * qt:128 * (qt + 1)], ptp[0:1, 0:128])
        q33v = ctx.cnt["v"]
        first_pv = True
        for gi, kt0 in enumerate(range(0, qt + 1, 4)):
            kts = [kt for kt in range(kt0, min(kt0 + 4, qt + 1))]
            gb = (gi % 2) * 512
            if gi >= 2:
                ctx.wait("t", "s", qrec["exps", gi - 2])
                if ("trlv", gi - 2) in qrec:
                    ctx.wait("t", "v", qrec["trlv", gi - 2])
            else:
                ctx.wait("t", "s")
                ctx.wait("t", "v", q33v)
            for j, kt in enumerate(kts):
                ctx.mm(psT[:, gb + 128 * j:gb + 128 * (j + 1)],
                       klat[:, 128 * kt:128 * (kt + 1)],
                       q33[:, 128 * qt:128 * (qt + 1)],
                       start=True, stop=True, inc=(j == len(kts) - 1))
            at = ctx.cnt["t"]
            if kts[-1] == qt:
                j = len(kts) - 1
                ctx.wait("v", "t", at)
                ctx.tt(psT[:, gb + 128 * j:gb + 128 * (j + 1)],
                       psT[:, gb + 128 * j:gb + 128 * (j + 1)],
                       tril2[:], OP.add)
                qrec["trlv", gi] = ctx.cnt["v"]
                ctx.wait("s", "v", qrec["trlv", gi])
            ctx.wait("s", "t", at)
            if gi >= 2:
                ctx.wait("s", "t", qrec["pvt", gi - 2])
            ctx.act(Pbf[:, gb:gb + 512], psT[:, gb:gb + 512], AF.Exp)
            qrec["exps", gi] = ctx.cnt["s"]
            ctx.wait("t", "s", qrec["exps", gi])
            for j, kt in enumerate(kts):
                ctx.mm(pov[:], v65[:, 33 * kt:33 * (kt + 1)],
                       Pbf[:, gb + 128 * j:gb + 128 * (j + 1)],
                       start=first_pv, stop=(kt == qt), inc=(kt == qt))
                first_pv = False
            qrec["pvt", gi] = ctx.cnt["t"]
        pv_at = ctx.cnt["t"]
        ctx.wait("v", "t", pv_at)
        ctx.cp(y32[:], pov[0:32, :])
        ctx.cp(lrow[:], pov[32:33, :])
        ctx.wait("t", "v")
        at = ctx.mm(pq3[:, 0:128], wvup[:], y32[:],
                    start=True, stop=True, inc=True)
        ctx.wait("v", "t", at)
        ctx.cp(y128[:], pq3[:, 0:128])
        ctx.wait("t", "v")
        tat = ctx.transpose(ptp[0:128, 0:1], lrow[:], ident[0:1, 0:1])
        ctx.wait("v", "t", tat)
        ctx.cp(rcol[:], ptp[0:128, 0:1])
        ctx.run("v", lambda E_: E_.reciprocal(out=rcol[:], in_=rcol[:]), inc="v")
        ctx.wait("t", "v")
        for f in range(2):
            ctx.mm(pwo[:, 512 * f:512 * (f + 1)], y128[:],
                   wos[:, 512 * f:512 * (f + 1)], start=True, stop=True,
                   inc=(f == 1))
        at = ctx.cnt["t"]
        ctx.wait("v", "t", at)
        ctx.ts(pout[:], pwo[:, 0:1024], rcol[:], OP.mult)
        if dbg and qt == 5:
            ctx.wait("g", "v")
            ctx.cp(hf[:], pout[:])
            ctx.run("v", lambda E_: E_.reciprocal(out=mrow[:, 7:8], in_=rcol[:]), inc="v")
            ctx.wait("g", "v")
            ctx.dma("g", d_part, hf[:], inc="gd")
            ctx.dma("g", d_l, mrow[:, 7:8], inc="gd")
            ctx.wait("v", "gd")
        r_, b_, a_ = qt // 16, (qt % 16) // 2, qt % 2
        ctx.wait("g", "v")
        ctx.dma("g", p_pub[b_].ap().rearrange("(k p) d -> k p d", p=128)[r_ * 2 + a_],
                pout[:], inc="gd")

    ctx.wait("g", "gd")
    for b in range(8):
        cc2 = ctx.run("g", lambda E_, b=b: E_.collective_compute(
            "ReduceScatter", OP.add, replica_groups=[list(range(NCORES))],
            ins=[p_pub[b].ap().opt()], outs=[p_rs[b].ap().opt()]), inc="cc", n=1)
    ctx.wait("q", "cc", cc2)

    if dbg:
        ctx.wait("g", "cc")
        ctx.dma("g", d_rs, p_rs[0].ap(), inc="gd")
        ctx.dma("g", d_pub, p_pub[0].ap(), inc="gd")
    for i in range(NT):
        ctx.wait("q", "v")
        ctx.wait("q", "od")
        ctx.dma("q", xtile[:], x_t[i])
        ctx.dma("q", hbf[:], p_rs[i // 2].ap().rearrange("(a p) d -> a p d", p=128)[i % 2])
        ctx.wait("v", "qd")
        if dbg and i == 3:
            ctx.cp(hf[:], hbf[:])
            ctx.wait("g", "v")
            ctx.dma("g", d_part, hf[:], inc="gd")
            ctx.wait("v", "gd")
        ctx.tt(xtile[:], xtile[:], hbf[:], OP.add)
        ctx.wait("g", "v")
        ctx.dma("g", y_t[i, :, :], xtile[:], inc="od")




def build_L2a(dbg=False):
    nc = bass.Bass("TRN2", target_bir_lowering=False, debug=False)
    x_in = nc.dram_tensor("x", [TOK, D], F32, kind="ExternalInput").ap()
    y_out = nc.dram_tensor("y", [TOK, D], F32, kind="ExternalOutput").ap()
    ctx = Ctx(nc)
    with ExitStack() as stack:
        ctx.alloc_sems(stack)
        ctx.ident, ctx.ones = emit_prelude(ctx, stack)
        ctx.scope = "a2_"
        emit_L2a(ctx, stack, x_in, y_out, dbg)
        ctx.wait("q", "od", ctx.cnt["od"])
        ctx.flush()
    return nc

def run_L2a(x_full, INP):
    nc = build_L2a()
    xs = _shard_x(x_full)
    wq = np.asarray(INP["L2_attn_wq"], np.float32)
    wkd = np.asarray(INP["L2_attn_wk_down"], np.float32)
    wvd = np.asarray(INP["L2_attn_wv_down"], np.float32)
    wkup = np.asarray(INP["L2_attn_wk_up"], np.float32)
    wvup = np.asarray(INP["L2_attn_wv_up"], np.float32)
    wo = np.asarray(INP["L2_attn_wo"], np.float32)
    m4 = np.zeros((128, 4, 512), np.float32)
    qi = np.arange(128)[:, None]
    for o in range(4):
        m4[:, o, (o + 1) * 128:] = -1e30
        ki = np.arange(128)[None, :]
        m4[:, o, o * 128:(o + 1) * 128] = np.where(ki <= qi, 0.0, -1e30)
    trl = np.where(np.arange(128)[:, None] <= np.arange(128)[None, :], 0.0, -1e30)
    maps = []
    for r in range(NCORES):
        kv = r // 4
        maps.append(dict(
            x=xs[r],
            wq_slice=np.ascontiguousarray(wq[:, r * 128:(r + 1) * 128]),
            wkd_slice=np.ascontiguousarray(wkd[:, kv * 32:(kv + 1) * 32]),
            wvd_slice=np.ascontiguousarray(wvd[:, kv * 32:(kv + 1) * 32]),
            wkupT=np.ascontiguousarray(wkup.T),
            wvup=np.ascontiguousarray(wvup),
            wo_slice=np.ascontiguousarray(wo[r * 128:(r + 1) * 128, :]),
            mask4=np.ascontiguousarray(m4.reshape(128, 4 * 512)),
            trilT2=np.ascontiguousarray(trl, np.float32),
        ))
    res = run_bass_kernel_spmd(nc, _prefix_maps(maps, "a2_"), core_ids=list(range(NCORES)))
    return np.concatenate([res.results[r]["y"] for r in range(NCORES)], 0)



def emit_L3f(ctx, stack, x_in, y_out, dbg=False, nph=12):
    nc = ctx.nc
    E = stack.enter_context
    ident, ones = ctx.ident, ctx.ones
    if dbg:
        d_P = ctx.dr("d_P", [128, 512], F32, kind="ExternalOutput").ap()
        d_sc = ctx.dr("d_sc", [128, 512], F32, kind="ExternalOutput").ap()
    wqkv_in = ctx.dr("wqkv", [6, D, 3072], F32, kind="ExternalInput").ap()
    wo_in = ctx.dr("wol", [6, D, D], F32, kind="ExternalInput").ap()
    w1_in = ctx.dr("w1l", [6, D, 5460], F32, kind="ExternalInput").ap()
    b1_in = ctx.dr("b1l", [6, 5460], F32, kind="ExternalInput").ap()
    w2_in = ctx.dr("w2l", [6, 2730, D], F32, kind="ExternalInput").ap()
    b2_in = ctx.dr("b2l", [6, D], F32, kind="ExternalInput").ap()
    qs_in = ctx.dr("qsl", [6, 64], F32, kind="ExternalInput").ap()
    ks_in = ctx.dr("ksl", [6, 64], F32, kind="ExternalInput").ap()
    lng_in = ctx.dr("lngl", [6, D], F32, kind="ExternalInput").ap()
    lnb_in = ctx.dr("lnbl", [6, D], F32, kind="ExternalInput").ap()
    tcq_in = ctx.dr("tabcq", [128, 19 * 64], F32, kind="ExternalInput").ap()
    tsq_in = ctx.dr("tabsq", [128, 19 * 64], F32, kind="ExternalInput").ap()
    tck_in = ctx.dr("tabck", [128, 19 * 64], F32, kind="ExternalInput").ap()
    tsk_in = ctx.dr("tabsk", [128, 19 * 64], F32, kind="ExternalInput").ap()
    msk_in = ctx.dr("masks", [128, 4 * 1024], F32, kind="ExternalInput").ap()
    hidx_in = ctx.dr("hidx", [128, 1], U32, kind="ExternalInput").ap()
    hv_in = ctx.dr("hv", [1, 1], F32, kind="ExternalInput").ap()
    xd = [ctx.dr(f"xbuf{b}", [TOK, D], F32) for b in range(2)]
    xh = ctx.dr("xh0", [TOK, D], F32)
    hp = [ctx.dr(f"hpub{j}", [128, D], BF16) for j in range(3)]
    ha = [ctx.dr(f"hag{j}", [NCORES * 128, D], BF16) for j in range(3)]


    def barrier():
        for e in ENGS:
            for k in ("t", "v", "s", "g", "qd", "gd", "cc", "od"):
                ctx.wait(e, k)

    tabs = E(ctx.sb("tabs", [128, 4 * 19 * 64], BF16))
    masks = E(ctx.sb("masks_sb", [128, 4 * 1024], BF16))
    hidx = E(ctx.sb("hidx_sb", [128, 1], U32))
    rowst = E(ctx.sb("rowst", [1, D], F32))
    rowbf = E(ctx.sb("rowbf", [1, D], BF16))
    gbc = E(ctx.sb("gbc", [128, D], BF16))
    bbc = E(ctx.sb("bbc", [128, D], BF16))
    qsbc = E(ctx.sb("qsbc", [128, 64], F32))
    ksbc = E(ctx.sb("ksbc", [128, 64], F32))
    qsr = E(ctx.sb("qsr", [128, 64], F32))
    ksr = E(ctx.sb("ksr", [128, 64], F32))
    AB = E(ctx.sb("AB", [128, 4 * 64], F32))
    xa = E(ctx.sb("xa", [128, D], F32))
    sh = E(ctx.sb("shf", [128, D], F32))
    shbf = E(ctx.sb("shbf", [128, D], BF16))
    shT = E(ctx.sb("shT", [128, D], BF16))
    prowf = E(ctx.sb("prowf", [1, 512], F32))
    hal = E(ctx.sb("hal", [128, 3 * D], BF16))
    kf = E(ctx.sb("kf", [128, D], F32))
    rot = E(ctx.sb("rot", [128, D], F32))
    rn = E(ctx.sb("rn", [128, 16], F32))
    rope = E(ctx.sb("rope", [128, D], BF16))
    kTr = E(ctx.sb("kTr", [128, 8 * 4 * 128], BF16))
    vr = E(ctx.sb("vr", [128, 4 * D], BF16))
    qT = E(ctx.sb("qT", [128, 8 * 128], BF16))
    Pbf = E(ctx.sb("Pbf2", [128, 2 * 512], BF16))
    PT = E(ctx.sb("PT", [128, 2 * 512], BF16))
    ytile = E(ctx.sb("ytile", [128, D], BF16))
    yT = E(ctx.sb("yT", [128, D], BF16))
    mcol = E(ctx.sb("mcol2", [128, 16], F32))
    tf2 = E(ctx.sb("tf2", [128, 512], BF16))
    agbf = E(ctx.sb("agbf", [128, 2 * 512], BF16))
    agT = E(ctx.sb("agT2", [128, 512], BF16))
    ptpd = [E(ctx.ps(f"ptp2_{j}", [128, 512], BF16)) for j in range(2)]
    ptp = ptpd[0]
    psc = E(ctx.ps("psc", [128, 2048], F32))

    ctx.dma("g", tabs[:, 0:19 * 64], tcq_in)
    ctx.dma("g", tabs[:, 19 * 64:2 * 19 * 64], tsq_in)
    ctx.dma("g", tabs[:, 2 * 19 * 64:3 * 19 * 64], tck_in)
    ctx.dma("g", tabs[:, 3 * 19 * 64:4 * 19 * 64], tsk_in)
    ctx.dma("g", masks[:], msk_in)
    ctx.dma("q", hidx[:], hidx_in)
    hv = E(ctx.sb("hv_sb", [1, 1], F32))
    ctx.dma("q", hv[:], hv_in)

    def bcast_row(src_ap, dst_bc, wid):
        ctx.wait("q", "v")
        ctx.wait("q", "t")
        ctx.dma("q", rowst[:, 0:wid], src_ap)
        ctx.wait("v", "qd")
        ctx.cp(rowbf[:, 0:wid], rowst[:, 0:wid])
        emit_bcast_f32(ctx, rowbf[:, 0:wid], dst_bc, ones, psc)

    def emit_shift(e, src_dram_tile, from_halo):
        if from_halo:
            ctx.wait("g", "v")
            ctx.wait("g", "t")
            ctx.wait("g", "s")
            ctx.dma("g", sh[:, 0:512], hal[:, (e * D):(e * D) + 512])
            ctx.dma("g", sh[1:128, 512:1024], hal[0:127, e * D + 512:e * D + 1024])
            ctx.dma("g", sh[0:1, 512:1024], prowf[:])
            ctx.wait("g", "gd")
            ctx.dma("g", prowf[:], hal[127:128, e * D + 512:e * D + 1024])
            ctx.wait("v", "gd")
            ctx.ts(prowf[:], prowf[:], hv[0:1, 0:1], OP.mult)
            ctx.wait("t", "gd")
            ctx.wait("s", "gd")
        else:
            ctx.wait("q", "v")
            ctx.wait("q", "t")
            ctx.wait("q", "s")
            ctx.wait("q", "od")
            ctx.wait("q", "gd")
            ctx.wait("g", "v")
            ctx.dma("q", xa[:], src_dram_tile)
            ctx.wait("q", "qd")
            ctx.dma("q", sh[:, 0:512], xa[:, 0:512])
            ctx.dma("q", sh[1:128, 512:1024], xa[0:127, 512:1024])
            ctx.dma("q", sh[0:1, 512:1024], prowf[:])
            ctx.wait("q", "qd")
            ctx.dma("q", prowf[:], xa[127:128, 512:1024])
            ctx.wait("v", "qd")
            ctx.wait("t", "qd")
            ctx.wait("s", "qd")

    def l2norm_rope(tab_base, dst_bf):
        ctx.tt(rot[:], kf[:], kf[:], OP.mult)
        ctx.red(rn[:], rot.ap().rearrange("p (h d) -> p h d", d=64), OP.add)
        ctx.wait("s", "v")
        ctx.act(rn[:], rn[:], AF.Sqrt)
        ctx.wait("v", "s")
        ctx.ts(rn[:], rn[:], 1e-12, OP.max)
        ctx.run("v", lambda E_: E_.reciprocal(out=rn[:], in_=rn[:]), inc="v")
        v3 = kf.ap().rearrange("p (h a d) -> p h a d", a=2, d=32)
        r3 = rot.ap().rearrange("p (h a d) -> p h a d", a=2, d=32)
        ctx.ts(r3[:, :, 0, :], v3[:, :, 1, :], -1.0, OP.mult)
        ctx.cp(r3[:, :, 1, :], v3[:, :, 0, :])
        rnb = rn.ap().unsqueeze(2).broadcast_to([128, 16, 64])
        Ab = AB.ap()[:, tab_base:tab_base + 64].unsqueeze(1).broadcast_to([128, 16, 64])
        Bb = AB.ap()[:, tab_base + 64:tab_base + 128].unsqueeze(1).broadcast_to([128, 16, 64])
        k3 = kf.ap().rearrange("p (h d) -> p h d", d=64)
        ro3 = rot.ap().rearrange("p (h d) -> p h d", d=64)
        t3 = sh.ap().rearrange("p (h d) -> p h d", d=64)
        d3 = dst_bf.ap().rearrange("p (h d) -> p h d", d=64)
        ctx.tt(t3, k3, rnb, OP.mult)
        ctx.tt(t3, t3, Ab, OP.mult)
        ctx.tt(ro3, ro3, rnb, OP.mult)
        ctx.tt(ro3, ro3, Bb, OP.mult)
        ctx.tt(d3, t3, ro3, OP.add)

    ctx.run("g", lambda E_: E_.memset(kTr[:], 0.0), inc="g")
    ctx.run("g", lambda E_: E_.memset(vr[:], 0.0), inc="g")
    ctx.run("g", lambda E_: E_.memset(qT[:], 0.0), inc="g")
    xin_t = x_in.rearrange("(n p) d -> n p d", p=128)
    xh_t = xh.ap().rearrange("(n p) d -> n p d", p=128)
    for i in range(NT):
        ctx.wait("q", "v")
        ctx.wait("q", "s")
        ctx.wait("q", "gd")
        ctx.dma("q", xa[:], xin_t[i])
        ctx.wait("v", "qd")
        ctx.wait("s", "qd")
        emit_ln(ctx, xa[:], shbf[:], sh[:], mcol, out_f32=kf[:])
        ctx.wait("g", "v")
        ctx.dma("g", xh_t[i, :, :], kf[:], inc="gd")

    for ph in range(nph):
        li = ph // 2
        is_attn = (ph % 2 == 0)
        src = xh.ap() if ph == 0 else xd[(ph - 1) % 2].ap()
        dst = xd[ph % 2].ap()
        src_t = src.rearrange("(n p) d -> n p d", p=128)
        dst_t = dst.rearrange("(n p) d -> n p d", p=128)
        nhalo = 3 if is_attn else 1
        barrier()
        for j in range(nhalo):
            ctx.dma("g", hp[j].ap(), src_t[NT - nhalo + j])
        ctx.wait("g", "gd")
        for j in range(nhalo):
            cc_at = emit_allgather(ctx, hp[j].ap(), ha[j].ap())
        ctx.wait("g", "cc", cc_at)
        ctx.run("g", lambda E_: E_.memset(hal[:], 0.0), inc="g")
        ctx.wait("g", "g")
        for j in range(nhalo):
            slot = 3 - nhalo + j
            ctx.run("g", lambda E_, j=j, slot=slot: E_.indirect_dma_start(
                out=hal[:, slot * D:(slot + 1) * D],
                out_offset=None, in_=ha[j].ap(),
                in_offset=bass.IndirectOffsetOnAxis(ap=hidx[:], axis=0)),
                inc="gd", n=16)
        barrier()

        if is_attn:
          with ExitStack() as pstack:
            PE = pstack.enter_context
            wqkv = PE(nc.sbuf_tensor(f"wqkv_sb{ph}", [128, 8 * 3072], BF16))
            wosb = PE(nc.sbuf_tensor(f"wosb{ph}", [128, 8 * D], BF16))
            povd = [PE(nc.psum_tensor(f"pov{ph}_{j}", [128, 64], F32))
                    for j in range(2)]
            wq_t = wqkv_in[li].rearrange("(c p) f -> c p f", p=128)
            for c in range(8):
                ctx.dma("g", wqkv[:, c * 3072:(c + 1) * 3072], wq_t[c])
            wo_t = wo_in[li].rearrange("(c p) f -> c p f", p=128)
            for c in range(8):
                ctx.dma("g", wosb[:, c * D:(c + 1) * D], wo_t[c])
            bcast_row(qs_in[li:li + 1, :], qsbc[:, 0:64], 64)
            bcast_row(ks_in[li:li + 1, :], ksbc[:, 0:64], 64)
            ctx.cp(qsr[:, 0:32], qsbc[:, 32:64])
            ctx.cp(qsr[:, 32:64], qsbc[:, 0:32])
            ctx.cp(ksr[:, 0:32], ksbc[:, 32:64])
            ctx.cp(ksr[:, 32:64], ksbc[:, 0:32])
            ctx.wait("t", "gd")
            ctx.wait("g", "v")
            ctx.dma("g", prowf[:], hal[127:128, 512:1024])
            ctx.wait("q", "gd")
            ctx.wait("v", "gd")
            ctx.ts(prowf[:], prowf[:], hv[0:1, 0:1], OP.mult)

            kTr3 = kTr.ap().rearrange("p (hp g) -> p hp g", g=512)

            def pair_transposes(src_bf, dst_of_round):
                for rnd in range(2):
                    pb = rnd % 2
                    ctx.wait("t", "v")
                    for j in range(4):
                        hp = rnd * 4 + j
                        t_at = ctx.transpose(ptpd[pb][:, 128 * j:128 * (j + 1)],
                                             src_bf[:, 128 * hp:128 * (hp + 1)],
                                             ident[0:128, 0:128])
                    ctx.wait("v", "t", t_at)
                    ctx.cp(dst_of_round(rnd),
                           ptpd[pb].ap().rearrange("p (a f) -> p a f", f=128))

            rec = {}
            for e in range(1, 19):
                toff = (e) * 64
                ctx.wait("v", "qd")
                ctx.wait("v", "gd")
                ctx.tt(AB[:, 0:64], tabs[:, toff:toff + 64], qsbc[:, 0:64], OP.mult)
                ctx.tt(AB[:, 64:128], tabs[:, 19 * 64 + toff:19 * 64 + toff + 64],
                       qsr[:, 0:64], OP.mult)
                ctx.tt(AB[:, 128:192], tabs[:, 2 * 19 * 64 + toff:2 * 19 * 64 + toff + 64],
                       ksbc[:, 0:64], OP.mult)
                ctx.tt(AB[:, 192:256], tabs[:, 3 * 19 * 64 + toff:3 * 19 * 64 + toff + 64],
                       ksr[:, 0:64], OP.mult)
                emit_shift(e, src_t[e - 3] if e >= 3 else None, from_halo=(e < 3))
                ctx.cp(shbf[:], sh[:])
                ctx.wait("t", "v")
                emit_transpose_tile8(ctx, shbf[:], shT[:], ident, ptpd[0])
                slot = e % 4
                ctx.wait("t", "v")
                ctx.wait("t", "s")
                for f in range(2):
                    for c in range(8):
                        ctx.mm(psc[:, 512 * f:512 * (f + 1)],
                               shT[:, 128 * c:128 * (c + 1)],
                               wqkv[:, c * 3072 + 1024 + 512 * f:c * 3072 + 1024 + 512 * (f + 1)],
                               start=(c == 0), stop=(c == 7), inc=(c == 7))
                kmm_t = ctx.cnt["t"]
                for f in range(2):
                    for c in range(8):
                        ctx.mm(psc[:, 1024 + 512 * f:1536 + 512 * f],
                               shT[:, 128 * c:128 * (c + 1)],
                               wqkv[:, c * 3072 + 2048 + 512 * f:c * 3072 + 2048 + 512 * (f + 1)],
                               start=(c == 0), stop=(c == 7), inc=(c == 7))
                vmm_t = ctx.cnt["t"]
                ctx.wait("v", "t", kmm_t)
                ctx.cp(kf[:], psc[:, 0:1024])
                kfcp_v = ctx.cnt["v"]
                l2norm_rope(128, rope)
                pair_transposes(rope[:], lambda rnd: kTr3[
                    :, rnd * 4:rnd * 4 + 4, slot * 128:slot * 128 + 128])
                ringT_t = ctx.cnt["t"]
                ctx.wait("v", "t", vmm_t)
                ctx.cp(vr[:, slot * D:(slot + 1) * D], psc[:, 1024:2048])
                if e < 3:
                    continue
                i = e - 3
                ctx.wait("t", "v", kfcp_v)
                for f in range(2):
                    for c in range(8):
                        ctx.mm(psc[:, 512 * f:512 * (f + 1)],
                               shT[:, 128 * c:128 * (c + 1)],
                               wqkv[:, c * 3072 + 512 * f:c * 3072 + 512 * (f + 1)],
                               start=(c == 0), stop=(c == 7), inc=(c == 7))
                qmm_t = ctx.cnt["t"]
                ctx.wait("v", "t", qmm_t)
                ctx.cp(kf[:], psc[:, 0:1024])
                ctx.wait("v", "t", ringT_t)
                l2norm_rope(0, rope)
                pair_transposes(rope[:], lambda rnd: qT.ap().rearrange(
                    "p (a f) -> p a f", f=128)[:, rnd * 4:rnd * 4 + 4, :])
                setup_v = ctx.cnt["v"]
                mslot = i if i < 2 else 2
                moff = mslot * 1024 + ((4 - (i % 4)) % 4) * 128
                for h in range(16):
                    b = h % 4
                    pb = h % 2
                    hb = (h % 2) * 64
                    hp_ = (h // 2) * 128
                    if h < 4:
                        ctx.wait("t", "v", setup_v)
                    else:
                        ctx.wait("t", "v", rec["vdone", h - 4])
                        ctx.wait("t", "s", rec["sdone", h - 4])
                    if h >= 2:
                        ctx.wait("t", "v", rec["scale", h - 2])
                        ctx.wait("t", "v", rec["ptev", h - 2])
                    sc_t = ctx.mm(psc[:, b * 512:(b + 1) * 512],
                                  qT[hb:hb + 64, hp_:hp_ + 128],
                                  kTr[hb:hb + 64, (h // 2) * 512:(h // 2) * 512 + 512],
                                  start=True, stop=True, inc=True)
                    ctx.wait("v", "t", sc_t)
                    ctx.tt(psc[:, b * 512:(b + 1) * 512],
                           psc[:, b * 512:(b + 1) * 512],
                           masks[:, moff:moff + 512], OP.add)
                    ctx.red(mcol[:, 4 + b:5 + b], psc[:, b * 512:(b + 1) * 512],
                            OP.max, negate=True)
                    rec["vdone", h] = ctx.cnt["v"]
                    ctx.wait("s", "v", rec["vdone", h])
                    if h >= 2:
                        ctx.wait("s", "t", rec["trdone", h - 2])
                    ctx.act(Pbf[:, pb * 512:(pb + 1) * 512],
                            psc[:, b * 512:(b + 1) * 512], AF.Exp,
                            bias=mcol[:, 4 + b:5 + b], accum=mcol[:, 8 + b:9 + b])
                    rec["sdone", h] = ctx.cnt["s"]
                    ctx.wait("v", "s", rec["sdone", h])
                    ctx.run("v", lambda E_, b=b: E_.reciprocal(
                        out=mcol[:, 12 + b:13 + b], in_=mcol[:, 8 + b:9 + b]), inc="v")
                    ctx.wait("t", "s", rec["sdone", h])
                    for s in range(4):
                        t_at = ctx.transpose(
                            ptpd[pb][:, 128 * s:128 * (s + 1)],
                            Pbf[:, pb * 512 + 128 * s:pb * 512 + 128 * (s + 1)],
                            ident[0:128, 0:128])
                    rec["trdone", h] = ctx.cnt["t"]
                    ctx.wait("v", "t", rec["trdone", h])
                    ctx.cp(PT[:, pb * 512:(pb + 1) * 512], ptpd[pb][:])
                    rec["ptev", h] = ctx.cnt["v"]
                    ctx.wait("t", "v", rec["ptev", h])
                    for s in range(4):
                        ctx.mm(povd[pb][:],
                               PT[:, pb * 512 + 128 * s:pb * 512 + 128 * (s + 1)],
                               vr[:, s * D + 64 * h:s * D + 64 * (h + 1)],
                               start=(s == 0), stop=(s == 3), inc=(s == 3))
                    pv_t = ctx.cnt["t"]
                    ctx.wait("v", "t", pv_t)
                    ctx.ts(ytile[:, 64 * h:64 * (h + 1)], povd[pb][:, 0:64],
                           mcol[:, 12 + b:13 + b], OP.mult)
                    rec["scale", h] = ctx.cnt["v"]
                ctx.wait("t", "v")
                ctx.wait("t", "s")
                emit_transpose_tile8(ctx, ytile[:], yT[:], ident, ptpd[0])
                ctx.wait("t", "v")
                for f in range(2):
                    for c in range(8):
                        ctx.mm(psc[:, 512 * f:512 * (f + 1)],
                               yT[:, 128 * c:128 * (c + 1)],
                               wosb[:, c * D + 512 * f:c * D + 512 * (f + 1)],
                               start=(c == 0), stop=(c == 7), inc=(c == 7))
                at = ctx.cnt["t"]
                ctx.wait("v", "t", at)
                ctx.tt(xa[:], xa[:], psc[:, 0:1024], OP.add)
                ctx.wait("g", "v")
                ctx.dma("g", dst_t[i], xa[:], inc="od")
        else:
          with ExitStack() as pstack:
            PE = pstack.enter_context
            xnT_all = PE(nc.sbuf_tensor(f"xnT_all{ph}", [128, NT * D], BF16))
            yacc = PE(nc.sbuf_tensor(f"yacc{ph}", [128, NT * D], BF16))
            w1a = [PE(nc.sbuf_tensor(f"w1a{ph}_{j}", [128, 8 * 512], BF16))
                   for j in range(2)]
            w1g = [PE(nc.sbuf_tensor(f"w1g{ph}_{j}", [128, 8 * 512], BF16))
                   for j in range(2)]
            w2c = PE(nc.sbuf_tensor(f"w2c{ph}", [128, 4 * D], BF16))
            b1r = PE(nc.sbuf_tensor(f"b1r{ph}", [1, 5460], BF16))
            b2bc = PE(nc.sbuf_tensor(f"b2bc{ph}", [128, D], BF16))
            pw2 = PE(nc.psum_tensor(f"pw2_{ph}", [128, 1024], F32))
            bcast_row(lng_in[li:li + 1, :], gbc[:], D)
            bcast_row(lnb_in[li:li + 1, :], bbc[:], D)
            ctx.wait("g", "v")
            ctx.dma("g", b1r[:], b1_in[li:li + 1, :])
            ctx.wait("g", "gd")
            ctx.dma("g", prowf[:], hal[127:128, 2 * D + 512:3 * D])
            ctx.wait("v", "gd")
            ctx.ts(prowf[:], prowf[:], hv[0:1, 0:1], OP.mult)
            bcast_row(b2_in[li:li + 1, :], b2bc[:], D)
            ctx.wait("q", "gd")
            for i in range(NT):
                emit_shift(3 + i if False else i, src_t[i], from_halo=False)
                ctx.wait("s", "qd")
                emit_ln(ctx, sh[:], shbf[:], kf[:], mcol, gain_bc=gbc[:],
                        bias_bc=bbc[:])
                ctx.wait("t", "v")
                emit_transpose_tile8(ctx, shbf[:], xnT_all[:, i * D:(i + 1) * D],
                                     ident, ptp)
                ctx.wait("v", "t")
                ctx.cp(yacc[:, i * D:(i + 1) * D], b2bc[:])
            fcs = [(0, 512), (512, 512), (1024, 512), (1536, 512), (2048, 512),
                   (2560, 170)]
            w1_t = w1_in[li].rearrange("(c p) f -> c p f", p=128)
            frec = {}

            def load_w1(ci):
                fcb, fcw = fcs[ci]
                cb = ci % 2
                if ci >= 2:
                    ctx.wait("g", "t", frec["w2t", ci - 2])
                for c in range(8):
                    ctx.dma("g", w1a[cb][:, c * 512:c * 512 + fcw],
                            w1_t[c, :, fcb:fcb + fcw])
                    ctx.dma("g", w1g[cb][:, c * 512:c * 512 + fcw],
                            w1_t[c, :, 2730 + fcb:2730 + fcb + fcw])
                frec["gdw1", ci] = ctx.cnt["gd"]

            def load_w2(ci):
                fcb, fcw = fcs[ci]
                if ci >= 1:
                    ctx.wait("g", "t", frec["w2t", ci - 1])
                for c in range((fcw + 127) // 128):
                    rr0 = fcb + c * 128
                    rw = min(128, fcb + fcw - rr0)
                    ctx.dma("g", w2c[0:rw, c * D:(c + 1) * D],
                            w2_in[li, rr0:rr0 + rw, :])
                frec["gdw2", ci] = ctx.cnt["gd"]

            ctx.wait("g", "t")
            load_w1(0)
            load_w2(0)
            if len(fcs) > 1:
                load_w1(1)
            for ci, (fcb, fcw) in enumerate(fcs):
                cb = ci % 2
                nw2 = (fcw + 127) // 128
                ctx.wait("t", "gd", frec["gdw1", ci])
                ctx.wait("t", "gd", frec["gdw2", ci])
                if ci == 0:
                    ctx.wait("t", "v")
                    ctx.wait("t", "s")
                for i in range(NT):
                    gt = ci * NT + i
                    wb = (gt % 2) * 1024
                    if ("sg", gt - 2) in frec:
                        ctx.wait("t", "s", frec["sg", gt - 2])
                        ctx.wait("t", "v", frec["av", gt - 2])
                    ctx.mm(psc[:, wb:wb + fcw], ones[:], b1r[:, fcb:fcb + fcw],
                           start=True, stop=False)
                    ctx.mm(psc[:, wb + 512:wb + 512 + fcw], ones[:],
                           b1r[:, 2730 + fcb:2730 + fcb + fcw],
                           start=True, stop=False)
                    for c in range(8):
                        ctx.mm(psc[:, wb:wb + fcw],
                               xnT_all[:, i * D + 128 * c:i * D + 128 * (c + 1)],
                               w1a[cb][:, c * 512:c * 512 + fcw],
                               start=False, stop=False)
                        ctx.mm(psc[:, wb + 512:wb + 512 + fcw],
                               xnT_all[:, i * D + 128 * c:i * D + 128 * (c + 1)],
                               w1g[cb][:, c * 512:c * 512 + fcw],
                               start=False, stop=(c == 7), inc=(c == 7))
                    w1t = ctx.cnt["t"]
                    ctx.wait("s", "t", w1t)
                    ctx.act(tf2[:, 0:fcw], psc[:, wb + 512:wb + 512 + fcw],
                            AF.Gelu)
                    frec["sg", gt] = ctx.cnt["s"]
                    ab = (gt % 2) * 512
                    ctx.wait("v", "s", frec["sg", gt])
                    ctx.wait("v", "t", w1t)
                    ctx.tt(agbf[:, ab:ab + fcw], psc[:, wb:wb + fcw],
                           tf2[:, 0:fcw], OP.mult)
                    frec["av", gt] = ctx.cnt["v"]
                    ctx.wait("t", "v", frec["av", gt])
                    for c in range(nw2):
                        cw = min(128, fcw - c * 128)
                        ctx.transpose(ptpd[gt % 2][0:cw, 128 * c:128 * (c + 1)],
                                      agbf[:, ab + 128 * c:ab + 128 * c + cw],
                                      ident[0:128, 0:128])
                    trt = ctx.cnt["t"]
                    ctx.wait("v", "t", trt)
                    ctx.cp(agT[:, 0:128 * nw2], ptpd[gt % 2][:, 0:128 * nw2])
                    agv = ctx.cnt["v"]
                    ctx.wait("t", "v", agv)
                    for f in range(2):
                        for c in range(nw2):
                            cw = min(128, fcw - c * 128)
                            ctx.mm(pw2[:, 512 * f:512 * (f + 1)],
                                   agT[0:cw, 128 * c:128 * (c + 1)],
                                   w2c[0:cw, c * D + 512 * f:c * D + 512 * (f + 1)],
                                   start=(c == 0), stop=(c == nw2 - 1),
                                   inc=(c == nw2 - 1))
                    frec["w2t", ci] = ctx.cnt["t"]
                    ctx.wait("v", "t", frec["w2t", ci])
                    ctx.tt(yacc[:, i * D:(i + 1) * D], yacc[:, i * D:(i + 1) * D],
                           pw2[:, 0:1024], OP.add)
                    if i == NT - 1:
                        if ci + 1 < len(fcs):
                            load_w2(ci + 1)
                        if ci + 2 < len(fcs):
                            load_w1(ci + 2)
            for i in range(NT):
                ctx.wait("q", "v")
                ctx.wait("q", "od")
                ctx.dma("q", xa[:], src_t[i])
                ctx.wait("v", "qd")
                ctx.tt(xa[:], xa[:], yacc[:, i * D:(i + 1) * D], OP.add)
                ctx.wait("g", "v")
                ctx.dma("g", dst_t[i], xa[:], inc="od")

    barrier()
    xlast = xd[(nph - 1) % 2].ap().rearrange("(n p) d -> n p d", p=128)
    for i in range(NT):
        ctx.wait("q", "v")
        ctx.wait("q", "od")
        ctx.dma("q", xa[:], xin_t[i])
        ctx.dma("q", sh[:], xlast[i, :, :])
        ctx.wait("v", "qd")
        ctx.tt(xa[:], xa[:], sh[:], OP.add)
        ctx.wait("g", "v")
        ctx.dma("g", y_out.rearrange("(n p) d -> n p d", p=128)[i, :, :], xa[:],
                inc="od")




def build_L3f(dbg=False, nph=12):
    nc = bass.Bass("TRN2", target_bir_lowering=False, debug=False)
    x_in = nc.dram_tensor("x", [TOK, D], F32, kind="ExternalInput").ap()
    y_out = nc.dram_tensor("y", [TOK, D], F32, kind="ExternalOutput").ap()
    ctx = Ctx(nc)
    with ExitStack() as stack:
        ctx.alloc_sems(stack)
        ctx.ident, ctx.ones = emit_prelude(ctx, stack)
        ctx.scope = "f3_"
        emit_L3f(ctx, stack, x_in, y_out, dbg, nph)
        ctx.wait("q", "od", ctx.cnt["od"])
        ctx.flush()
    return nc

def run_L3f(x_full, INP, nc=None):
    nc = nc or build_L3f()
    xs = _shard_x(x_full)
    DIM_HEAD, WINDOW = 64, 256
    half = np.arange(0, DIM_HEAD, 2, dtype=np.float64)
    inv_freq = 1.0 / 10000.0 ** (half / DIM_HEAD)
    base_scale = (half + 0.4 * DIM_HEAD) / (1.4 * DIM_HEAD)

    def tables(pos):
        t = pos.astype(np.float64)
        freqs = t[:, None] * inv_freq[None, :]
        freqs = np.concatenate([freqs, freqs], -1)
        power = (t - 16384 // 2) / (WINDOW // 2)
        scale = base_scale[None, :] ** power[:, None]
        scale = np.concatenate([scale, scale], -1)
        cos, sin = np.cos(freqs), np.sin(freqs)
        return (cos * scale * 8.0, sin * scale * 8.0, cos / scale, sin / scale)

    def win_mask(i, first_block):
        m = np.full((128, 512), -1e30, np.float32)
        for p in range(128):
            qa = 384 + p
            lo = qa - 256
            if first_block and i < 2:
                lo = max(lo, 384 - 128 * i)
            a = np.arange(512)
            ok = (a >= lo) & (a <= qa)
            m[p, ok] = 0.0
        return m

    res_maps = []
    common = dict(
        wqkv=np.ascontiguousarray(INP["L3_ffn_Wqkv"], np.float32),
        wol=np.ascontiguousarray(INP["L3_ffn_Wo"], np.float32),
        w1l=np.ascontiguousarray(INP["L3_ffn_W1"], np.float32),
        b1l=np.ascontiguousarray(INP["L3_ffn_b1"], np.float32),
        w2l=np.ascontiguousarray(INP["L3_ffn_W2"], np.float32),
        b2l=np.ascontiguousarray(INP["L3_ffn_b2"], np.float32),
        qsl=np.ascontiguousarray(INP["L3_ffn_q_scale"], np.float32),
        ksl=np.ascontiguousarray(INP["L3_ffn_k_scale"], np.float32),
        lngl=np.ascontiguousarray(INP["L3_ffn_ln_g"], np.float32),
        lnbl=np.ascontiguousarray(INP["L3_ffn_ln_b"], np.float32),
    )
    for r in range(NCORES):
        base = r * TOK
        cq = np.zeros((128, 19 * 64), np.float32)
        sq = np.zeros((128, 19 * 64), np.float32)
        ck = np.zeros((128, 19 * 64), np.float32)
        sk = np.zeros((128, 19 * 64), np.float32)
        for e in range(19):
            pos = base - 384 + e * 128 + np.arange(128)
            pos = np.maximum(pos, 0)
            a, b, c, d = tables(pos)
            cq[:, e * 64:(e + 1) * 64] = a
            sq[:, e * 64:(e + 1) * 64] = b
            ck[:, e * 64:(e + 1) * 64] = c
            sk[:, e * 64:(e + 1) * 64] = d
        mk = np.zeros((4, 128, 512), np.float32)
        mk[0] = win_mask(0, r == 0)
        mk[1] = win_mask(1, r == 0)
        mk[2] = win_mask(2, False)
        mk[3] = win_mask(3, False)
        mkd = np.zeros((4, 128, 8, 128), np.float32)
        for t in range(8):
            mkd[:, :, t, :] = mk[:, :, (t % 4) * 128:(t % 4) * 128 + 128]
        hidx = np.full((128, 1), 0, np.uint32)
        if r > 0:
            hidx[:, 0] = (r - 1) * 128 + np.arange(128)
        res_maps.append(dict(x=xs[r], hv=np.full((1, 1), 0.0 if r == 0 else 1.0, np.float32),
                             tabcq=cq, tabsq=sq, tabck=ck, tabsk=sk,
                             masks=np.ascontiguousarray(mkd.transpose(1, 0, 2, 3).reshape(128, 4096)),
                             hidx=hidx, **common))
    res = run_bass_kernel_spmd(nc, _prefix_maps(res_maps, "f3_"), core_ids=list(range(NCORES)))
    return np.concatenate([res.results[r]["y"] for r in range(NCORES)], 0)




def _prefix_maps(maps, pre):
    return [{(k if k == "x" else pre + k): v for k, v in m.items()} for m in maps]


def _shard_x(x):
    x = np.ascontiguousarray(x.reshape(SEQ, D), dtype=np.float32)
    return [x[r * TOK:(r + 1) * TOK] for r in range(NCORES)]


def run_L0f(x_full, w, b):
    nc = build_L0f()
    xs = _shard_x(x_full)
    maps = [dict(x=xs[r], w=np.ascontiguousarray(w, np.float32),
                 b=np.ascontiguousarray(b.reshape(1, D), np.float32))
            for r in range(NCORES)]
    res = run_bass_kernel_spmd(nc, _prefix_maps(maps, "f0_"), core_ids=list(range(NCORES)))
    return np.concatenate([res.results[r]["y"] for r in range(NCORES)], 0)




def prep_L0f(INP):
    m = dict(w=np.ascontiguousarray(INP["L0_ffn_weight"], np.float32),
             b=np.ascontiguousarray(np.asarray(INP["L0_ffn_bias"]).reshape(1, D), np.float32))
    return [m for _ in range(NCORES)]


def prep_L0a(INP):
    T = lambda a: np.ascontiguousarray(np.asarray(a, np.float32).T)
    R1 = lambda a: np.ascontiguousarray(np.asarray(a, np.float32).reshape(1, -1))
    common = dict(
        qkvuT=T(INP["L0_attn_qkvu_w"]), qkvub=R1(INP["L0_attn_qkvu_b"]),
        outwT=T(INP["L0_attn_out_w"]), outb=R1(INP["L0_attn_out_b"]),
        slope=R1(INP["L0_attn_slope_rate"]), theta=R1(INP["L0_attn_theta"]),
        trilT=np.ascontiguousarray(np.triu(np.ones((128, 128)), 0), np.float32),
    )
    out = []
    for r in range(NCORES):
        tok = np.arange(TOK, dtype=np.float32) + r * TOK
        tokcol = np.ascontiguousarray(tok.reshape(NT, 128).T)
        wprev = np.zeros((128, 8), np.float32)
        wprev[:, :r] = 1.0
        out.append(dict(tokcol=tokcol, wprev=wprev, **common))
    return out


def prep_L1a(INP):
    T = lambda a: np.ascontiguousarray(np.asarray(a, np.float32).T)
    R1 = lambda a: np.ascontiguousarray(np.asarray(a, np.float32).reshape(1, -1))
    common = dict(
        wkT=T(INP["L1_attn_Wk"]), wvT=T(INP["L1_attn_Wv"]), wqT=T(INP["L1_attn_Wq"]),
        wgT=T(INP["L1_attn_Wg"]), woT=T(INP["L1_attn_Wo"]),
        w0=np.ascontiguousarray(INP["L1_attn_W0"], np.float32),
        ainit=np.ascontiguousarray(INP["L1_attn_A_init"], np.float32),
        binit=np.ascontiguousarray(INP["L1_attn_B_init"], np.float32),
        lng=R1(INP["L1_attn_ln_g"]), lnb=R1(INP["L1_attn_ln_b"]),
        loglr=R1(INP["L1_attn_log_lr"]), lsg=R1(INP["L1_attn_ls_g"]),
        strictLT=np.ascontiguousarray(np.triu(np.ones((128, 128)), 1), np.float32),
    )
    out = []
    for r in range(NCORES):
        pos = (np.arange(TOK) + r * TOK + 1).astype(np.float64)
        ps = (1.0 / (1.0 + 0.1 * np.log(pos))).astype(np.float32)
        poscol = np.ascontiguousarray(ps.reshape(NT, 128).T)
        maskp = np.zeros((8, 1), np.float32)
        maskp[:r] = 1.0
        out.append(dict(poscol=poscol, maskprev=maskp, **common))
    return out


def prep_L1f(INP):
    T = lambda a: np.ascontiguousarray(np.asarray(a, np.float32).T)
    m = dict(
        rmsw=np.ascontiguousarray(np.asarray(INP["L1_ffn_rms_w"]).reshape(1, D), np.float32),
        gwT=T(INP["L1_ffn_gate_w"]),
        w1=np.ascontiguousarray(INP["L1_ffn_W1"], np.float32),
        b1=np.ascontiguousarray(INP["L1_ffn_b1"], np.float32),
        w2=np.ascontiguousarray(INP["L1_ffn_W2"], np.float32),
        b2=np.ascontiguousarray(INP["L1_ffn_b2"], np.float32),
    )
    return [m for _ in range(NCORES)]


def prep_L2a(INP):
    wq = np.asarray(INP["L2_attn_wq"], np.float32)
    wkd = np.asarray(INP["L2_attn_wk_down"], np.float32)
    wvd = np.asarray(INP["L2_attn_wv_down"], np.float32)
    wkup = np.asarray(INP["L2_attn_wk_up"], np.float32)
    wvup = np.asarray(INP["L2_attn_wv_up"], np.float32)
    wo = np.asarray(INP["L2_attn_wo"], np.float32)
    m4 = np.zeros((128, 4, 512), np.float32)
    qi = np.arange(128)[:, None]
    for o in range(4):
        m4[:, o, (o + 1) * 128:] = -1e30
        ki = np.arange(128)[None, :]
        m4[:, o, o * 128:(o + 1) * 128] = np.where(ki <= qi, 0.0, -1e30)
    trl = np.where(np.arange(128)[:, None] <= np.arange(128)[None, :], 0.0,
                   -1e30).astype(np.float32)
    out = []
    for r in range(NCORES):
        kv = r // 4
        out.append(dict(
            wq_slice=np.ascontiguousarray(wq[:, r * 128:(r + 1) * 128]),
            wkd_slice=np.ascontiguousarray(wkd[:, kv * 32:(kv + 1) * 32]),
            wvd_slice=np.ascontiguousarray(wvd[:, kv * 32:(kv + 1) * 32]),
            wkupT=np.ascontiguousarray(wkup.T), wvup=np.ascontiguousarray(wvup),
            wo_slice=np.ascontiguousarray(wo[r * 128:(r + 1) * 128, :]),
            mask4=np.ascontiguousarray(m4.reshape(128, 2048)), trilT2=trl))
    return out


def prep_L2f(INP):
    T = lambda a: np.ascontiguousarray(np.asarray(a, np.float32).T)
    SC = lambda a, n: np.ascontiguousarray(np.asarray(a, np.float32).reshape(n, 128).T)
    scl = np.concatenate([SC(INP["L2_ffn_w1_s"], 32), SC(INP["L2_ffn_w2_s"], 32),
                          SC(INP["L2_ffn_h1_s"], 32), SC(INP["L2_ffn_h2_s"], 32),
                          SC(INP["L2_ffn_w3_s"], 8)], axis=1)
    m = dict(w1T=T(INP["L2_ffn_w1_w"]), w2T=T(INP["L2_ffn_w2_w"]),
             h1T=T(INP["L2_ffn_h1_w"]), h2T=T(INP["L2_ffn_h2_w"]),
             w3T=T(INP["L2_ffn_w3_w"]), scl=np.ascontiguousarray(scl))
    return [m for _ in range(NCORES)]


def prep_L3a(INP):
    T = lambda a: np.ascontiguousarray(np.asarray(a, np.float32).T)
    R = lambda a, s: np.ascontiguousarray(np.asarray(a, np.float32).reshape(s))
    m = dict(
        wvT=T(INP["L3_attn_Wv"]), w1T=T(INP["L3_attn_W1"]),
        w2T=T(INP["L3_attn_W2"]), woT=T(INP["L3_attn_Wo"]),
        sigT=T(INP["L3_attn_head_signatures"]),
        bv=R(INP["L3_attn_bv"], (1, D)), b1=R(INP["L3_attn_b1"], (1, D)),
        b2=R(INP["L3_attn_b2"], (1, 64)), bo=R(INP["L3_attn_bo"], (1, D)),
    )
    return [m for _ in range(NCORES)]


def prep_L3f(INP):
    DIM_HEAD, WINDOW = 64, 256
    half = np.arange(0, DIM_HEAD, 2, dtype=np.float64)
    inv_freq = 1.0 / 10000.0 ** (half / DIM_HEAD)
    base_scale = (half + 0.4 * DIM_HEAD) / (1.4 * DIM_HEAD)

    def tables(pos):
        t = pos.astype(np.float64)
        freqs = t[:, None] * inv_freq[None, :]
        freqs = np.concatenate([freqs, freqs], -1)
        power = (t - 16384 // 2) / (WINDOW // 2)
        scale = base_scale[None, :] ** power[:, None]
        scale = np.concatenate([scale, scale], -1)
        cos, sin = np.cos(freqs), np.sin(freqs)
        return (cos * scale * 8.0, sin * scale * 8.0, cos / scale, sin / scale)

    def win_mask(i, first_block):
        m = np.full((128, 512), -1e30, np.float32)
        for p in range(128):
            qa = 384 + p
            lo = qa - 256
            if first_block and i < 2:
                lo = max(lo, 384 - 128 * i)
            a = np.arange(512)
            ok = (a >= lo) & (a <= qa)
            m[p, ok] = 0.0
        return m

    common = dict(
        wqkv=np.ascontiguousarray(INP["L3_ffn_Wqkv"], np.float32),
        wol=np.ascontiguousarray(INP["L3_ffn_Wo"], np.float32),
        w1l=np.ascontiguousarray(INP["L3_ffn_W1"], np.float32),
        b1l=np.ascontiguousarray(INP["L3_ffn_b1"], np.float32),
        w2l=np.ascontiguousarray(INP["L3_ffn_W2"], np.float32),
        b2l=np.ascontiguousarray(INP["L3_ffn_b2"], np.float32),
        qsl=np.ascontiguousarray(INP["L3_ffn_q_scale"], np.float32),
        ksl=np.ascontiguousarray(INP["L3_ffn_k_scale"], np.float32),
        lngl=np.ascontiguousarray(INP["L3_ffn_ln_g"], np.float32),
        lnbl=np.ascontiguousarray(INP["L3_ffn_ln_b"], np.float32),
    )
    out = []
    for r in range(NCORES):
        base = r * TOK
        cq = np.zeros((128, 19 * 64), np.float32)
        sq = np.zeros((128, 19 * 64), np.float32)
        ck = np.zeros((128, 19 * 64), np.float32)
        sk = np.zeros((128, 19 * 64), np.float32)
        for e in range(19):
            pos = base - 384 + e * 128 + np.arange(128)
            pos = np.maximum(pos, 0)
            a, b, c, d = tables(pos)
            cq[:, e * 64:(e + 1) * 64] = a
            sq[:, e * 64:(e + 1) * 64] = b
            ck[:, e * 64:(e + 1) * 64] = c
            sk[:, e * 64:(e + 1) * 64] = d
        mk = np.zeros((4, 128, 512), np.float32)
        mk[0] = win_mask(0, r == 0)
        mk[1] = win_mask(1, r == 0)
        mk[2] = win_mask(2, False)
        mk[3] = win_mask(3, False)
        mkd = np.zeros((4, 128, 8, 128), np.float32)
        for t in range(8):
            mkd[:, :, t, :] = mk[:, :, (t % 4) * 128:(t % 4) * 128 + 128]
        hidx = np.full((128, 1), 0, np.uint32)
        if r > 0:
            hidx[:, 0] = (r - 1) * 128 + np.arange(128)
        out.append(dict(hv=np.full((1, 1), 0.0 if r == 0 else 1.0, np.float32),
                        tabcq=cq, tabsq=sq, tabck=ck, tabsk=sk,
                        masks=np.ascontiguousarray(mkd.transpose(1, 0, 2, 3).reshape(128, 4096)),
                        hidx=hidx, **common))
    return out


FUSE_PLAN = [("a0_", emit_L0a, prep_L0a), ("f0_", emit_L0f, prep_L0f),
             ("a1_", emit_L1a, prep_L1a), ("f1_", emit_L1f, prep_L1f),
             ("a2_", emit_L2a, prep_L2a), ("f2_", emit_L2f, prep_L2f),
             ("a3_", emit_L3a, prep_L3a), ("f3_", emit_L3f, prep_L3f)]


def build_fused():
    nc = bass.Bass("TRN2", target_bir_lowering=False, debug=False)
    x_in = nc.dram_tensor("x", [TOK, D], F32, kind="ExternalInput").ap()
    y_out = nc.dram_tensor("y", [TOK, D], F32, kind="ExternalOutput").ap()
    chain = [nc.dram_tensor(f"chain{b}", [TOK, D], F32) for b in range(2)]
    ctx = Ctx(nc)
    with ExitStack() as stack:
        ctx.alloc_sems(stack)
        ctx.ident, ctx.ones = emit_prelude(ctx, stack)
        nblk = len(FUSE_PLAN)
        for i, (pre, fn, _) in enumerate(FUSE_PLAN):
            src = x_in if i == 0 else chain[(i - 1) % 2].ap()
            dst = y_out if i == nblk - 1 else chain[i % 2].ap()
            ctx.scope = pre
            with ExitStack() as bstack:
                fn(ctx, bstack, src, dst)
                for e in ENGS:
                    for k in SEMS:
                        ctx.wait(e, k)
        ctx.wait("q", "od", ctx.cnt["od"])
        ctx.flush()
    return nc


_FUSED_CACHE = {}


def kernel_fused(**inputs):
    if "nc" not in _FUSED_CACHE:
        _FUSED_CACHE["nc"] = build_fused()
    nc = _FUSED_CACHE["nc"]
    x = np.ascontiguousarray(np.asarray(inputs["L0_attn_x"], np.float32).reshape(SEQ, D))
    xs = _shard_x(x)
    preps = [(pre, pf(inputs)) for (pre, _, pf) in FUSE_PLAN]
    maps = []
    for r in range(NCORES):
        m = {"x": xs[r]}
        for pre, pm in preps:
            for k, v in pm[r].items():
                m[pre + k] = v
        maps.append(m)
    res = run_bass_kernel_spmd(nc, maps, core_ids=list(range(NCORES)))
    y = np.concatenate([res.results[r]["y"] for r in range(NCORES)], 0)
    return y.reshape(1, SEQ, D)


def kernel(**inputs):
    _ = (inputs["L3_attn_Wq"], inputs["L3_attn_Wk"],
         inputs["L3_attn_bq"], inputs["L3_attn_bk"])
    return kernel_fused(**inputs)


# version 10 (final)
# speedup vs baseline: 1.2877x; 1.0283x over previous
import numpy as np
import ml_dtypes
from contextlib import ExitStack
import concourse.bass as bass
import concourse.mybir as mybir
from concourse.alu_op_type import AluOpType as OP
from concourse.bass_utils import run_bass_kernel_spmd

F32 = mybir.dt.float32
BF16 = mybir.dt.bfloat16
I32 = mybir.dt.int32
U32 = mybir.dt.uint32
AF = mybir.ActivationFunctionType
AX = mybir.AxisListType

NCORES = 8
SEQ = 16384
D = 1024
TOK = SEQ // NCORES
NT = TOK // 128
BF = ml_dtypes.bfloat16

ENGS = ("t", "v", "s", "g", "q")
SEMS = ("t", "v", "s", "g", "qd", "gd", "cc", "od", "wd")


class Ctx:

    def __init__(self, nc):
        self.nc = nc
        self.eng = {"t": nc.tensor, "v": nc.vector, "s": nc.scalar,
                    "g": nc.gpsimd, "q": nc.sync}
        self.progs = {k: [] for k in ENGS}
        self.cnt = {k: 0 for k in SEMS}
        self.seen = {e: {} for e in ENGS}
        self.sem = {}
        self.stack = None

    def alloc_sems(self, stack):
        for k in SEMS:
            self.sem[k] = stack.enter_context(self.nc.semaphore(f"sem_{k}"))
        self.scope = ""

    def sb(self, name, shape, dtype, **kw):
        return self.nc.sbuf_tensor(self.scope + name, shape, dtype, **kw)

    def ps(self, name, shape, dtype, **kw):
        return self.nc.psum_tensor(self.scope + name, shape, dtype, **kw)

    def dr(self, name, *a, **kw):
        return self.nc.dram_tensor(self.scope + name, *a, **kw)

    def run(self, eng, f, inc=None, n=1):
        if eng in ("v", "s", "g"):
            self.wait(eng, eng)
        if inc is not None:
            self.cnt[inc] += n
            tgt = self.cnt[inc]
            sem = self.sem

            def thunk(E, f=f, inc=inc, n=n):
                inst = f(E)
                inst.then_inc(sem[inc], n)
        else:
            def thunk(E, f=f):
                f(E)
        self.progs[eng].append(thunk)
        return self.cnt[inc] if inc is not None else None

    def wait(self, eng, key, val=None):
        val = self.cnt[key] if val is None else val
        if val <= 0 or self.seen[eng].get(key, 0) >= val:
            return
        self.seen[eng][key] = val
        sem = self.sem
        self.progs[eng].append(lambda E, key=key, val=val: E.wait_ge(sem[key], val))

    def barrier_from(self, eng, keys):
        for k in keys:
            self.wait(eng, k)

    def flush(self):
        names = {"t": "tensor", "v": "vector", "s": "scalar", "g": "gpsimd", "q": "sync"}
        with self.nc.Block() as block:
            for e in ENGS:
                prog = self.progs[e]
                if not prog:
                    continue

                def section(E, prog=prog):
                    for th in prog:
                        th(E)
                getattr(block, names[e])(section)

    def dma(self, queue, out, in_, inc=None):
        inc = inc or ("qd" if queue == "q" else "gd")
        return self.run(queue, lambda E: E.dma_start(out=out, in_=in_), inc=inc, n=16)

    def ts(self, out, in0, s1, op0, s2=None, op1=None, accum=None, eng="v"):
        kw = {}
        if op1 is not None:
            kw["op1"] = op1
        return self.run(eng, lambda E: E.tensor_scalar(
            out=out, in0=in0, scalar1=s1, scalar2=s2, op0=op0,
            accum_out=accum, **kw), inc=eng)

    def tt(self, out, in0, in1, op, eng="v"):
        return self.run(eng, lambda E: E.tensor_tensor(out=out, in0=in0, in1=in1, op=op),
                        inc=eng)

    def stt(self, out, in0, scalar, in1, op0, op1, accum=None, eng="v"):
        return self.run(eng, lambda E: E.scalar_tensor_tensor(
            out=out, in0=in0, scalar=scalar, in1=in1, op0=op0, op1=op1,
            accum_out=accum), inc=eng)

    def cp(self, out, in_, eng="v"):
        if eng == "s":
            return self.run("s", lambda E: E.copy(out=out, in_=in_), inc="s")
        return self.run(eng, lambda E: E.tensor_copy(out=out, in_=in_), inc=eng)

    def act(self, out, in_, func, bias=None, scale=None, accum=None):
        kw = {}
        if bias is not None:
            kw["bias"] = bias
        if scale is not None:
            kw["scale"] = scale
        return self.run("s", lambda E: E.activation(out=out, in_=in_, func=func,
                                                    accum_out=accum, **kw), inc="s")

    def red(self, out, in_, op, absval=False, negate=False, ax=None):
        kw = {}
        if absval:
            kw["apply_absolute_value"] = True
        if negate:
            kw["negate"] = True
        ax = ax or AX.X
        return self.run("v", lambda E: E.tensor_reduce(out=out, in_=in_, axis=ax,
                                                       op=op, **kw), inc="v")

    def mm(self, out, lhsT, rhs, start, stop, inc=False):
        self.wait("t", "g")
        return self.run("t", lambda E: E.matmul(out, lhsT, rhs, start=start, stop=stop),
                        inc="t" if inc else None)

    def transpose(self, out_psum, in_, ident, inc=True):
        self.wait("t", "g")
        return self.run("t", lambda E: E.transpose(out_psum, in_, ident),
                        inc="t" if inc else None)



def emit_prelude(ctx, stack):
    nc = ctx.nc
    ident = stack.enter_context(nc.sbuf_tensor("c_ident", [128, 128], BF16))
    ones = stack.enter_context(nc.sbuf_tensor("c_ones", [1, 128], BF16))
    ii = stack.enter_context(nc.sbuf_tensor("c_iota", [128, 128], I32))
    ctx.run("g", lambda E: E.iota(ii[:], pattern=[[-1, 128]], base=0,
                                  channel_multiplier=1), inc="g")
    ctx.run("g", lambda E: E.memset(ones[:], 1.0), inc="g")
    ctx.wait("v", "g")
    ctx.ts(ident[:], ii[:], 0.0, OP.is_equal)
    return ident, ones


def emit_ln(ctx, x_ap, out_bf, scr_fp32, mcol, eps=1e-5, out_f32=None,
            gain_bc=None, bias_bc=None):
    n = x_ap.shape[-1]
    ctx.red(mcol[:, 0:1], x_ap, OP.add)
    ctx.ts(mcol[:, 0:1], mcol[:, 0:1], 1.0 / n, OP.mult)
    ctx.ts(scr_fp32, x_ap, mcol[:, 0:1], OP.subtract)
    ctx.wait("s", "v")
    ctx.act(out_bf, scr_fp32, AF.Square, accum=mcol[:, 1:2])
    ctx.wait("v", "s")
    ctx.ts(mcol[:, 1:2], mcol[:, 1:2], 1.0 / n, OP.mult, eps, OP.add)
    ctx.wait("s", "v")
    ctx.act(mcol[:, 2:3], mcol[:, 1:2], AF.Sqrt)
    ctx.wait("v", "s")
    ctx.run("v", lambda E: E.reciprocal(out=mcol[:, 3:4], in_=mcol[:, 2:3]), inc="v")
    if gain_bc is None:
        ctx.ts(out_bf, scr_fp32, mcol[:, 3:4], OP.mult)
        if out_f32 is not None:
            ctx.ts(out_f32, scr_fp32, mcol[:, 3:4], OP.mult)
    else:
        tgt = out_f32 if out_f32 is not None else scr_fp32
        ctx.ts(scr_fp32, scr_fp32, mcol[:, 3:4], OP.mult)
        if bias_bc is not None:
            ctx.tt(scr_fp32, scr_fp32, gain_bc, OP.mult)
            ctx.tt(tgt if out_f32 is not None else scr_fp32, scr_fp32, bias_bc, OP.add)
            src = tgt if out_f32 is not None else scr_fp32
            ctx.cp(out_bf, src)
        else:
            ctx.tt(tgt, scr_fp32, gain_bc, OP.mult)
            ctx.cp(out_bf, tgt)
    return ctx.cnt["v"]


_SQUARE_DUMP = {}


def emit_transpose(ctx, src_ap, dst_ap, ident, psum_bf):
    p_out, f_out = src_ap.shape[-1], src_ap.shape[0]
    ctx.wait("t", "v")
    ctx.wait("t", "s")
    t_at = ctx.transpose(psum_bf[0:p_out, 0:f_out], src_ap,
                         ident[0:src_ap.shape[0], 0:src_ap.shape[0]])
    ctx.wait("v", "t", t_at)
    ctx.cp(dst_ap, psum_bf[0:p_out, 0:f_out])


def emit_bcast_row(ctx, row_ap, out_ap, ones, psum_f32, n=None):
    F = row_ap.shape[-1]
    assert F <= 512
    t_at = ctx.mm(psum_f32[:, 0:F], ones[:], row_ap, start=True, stop=True, inc=True)
    ctx.wait("v", "t", t_at)
    ctx.cp(out_ap, psum_f32[:, 0:F])


def emit_quantize_tile(ctx, X, XD, scr):
    y, b1, b2, t1 = scr["y"], scr["b1"], scr["b2"], scr["t1"]
    ints = scr["ints"]
    amax, s, inv = scr["amax"], scr["s"], scr["inv"]
    ctx.wait("v", "g")
    ctx.wait("v", "s")
    x3 = X.rearrange("p (b j) -> p b j", j=32)
    ctx.red(amax[:], x3, OP.max, absval=True)
    ctx.ts(s[:], amax[:], 1.0 / 7.5, OP.mult, 1e-30, OP.max)
    ctx.run("v", lambda E: E.reciprocal(out=inv[:], in_=s[:]), inc="v")
    for b in range(32):
        ctx.ts(y[:, 32 * b:32 * b + 32], X[:, 32 * b:32 * b + 32],
               inv[:, b:b + 1], OP.mult)
    ctx.tt(t1[:], y[:], y[:], OP.mult)
    ctx.ts(b1[:], t1[:], 4.0, OP.is_lt)
    ctx.ts(b2[:], t1[:], 16.0, OP.is_lt)
    ctx.ts(t1[:], b1[:], 1.0, OP.add, 2.0, OP.mult)
    ctx.stt(b2[:], b2[:], 1.0, t1[:], OP.add, OP.mult)
    ctx.tt(y[:], y[:], b2[:], OP.mult)
    ctx.cp(ints[:], y[:])
    ctx.cp(t1[:], ints[:])
    ctx.tt(b1[:], y[:], t1[:], OP.subtract)
    ctx.ts(y[:], b1[:], 0.5, OP.is_ge)
    ctx.stt(t1[:], y[:], 1.0, t1[:], OP.mult, OP.add)
    ctx.ts(y[:], b1[:], -0.5, OP.is_le)
    ctx.stt(t1[:], y[:], -1.0, t1[:], OP.mult, OP.add)
    ctx.run("v", lambda E: E.reciprocal(out=b2[:], in_=b2[:]), inc="v")
    ctx.wait("v", "s")
    ctx.tt(t1[:], t1[:], b2[:], OP.mult)
    ctx.ts(t1[:], t1[:], 7.5, OP.min, -7.5, OP.max)
    for b in range(32):
        ctx.ts(XD[:, 32 * b:32 * b + 32], t1[:, 32 * b:32 * b + 32],
               s[:, b:b + 1], OP.mult)
    return ctx.cnt["v"]



def emit_L0f(ctx, stack, x_in, y_out):
    nc = ctx.nc
    E = stack.enter_context
    ident, ones = ctx.ident, ctx.ones
    w_in = ctx.dr("w", [D, D], F32, kind="ExternalInput").ap()
    b_in = ctx.dr("b", [1, D], F32, kind="ExternalInput").ap()

    xs = E(ctx.sb("xs", [128, NT * D], F32))
    wf = E(ctx.sb("wf", [128, 8 * D], F32))
    wd = E(ctx.sb("wd", [128, 8 * D], BF16))
    wdT = E(ctx.sb("wdT", [128, 8 * D], BF16))
    brow = E(ctx.sb("brow", [1, D], BF16))
    h = E(ctx.sb("h", [128, D], F32))
    xd = E(ctx.sb("xd", [128, D], BF16))
    xdT = E(ctx.sb("xdT", [128, D], BF16))
    mcol = E(ctx.sb("mcol", [128, 4], F32))
    scr = {
        "y": E(ctx.sb("q_y", [128, D], F32)),
        "b1": E(ctx.sb("q_b1", [128, D], F32)),
        "b2": E(ctx.sb("q_b2", [128, D], F32)),
        "t1": E(ctx.sb("q_t1", [128, D], F32)),
        "ints": E(ctx.sb("q_i", [128, D], I32)),
        "amax": E(ctx.sb("q_am", [128, 32], F32)),
        "s": E(ctx.sb("q_s", [128, 32], F32)),
        "inv": E(ctx.sb("q_iv", [128, 32], F32)),
    }
    ptp = E(ctx.ps("ptp", [128, 128], BF16))
    pmm = E(ctx.ps("pmm", [128, 1024], F32))

    x_t = x_in.rearrange("(n p) d -> n p d", p=128)
    w_t = w_in.rearrange("(c p) d -> c p d", p=128)
    for i in range(NT):
        ctx.dma("q", xs[:, i * D:(i + 1) * D], x_t[i])
    for c in range(8):
        ctx.dma("q", wf[:, c * D:(c + 1) * D], w_t[c])
    ctx.dma("g", brow[:], b_in)
    ld_at = ctx.cnt["qd"]

    ctx.wait("v", "qd", ld_at)
    for c in range(8):
        ctx.wait("g", "qd", ld_at)
        ctx.wait("s", "qd", ld_at)
        emit_quantize_tile(ctx, wf[:, c * D:(c + 1) * D], wd[:, c * D:(c + 1) * D], scr)
    for r in range(8):
        for c in range(8):
            ctx.wait("t", "v")
            emit_transpose(ctx, wd[:, r * D + 128 * c: r * D + 128 * (c + 1)],
                           wdT[:, c * D + 128 * r: c * D + 128 * (r + 1)], ident, ptp)

    for i in range(NT):
        xt = xs[:, i * D:(i + 1) * D]
        emit_ln(ctx, xt, xd[:], h[:], mcol)
        v_ln = ctx.cnt["v"]
        ctx.ts(scr["t1"][:], h[:], mcol[:, 3:4], OP.mult)
        ctx.cp(h[:], scr["t1"][:])
        emit_quantize_tile(ctx, h[:], xd[:], scr)
        v_q = ctx.cnt["v"]
        for c in range(8):
            ctx.wait("t", "v", v_q)
            emit_transpose(ctx, xd[:, 128 * c:128 * (c + 1)],
                           xdT[:, 128 * c:128 * (c + 1)], ident, ptp)
        v_tp = ctx.cnt["v"]
        ctx.wait("t", "gd")
        ctx.wait("t", "v", v_tp)
        for f in range(2):
            ctx.mm(pmm[:, f * 512:(f + 1) * 512], ones[:], brow[:, f * 512:(f + 1) * 512],
                   start=True, stop=False)
            for c in range(8):
                ctx.mm(pmm[:, f * 512:(f + 1) * 512], xdT[:, 128 * c:128 * (c + 1)],
                       wdT[:, c * D + f * 512: c * D + (f + 1) * 512],
                       start=False, stop=(c == 7), inc=(c == 7))
        t_mm = ctx.cnt["t"]
        ctx.wait("v", "t", t_mm)
        ctx.tt(xt, xt, pmm[:], OP.add)
        ctx.wait("g", "v")
        ctx.dma("g", y_out.rearrange("(n p) d -> n p d", p=128)[i, :, :], xt, inc="od")





def build_L0f():
    nc = bass.Bass("TRN2", target_bir_lowering=False, debug=False)
    x_in = nc.dram_tensor("x", [TOK, D], F32, kind="ExternalInput").ap()
    y_out = nc.dram_tensor("y", [TOK, D], F32, kind="ExternalOutput").ap()
    ctx = Ctx(nc)
    with ExitStack() as stack:
        ctx.alloc_sems(stack)
        ctx.ident, ctx.ones = emit_prelude(ctx, stack)
        ctx.scope = "f0_"
        emit_L0f(ctx, stack, x_in, y_out)
        ctx.wait("q", "od", ctx.cnt["od"])
        ctx.flush()
    return nc

def emit_transpose_tile8(ctx, src_bf, dst_bf, ident, ptp, n=8):
    grp = max(1, min(4, ptp[:].shape[-1] // 128))
    c = 0
    while c < n:
        g = min(grp, n - c)
        ctx.wait("t", "v")
        ctx.wait("t", "s")
        for j in range(g):
            t_at = ctx.transpose(ptp[:, 128 * j:128 * (j + 1)],
                                 src_bf[:, 128 * (c + j):128 * (c + j + 1)],
                                 ident[0:128, 0:128])
        ctx.wait("v", "t", t_at)
        ctx.cp(dst_bf[:, 128 * c:128 * (c + g)], ptp[:, 0:128 * g])
        c += g


def emit_proj(ctx, xT, W, psum, f_sizes, nc_=8, bias=None, ones=None, xoff=0,
              close=True):
    ctx.wait("t", "v")
    ctx.wait("t", "s")
    f0 = 0
    for fs in f_sizes:
        if bias is not None:
            ctx.mm(psum[:, f0:f0 + fs], ones[:], bias[:, f0:f0 + fs],
                   start=True, stop=False)
        for c in range(nc_):
            F = W.shape[-1] // nc_
            ctx.mm(psum[:, f0:f0 + fs], xT[:, xoff + 128 * c:xoff + 128 * (c + 1)],
                   W[:, c * F + f0:c * F + f0 + fs],
                   start=(c == 0 and bias is None),
                   stop=(c == nc_ - 1 and close),
                   inc=(c == nc_ - 1 and close))
        f0 += fs


def emit_L3a(ctx, stack, x_in, y_out, dbg=False):
    nc = ctx.nc
    E = stack.enter_context
    ident, ones = ctx.ident, ctx.ones
    if dbg:
        d_h = ctx.dr("d_h", [128, D], F32, kind="ExternalOutput").ap()
        d_v = ctx.dr("d_v", [128, D], F32, kind="ExternalOutput").ap()
        d_t2 = ctx.dr("d_t2", [128, 64], F32, kind="ExternalOutput").ap()
        d_gate = ctx.dr("d_gate", [128, 16], F32, kind="ExternalOutput").ap()
        d_vg = ctx.dr("d_vg", [128, D], F32, kind="ExternalOutput").ap()
    wv_in = ctx.dr("wvT", [D, D], F32, kind="ExternalInput").ap()
    w1_in = ctx.dr("w1T", [D, D], F32, kind="ExternalInput").ap()
    w2_in = ctx.dr("w2T", [D, 64], F32, kind="ExternalInput").ap()
    wo_in = ctx.dr("woT", [D, D], F32, kind="ExternalInput").ap()
    sig_in = ctx.dr("sigT", [64, 16], F32, kind="ExternalInput").ap()
    bv_in = ctx.dr("bv", [1, D], F32, kind="ExternalInput").ap()
    b1_in = ctx.dr("b1", [1, D], F32, kind="ExternalInput").ap()
    b2_in = ctx.dr("b2", [1, 64], F32, kind="ExternalInput").ap()
    bo_in = ctx.dr("bo", [1, D], F32, kind="ExternalInput").ap()

    xs = E(ctx.sb("xs", [128, NT * D], F32))
    wv = E(ctx.sb("wv", [128, 8 * D], BF16))
    w1 = E(ctx.sb("w1", [128, 8 * D], BF16))
    w2 = E(ctx.sb("w2", [128, 8 * 64], BF16))
    wo = E(ctx.sb("wo", [128, 8 * D], BF16))
    sig = E(ctx.sb("sig", [64, 16], BF16))
    brows = E(ctx.sb("brows", [1, 3 * D + 64], BF16))
    h = E(ctx.sb("h", [128, D], BF16))
    hf = E(ctx.sb("hf", [128, D], F32))
    hT = E(ctx.sb("hT", [128, D], BF16))
    vt = E(ctx.sb("vt", [128, D], F32))
    t1 = E(ctx.sb("t1", [128, D], BF16))
    t1T = E(ctx.sb("t1T", [128, D], BF16))
    t2 = E(ctx.sb("t2", [128, 64], BF16))
    t2T = E(ctx.sb("t2T", [64, 128], BF16))
    gate = E(ctx.sb("gate", [128, 16], F32))
    vg = E(ctx.sb("vg", [128, D], BF16))
    vgT = E(ctx.sb("vgT", [128, D], BF16))
    mcol = E(ctx.sb("mcol", [128, 8], F32))
    ptp = E(ctx.ps("ptp", [128, 128], BF16))
    pmm = E(ctx.ps("pmm", [128, 1024], F32))
    pmm2 = E(ctx.ps("pmm2", [128, 1024], F32))
    psm = E(ctx.ps("psm", [128, 128], F32))

    x_t = x_in.rearrange("(n p) d -> n p d", p=128)
    y_t = y_out.rearrange("(n p) d -> n p d", p=128)
    for i in range(NT):
        ctx.dma("q", xs[:, i * D:(i + 1) * D], x_t[i])
    for w_sb, w_ap in [(wv, wv_in), (w1, w1_in), (wo, wo_in)]:
        wt = w_ap.rearrange("(c p) d -> c p d", p=128)
        for c in range(8):
            ctx.dma("g", w_sb[:, c * D:(c + 1) * D], wt[c])
    w2t = w2_in.rearrange("(c p) d -> c p d", p=128)
    for c in range(8):
        ctx.dma("g", w2[:, c * 64:(c + 1) * 64], w2t[c])
    ctx.dma("g", sig[:], sig_in)
    ctx.dma("g", brows[:, 0:D], bv_in)
    ctx.dma("g", brows[:, D:2 * D], b1_in)
    ctx.dma("g", brows[:, 2 * D:2 * D + 64], b2_in)
    ctx.dma("g", brows[:, 2 * D + 64:3 * D + 64], bo_in)
    ld_q, ld_g = ctx.cnt["qd"], ctx.cnt["gd"]
    ctx.wait("v", "qd", ld_q)
    ctx.wait("t", "gd", ld_g)
    bv_r = brows[:, 0:D]
    b1_r = brows[:, D:2 * D]
    b2_r = brows[:, 2 * D:2 * D + 64]
    bo_r = brows[:, 2 * D + 64:3 * D + 64]

    for i in range(NT):
        xt = xs[:, i * D:(i + 1) * D]
        emit_ln(ctx, xt, h[:], hf[:], mcol)
        ctx.wait("t", "v")
        emit_transpose_tile8(ctx, h[:], hT[:], ident, ptp)
        emit_proj(ctx, hT[:], wv[:], pmm, [512, 512], bias=bv_r, ones=ones)
        ctx.wait("v", "t")
        ctx.cp(vt[:], pmm[:])
        emit_proj(ctx, hT[:], w1[:], pmm2, [512, 512], bias=b1_r, ones=ones)
        ctx.wait("s", "t")
        ctx.act(t1[:], pmm2[:], AF.Relu)
        ctx.wait("t", "s")
        emit_transpose_tile8(ctx, t1[:], t1T[:], ident, ptp)
        emit_proj(ctx, t1T[:], w2[:], psm, [64], bias=b2_r, ones=ones)
        ctx.wait("v", "t")
        ctx.cp(t2[:], psm[:, 0:64])
        ctx.wait("t", "v")
        emit_transpose(ctx, t2[:], t2T[:], ident, ptp)
        ctx.wait("t", "v")
        lg_at = ctx.mm(psm[:, 0:16], t2T[:], sig[:], start=True, stop=True, inc=True)
        ctx.wait("v", "t", lg_at)
        ctx.red(mcol[:, 4:5], psm[:, 0:16], OP.max)
        ctx.ts(mcol[:, 4:5], mcol[:, 4:5], -1.0, OP.mult)
        ctx.wait("s", "v")
        ctx.act(gate[:], psm[:, 0:16], AF.Exp, bias=mcol[:, 4:5], accum=mcol[:, 5:6])
        ctx.wait("v", "s")
        ctx.run("v", lambda E: E.reciprocal(out=mcol[:, 6:7], in_=mcol[:, 5:6]), inc="v")
        ctx.ts(gate[:], gate[:], mcol[:, 6:7], OP.mult)
        for hh in range(16):
            ctx.ts(vg[:, 64 * hh:64 * (hh + 1)], vt[:, 64 * hh:64 * (hh + 1)],
                   gate[:, hh:hh + 1], OP.mult)
        if dbg and i == 0:
            ctx.wait("g", "v")
            ctx.wait("g", "s")
            ctx.cp(hf[:], h[:])
            ctx.wait("g", "v")
            ctx.dma("g", d_h, hf[:], inc="gd")
            ctx.dma("g", d_v, vt[:], inc="gd")
            ctx.cp(hf[:, 0:64], t2[:])
            ctx.wait("g", "v")
            ctx.dma("g", d_t2, hf[:, 0:64], inc="gd")
            ctx.dma("g", d_gate, gate[:], inc="gd")
            ctx.cp(hf[:], vg[:])
            ctx.wait("g", "v")
            ctx.dma("g", d_vg, hf[:], inc="gd")
            ctx.wait("v", "gd")
        ctx.wait("t", "v")
        emit_transpose_tile8(ctx, vg[:], vgT[:], ident, ptp)
        emit_proj(ctx, vgT[:], wo[:], pmm, [512, 512], bias=bo_r, ones=ones)
        ctx.wait("v", "t")
        ctx.tt(xt, xt, pmm[:], OP.add)
        ctx.wait("g", "v")
        ctx.dma("g", y_t[i, :, :], xt, inc="od")




def build_L3a(dbg=False):
    nc = bass.Bass("TRN2", target_bir_lowering=False, debug=False)
    x_in = nc.dram_tensor("x", [TOK, D], F32, kind="ExternalInput").ap()
    y_out = nc.dram_tensor("y", [TOK, D], F32, kind="ExternalOutput").ap()
    ctx = Ctx(nc)
    with ExitStack() as stack:
        ctx.alloc_sems(stack)
        ctx.ident, ctx.ones = emit_prelude(ctx, stack)
        ctx.scope = "a3_"
        emit_L3a(ctx, stack, x_in, y_out, dbg)
        ctx.wait("q", "od", ctx.cnt["od"])
        ctx.flush()
    return nc

def run_L3a(x_full, INP):
    nc = build_L3a()
    xs = _shard_x(x_full)
    T = lambda a: np.ascontiguousarray(np.asarray(a, np.float32).T)
    R = lambda a, s: np.ascontiguousarray(np.asarray(a, np.float32).reshape(s))
    common = dict(
        wvT=T(INP["L3_attn_Wv"]), w1T=T(INP["L3_attn_W1"]),
        w2T=T(INP["L3_attn_W2"]), woT=T(INP["L3_attn_Wo"]),
        sigT=T(INP["L3_attn_head_signatures"]),
        bv=R(INP["L3_attn_bv"], (1, D)), b1=R(INP["L3_attn_b1"], (1, D)),
        b2=R(INP["L3_attn_b2"], (1, 64)), bo=R(INP["L3_attn_bo"], (1, D)),
    )
    maps = [dict(x=xs[r], **common) for r in range(NCORES)]
    res = run_bass_kernel_spmd(nc, _prefix_maps(maps, "a3_"), core_ids=list(range(NCORES)))
    return np.concatenate([res.results[r]["y"] for r in range(NCORES)], 0)



DBGSER = False

def emit_L1f(ctx, stack, x_in, y_out, dbg=False):
    nc = ctx.nc
    E = stack.enter_context
    ident, ones = ctx.ident, ctx.ones
    rmsw_in = ctx.dr("rmsw", [1, D], F32, kind="ExternalInput").ap()
    gw_in = ctx.dr("gwT", [D, 8], F32, kind="ExternalInput").ap()
    w1_in = ctx.dr("w1", [8, D, 4096], F32, kind="ExternalInput").ap()
    b1_in = ctx.dr("b1", [8, 4096], F32, kind="ExternalInput").ap()
    w2_in = ctx.dr("w2", [8, 4096, D], F32, kind="ExternalInput").ap()
    b2_in = ctx.dr("b2", [8, D], F32, kind="ExternalInput").ap()
    if dbg:
        d_S = ctx.dr("d_S", [128, NT * 8], F32, kind="ExternalOutput").ap()
        d_xn = ctx.dr("d_xn", [128, D], F32, kind="ExternalOutput").ap()

    xs = E(ctx.sb("xs", [128, NT * D], F32))
    xnT = E(ctx.sb("xnT", [128, NT * D], BF16))
    S = E(ctx.sb("S", [128, NT * 8], F32))
    gw = E(ctx.sb("gw", [128, 8 * 8], BF16))
    gwf = E(ctx.sb("gwf", [128, 8 * 8], F32))
    gwlo = E(ctx.sb("gwlo", [128, 8 * 8], BF16))
    xnlo = E(ctx.sb("xnlo", [128, D], BF16))
    xnloT = E(ctx.sb("xnloT", [128, D], BF16))
    rms_bc = E(ctx.sb("rms_bc", [128, D], F32))
    rmsrow = E(ctx.sb("rmsrow", [1, D], BF16))
    ones8 = E(ctx.sb("ones8", [128, 8], F32))
    w1sb = [E(nc.sbuf_tensor(f"w1sb{b}", [128, 8 * D], BF16)) for b in range(2)]
    w2sb = [E(nc.sbuf_tensor(f"w2sb{b}", [128, 8 * D], BF16)) for b in range(2)]
    b1row = [E(nc.sbuf_tensor(f"b1row{b}", [1, D], BF16)) for b in range(2)]
    b2row = [E(nc.sbuf_tensor(f"b2row{b}", [1, D], BF16)) for b in range(2)]
    hf = E(ctx.sb("hf", [128, D], F32))
    xnbf = E(ctx.sb("xnbf", [128, D], BF16))
    hbf = E(ctx.sb("hbf", [128, D], BF16))
    hT = E(ctx.sb("hT", [128, D], BF16))
    sel = E(ctx.sb("sel", [128, 8], F32))
    csum = E(ctx.sb("csum", [128, 8], F32))
    mcol = E(ctx.sb("mcol", [128, 8], F32))
    ptp = E(ctx.ps("ptp", [128, 128], BF16))
    ph = E(ctx.ps("ph", [128, 1024], F32))
    py = E(ctx.ps("py", [128, 1024], F32))
    plog = E(ctx.ps("plog", [128, 16], F32))

    x_t = x_in.rearrange("(n p) d -> n p d", p=128)
    y_t = y_out.rearrange("(n p) d -> n p d", p=128)
    for i in range(NT):
        ctx.dma("q", xs[:, i * D:(i + 1) * D], x_t[i])
    gw_t = gw_in.rearrange("(c p) e -> c p e", p=128)
    for c in range(8):
        ctx.dma("g", gwf[:, c * 8:(c + 1) * 8], gw_t[c])
    ctx.wait("v", "gd")
    ctx.cp(gw[:], gwf[:])
    ctx.tt(gwlo[:], gwf[:], gw[:], OP.subtract)
    ctx.dma("g", rmsrow[:], rmsw_in)
    ctx.run("g", lambda E_: E_.memset(ones8[:], 1.0), inc="g")
    ld_q, ld_g = ctx.cnt["qd"], ctx.cnt["gd"]
    ctx.wait("t", "gd", ld_g)
    for f in range(2):
        bc_at = ctx.mm(ph[:, 512 * f:512 * (f + 1)], ones[:],
                       rmsrow[:, 512 * f:512 * (f + 1)], start=True, stop=True,
                       inc=True)
    ctx.wait("v", "t")
    ctx.cp(rms_bc[:], ph[:])
    ctx.wait("v", "qd", ld_q)

    for i in range(NT):
        xt = xs[:, i * D:(i + 1) * D]
        emit_ln(ctx, xt, xnbf[:], hf[:], mcol)
        ctx.ts(hf[:], hf[:], mcol[:, 3:4], OP.mult)
        ctx.wait("s", "v")
        ctx.act(xnbf[:], hf[:], AF.Square, accum=mcol[:, 4:5])
        ctx.wait("v", "s")
        ctx.ts(mcol[:, 4:5], mcol[:, 4:5], 1.0 / D, OP.mult, 1e-6, OP.add,
               eng="v")
        ctx.wait("s", "v")
        ctx.act(mcol[:, 5:6], mcol[:, 4:5], AF.Sqrt)
        ctx.wait("v", "s")
        ctx.run("v", lambda E_: E_.reciprocal(out=mcol[:, 6:7], in_=mcol[:, 5:6]),
                inc="v")
        ctx.ts(hf[:], hf[:], mcol[:, 6:7], OP.mult)
        ctx.tt(hf[:], hf[:], rms_bc[:], OP.mult)
        ctx.cp(xnbf[:], hf[:])
        ctx.tt(xnlo[:], hf[:], xnbf[:], OP.subtract)
        ctx.wait("t", "v")
        emit_transpose_tile8(ctx, xnbf[:], xnT[:, i * D:(i + 1) * D], ident, ptp)
        emit_transpose_tile8(ctx, xnlo[:], xnloT[:], ident, ptp)
        ctx.wait("t", "v")
        for c in range(8):
            ctx.mm(plog[:, 0:8], xnT[:, i * D + 128 * c:i * D + 128 * (c + 1)],
                   gw[:, c * 8:(c + 1) * 8], start=(c == 0), stop=False)
        for c in range(8):
            ctx.mm(plog[:, 0:8], xnloT[:, 128 * c:128 * (c + 1)],
                   gw[:, c * 8:(c + 1) * 8], start=False, stop=False)
        for c in range(8):
            ctx.mm(plog[:, 0:8], xnT[:, i * D + 128 * c:i * D + 128 * (c + 1)],
                   gwlo[:, c * 8:(c + 1) * 8], start=False, stop=(c == 7),
                   inc=(c == 7))
        ctx.wait("v", "t")
        ctx.red(mcol[:, 0:1], plog[:, 0:8], OP.max)
        ctx.ts(sel[:], plog[:, 0:8], mcol[:, 0:1], OP.is_equal)
        ctx.ts(mcol[:, 1:2], mcol[:, 0:1], -1.0, OP.mult)
        ctx.wait("s", "v")
        ctx.act(csum[:], plog[:, 0:8], AF.Exp, bias=mcol[:, 1:2],
                accum=mcol[:, 2:3])
        ctx.wait("v", "s")
        ctx.run("v", lambda E_: E_.reciprocal(out=mcol[:, 3:4], in_=mcol[:, 2:3]),
                inc="v")
        ctx.run("v", lambda E_: E_.tensor_tensor_scan(
            out=csum[:], data0=ones8[:], data1=sel[:], initial=0.0,
            op0=OP.mult, op1=OP.add), inc="v")
        ctx.ts(csum[:], csum[:], 1.0, OP.is_le)
        ctx.tt(sel[:], sel[:], csum[:], OP.mult)
        ctx.ts(S[:, i * 8:(i + 1) * 8], sel[:], mcol[:, 3:4], OP.mult)

    if dbg:
        ctx.wait("g", "v")
        ctx.dma("g", d_S, S[:], inc="gd")
        ctx.cp(hf[:], xnT[:, 0:D])
        ctx.wait("g", "v")
        ctx.dma("g", d_xn, hf[:], inc="gd")
        ctx.wait("v", "gd")

    sets = [(e, fc) for e in range(8) for fc in range(4)]
    t_done = {}

    def load_set(k):
        e, fc = sets[k]
        b = k % 2
        if DBGSER:
            ctx.wait("g", "t")
        elif k >= 2:
            ctx.wait("g", "t", t_done[k - 2])
        for c in range(8):
            ctx.dma("g", w1sb[b][:, c * D:(c + 1) * D],
                    w1_in[e, c * 128:(c + 1) * 128, fc * D:(fc + 1) * D])
            ctx.dma("g", w2sb[b][:, c * D:(c + 1) * D],
                    w2_in[e, fc * 8 * 128 + c * 128: fc * 8 * 128 + (c + 1) * 128, :])
        ctx.dma("g", b1row[b][:], b1_in[e:e + 1, fc * D:(fc + 1) * D])
        ctx.dma("g", b2row[b][:], b2_in[e:e + 1, :])
        return ctx.cnt["gd"]

    if DBGSER:
        ld_at = {}
    else:
        ld_at = {0: load_set(0), 1: load_set(1)}
    for k, (e, fc) in enumerate(sets):
        b = k % 2
        if DBGSER:
            ld_at[k] = load_set(k)
        ctx.wait("t", "gd", ld_at[k])
        for i in range(NT):
            scol = S[:, i * 8 + e: i * 8 + e + 1]
            emit_proj(ctx, xnT[:, i * D:(i + 1) * D], w1sb[b], ph, [512, 512],
                      bias=b1row[b], ones=ones)
            ctx.wait("s", "t")
            ctx.act(hbf[:], ph[:], AF.Silu)
            ctx.wait("t", "s")
            emit_transpose_tile8(ctx, hbf[:], hT[:], ident, ptp)
            emit_proj(ctx, hT[:], w2sb[b], py, [512, 512],
                      bias=(b2row[b] if fc == 0 else None), ones=ones)
            t_at = ctx.cnt["t"]
            ctx.wait("v", "t", t_at)
            ctx.stt(xs[:, i * D:(i + 1) * D], py[:], scol,
                    xs[:, i * D:(i + 1) * D], OP.mult, OP.add)
        t_done[k] = ctx.cnt["t"]
        if not DBGSER and k + 2 < len(sets):
            ld_at[k + 2] = load_set(k + 2)

    for i in range(NT):
        ctx.wait("g", "v")
        ctx.dma("g", y_t[i, :, :], xs[:, i * D:(i + 1) * D], inc="od")



def build_L1f(dbg=False):
    nc = bass.Bass("TRN2", target_bir_lowering=False, debug=False)
    x_in = nc.dram_tensor("x", [TOK, D], F32, kind="ExternalInput").ap()
    y_out = nc.dram_tensor("y", [TOK, D], F32, kind="ExternalOutput").ap()
    ctx = Ctx(nc)
    with ExitStack() as stack:
        ctx.alloc_sems(stack)
        ctx.ident, ctx.ones = emit_prelude(ctx, stack)
        ctx.scope = "f1_"
        emit_L1f(ctx, stack, x_in, y_out, dbg)
        ctx.wait("q", "od", ctx.cnt["od"])
        ctx.flush()
    return nc

def run_L1f(x_full, INP):
    nc = build_L1f()
    xs = _shard_x(x_full)
    T = lambda a: np.ascontiguousarray(np.asarray(a, np.float32).T)
    common = dict(
        rmsw=np.ascontiguousarray(INP["L1_ffn_rms_w"].reshape(1, D), np.float32),
        gwT=T(INP["L1_ffn_gate_w"]),
        w1=np.ascontiguousarray(INP["L1_ffn_W1"], np.float32),
        b1=np.ascontiguousarray(INP["L1_ffn_b1"], np.float32),
        w2=np.ascontiguousarray(INP["L1_ffn_W2"], np.float32),
        b2=np.ascontiguousarray(INP["L1_ffn_b2"], np.float32),
    )
    maps = [dict(x=xs[r], **common) for r in range(NCORES)]
    res = run_bass_kernel_spmd(nc, _prefix_maps(maps, "f1_"), core_ids=list(range(NCORES)))
    return np.concatenate([res.results[r]["y"] for r in range(NCORES)], 0)



def emit_L2f(ctx, stack, x_in, y_out):
    nc = ctx.nc
    E = stack.enter_context
    ident, ones = ctx.ident, ctx.ones
    w1_in = ctx.dr("w1T", [D, 4096], F32, kind="ExternalInput").ap()
    w2_in = ctx.dr("w2T", [D, 4096], F32, kind="ExternalInput").ap()
    h1_in = ctx.dr("h1T", [4096, 4096], F32, kind="ExternalInput").ap()
    h2_in = ctx.dr("h2T", [4096, 4096], F32, kind="ExternalInput").ap()
    w3_in = ctx.dr("w3T", [4096, D], F32, kind="ExternalInput").ap()
    s_in = ctx.dr("scl", [128, 32 * 4 + 8], F32, kind="ExternalInput").ap()

    HTOK = 1024
    scl = E(ctx.sb("scl_sb", [128, 32 * 4 + 8], F32))
    xT = E(ctx.sb("xT", [128, 8 * HTOK], BF16))
    hsb = E(ctx.sb("hsb", [128, 32 * HTOK], BF16))
    hnsb = E(ctx.sb("hnsb", [128, 32 * HTOK], BF16))
    wba = [E(nc.sbuf_tensor(f"wba{b}", [128, 8 * 128], BF16)) for b in range(2)]
    wbb = [E(nc.sbuf_tensor(f"wbb{b}", [128, 8 * 128], BF16)) for b in range(2)]
    wbig = [E(nc.sbuf_tensor(f"wbig{b}", [128, 32 * 128], BF16)) for b in range(2)]
    xtile = E(ctx.sb("xtile", [128, D], F32))
    xn = E(ctx.sb("xn", [128, D], BF16))
    tf = E(ctx.sb("tf", [128, 512], F32))
    ofb = E(ctx.sb("ofb", [128, 512], BF16))
    mcol = E(ctx.sb("mcol", [128, 8], F32))
    ptp = E(ctx.ps("ptp", [128, 128], BF16))
    pz1 = E(ctx.ps("pz1", [128, 512], F32))
    pz2 = E(ctx.ps("pz2", [128, 512], F32))

    s1c = lambda oc: scl[:, oc:oc + 1]
    s2c = lambda oc: scl[:, 32 + oc:32 + oc + 1]
    h1c = lambda oc: scl[:, 64 + oc:64 + oc + 1]
    h2c = lambda oc: scl[:, 96 + oc:96 + oc + 1]
    w3c = lambda oc: scl[:, 128 + oc:128 + oc + 1]
    ctx.dma("q", scl[:], s_in)

    x_t = x_in.rearrange("(n p) d -> n p d", p=128)
    y_t = y_out.rearrange("(n p) d -> n p d", p=128)
    xs2 = hnsb.ap().bitcast(F32)

    for half in range(2):
        t0 = half * 8
        for i in range(8):
            ctx.wait("q", "v")
            ctx.dma("q", xtile[:], x_t[t0 + i])
            ctx.wait("v", "qd")
            ctx.wait("s", "qd")
            emit_ln(ctx, xtile[:], xn[:], xtile[:], mcol)
            ctx.wait("t", "v")
            for c in range(8):
                emit_transpose(ctx, xn[:, 128 * c:128 * (c + 1)],
                               xT[:, c * HTOK + i * 128:c * HTOK + (i + 1) * 128],
                               ident, ptp)

        t_doneA = {}
        ld_atA = {}

        def loadA(k):
            if k - 2 in t_doneA:
                ctx.wait("g", "t", t_doneA[k - 2])
            else:
                ctx.wait("g", "t")
            wt1 = w1_in.rearrange("(c p) o -> c p o", p=128)
            wt2 = w2_in.rearrange("(c p) o -> c p o", p=128)
            b = k % 2
            for c in range(8):
                ctx.dma("g", wba[b][:, 128 * c:128 * (c + 1)],
                        wt1[c, :, 128 * k:128 * (k + 1)])
                ctx.dma("g", wbb[b][:, 128 * c:128 * (c + 1)],
                        wt2[c, :, 128 * k:128 * (k + 1)])
            return ctx.cnt["gd"]

        ld_atA[0] = loadA(0)
        ld_atA[1] = loadA(1)
        for oc in range(32):
            b = oc % 2
            ctx.wait("t", "gd", ld_atA[oc])
            for g in range(2):
                ctx.wait("t", "s")
                ctx.wait("t", "v")
                for c in range(8):
                    ctx.mm(pz1[:], wba[b][:, 128 * c:128 * (c + 1)],
                           xT[:, c * HTOK + g * 512:c * HTOK + (g + 1) * 512],
                           start=(c == 0), stop=(c == 7))
                    ctx.mm(pz2[:], wbb[b][:, 128 * c:128 * (c + 1)],
                           xT[:, c * HTOK + g * 512:c * HTOK + (g + 1) * 512],
                           start=(c == 0), stop=(c == 7), inc=(c == 7))
                ctx.wait("s", "t")
                ctx.act(tf[:], pz1[:], AF.Silu, scale=s1c(oc))
                ctx.wait("v", "s")
                ctx.stt(hsb[:, oc * HTOK + g * 512:oc * HTOK + (g + 1) * 512],
                        pz2[:], s2c(oc), tf[:], OP.mult, OP.mult)
            t_doneA[oc] = ctx.cnt["t"]
            if oc + 2 < 32:
                ld_atA[oc + 2] = loadA(oc + 2)

        for (w_in_ap, sc, src_sb, dst_sb) in [(h1_in, h1c, hsb, hnsb),
                                              (h2_in, h2c, hnsb, hsb)]:
            t_doneB = {}
            ld_atB = {}

            def loadB(k, w_in_ap=w_in_ap, t_doneB=t_doneB):
                if k - 2 in t_doneB:
                    ctx.wait("g", "t", t_doneB[k - 2])
                else:
                    ctx.wait("g", "t")
                wt = w_in_ap.rearrange("(c p) o -> c p o", p=128)
                b = k % 2
                for c4 in range(4):
                    ctx.dma("g", wbig[b][:, 128 * 8 * c4:128 * 8 * (c4 + 1)].rearrange(
                        "p (c o) -> p c o", o=128),
                        wt[8 * c4:8 * (c4 + 1), :, 128 * k:128 * (k + 1)].rearrange(
                        "c p o -> p c o"))
                return ctx.cnt["gd"]

            ld_atB[0] = loadB(0)
            ld_atB[1] = loadB(1)
            for oc in range(32):
                b = oc % 2
                ctx.wait("t", "gd", ld_atB[oc])
                for g in range(2):
                    ctx.wait("t", "s")
                    for c in range(32):
                        ctx.mm(pz1[:], wbig[b][:, 128 * c:128 * (c + 1)],
                               src_sb[:, c * HTOK + g * 512:c * HTOK + (g + 1) * 512],
                               start=(c == 0), stop=(c == 31), inc=(c == 31))
                    ctx.wait("s", "t")
                    ctx.act(dst_sb[:, oc * HTOK + g * 512:oc * HTOK + (g + 1) * 512],
                            pz1[:], AF.Gelu, scale=sc(oc))
                t_doneB[oc] = ctx.cnt["t"]
                if oc + 2 < 32:
                    ld_atB[oc + 2] = loadB(oc + 2)

        ctx.wait("q", "t")
        ctx.wait("q", "s")
        for i in range(8):
            ctx.dma("q", xs2[:, i * D:(i + 1) * D], x_t[t0 + i])
        xld = ctx.cnt["qd"]
        ctx.wait("v", "qd", xld)
        t_doneD = {}
        ld_atD = {}

        def loadD(k):
            if k - 2 in t_doneD:
                ctx.wait("g", "t", t_doneD[k - 2])
            else:
                ctx.wait("g", "t")
            wt = w3_in.rearrange("(c p) o -> c p o", p=128)
            b = k % 2
            for c4 in range(4):
                ctx.dma("g", wbig[b][:, 128 * 8 * c4:128 * 8 * (c4 + 1)].rearrange(
                    "p (c o) -> p c o", o=128),
                    wt[8 * c4:8 * (c4 + 1), :, 128 * k:128 * (k + 1)].rearrange(
                    "c p o -> p c o"))
            return ctx.cnt["gd"]

        ld_atD[0] = loadD(0)
        ld_atD[1] = loadD(1)
        for oc in range(8):
            b = oc % 2
            ctx.wait("t", "gd", ld_atD[oc])
            for g in range(2):
                ctx.wait("t", "v")
                for c in range(32):
                    ctx.mm(pz1[:], wbig[b][:, 128 * c:128 * (c + 1)],
                           hsb[:, c * HTOK + g * 512:c * HTOK + (g + 1) * 512],
                           start=(c == 0), stop=(c == 31), inc=(c == 31))
                ctx.wait("v", "t")
                ctx.ts(ofb[:], pz1[:], w3c(oc), OP.mult)
                for c4 in range(4):
                    i = g * 4 + c4
                    ctx.wait("t", "v")
                    tat = ctx.transpose(ptp[:], ofb[:, 128 * c4:128 * (c4 + 1)],
                                        ident[:])
                    ctx.wait("v", "t", tat)
                    ctx.tt(xs2[:, i * D + oc * 128:i * D + (oc + 1) * 128],
                           xs2[:, i * D + oc * 128:i * D + (oc + 1) * 128],
                           ptp[:], OP.add)
            t_doneD[oc] = ctx.cnt["t"]
            if oc + 2 < 8:
                ld_atD[oc + 2] = loadD(oc + 2)
        for i in range(8):
            ctx.wait("g", "v")
            ctx.dma("g", y_t[t0 + i, :, :], xs2[:, i * D:(i + 1) * D], inc="od")
        ctx.wait("v", "od", ctx.cnt["od"])
        ctx.wait("s", "od", ctx.cnt["od"])
        ctx.wait("t", "od", ctx.cnt["od"])




def build_L2f():
    nc = bass.Bass("TRN2", target_bir_lowering=False, debug=False)
    x_in = nc.dram_tensor("x", [TOK, D], F32, kind="ExternalInput").ap()
    y_out = nc.dram_tensor("y", [TOK, D], F32, kind="ExternalOutput").ap()
    ctx = Ctx(nc)
    with ExitStack() as stack:
        ctx.alloc_sems(stack)
        ctx.ident, ctx.ones = emit_prelude(ctx, stack)
        ctx.scope = "f2_"
        emit_L2f(ctx, stack, x_in, y_out)
        ctx.wait("q", "od", ctx.cnt["od"])
        ctx.flush()
    return nc

def run_L2f(x_full, INP):
    nc = build_L2f()
    xs = _shard_x(x_full)
    T = lambda a: np.ascontiguousarray(np.asarray(a, np.float32).T)
    SC = lambda a, n: np.ascontiguousarray(
        np.asarray(a, np.float32).reshape(n, 128).T)
    scl = np.concatenate([SC(INP["L2_ffn_w1_s"], 32), SC(INP["L2_ffn_w2_s"], 32),
                          SC(INP["L2_ffn_h1_s"], 32), SC(INP["L2_ffn_h2_s"], 32),
                          SC(INP["L2_ffn_w3_s"], 8)], axis=1)
    common = dict(
        w1T=T(INP["L2_ffn_w1_w"]), w2T=T(INP["L2_ffn_w2_w"]),
        h1T=T(INP["L2_ffn_h1_w"]), h2T=T(INP["L2_ffn_h2_w"]),
        w3T=T(INP["L2_ffn_w3_w"]), scl=np.ascontiguousarray(scl),
    )
    maps = [dict(x=xs[r], **common) for r in range(NCORES)]
    res = run_bass_kernel_spmd(nc, _prefix_maps(maps, "f2_"), core_ids=list(range(NCORES)))
    return np.concatenate([res.results[r]["y"] for r in range(NCORES)], 0)



def emit_allgather(ctx, in_ap, out_ap):
    return ctx.run("g", lambda E: E.collective_compute(
        "AllGather", OP.bypass, replica_groups=[list(range(NCORES))],
        ins=[in_ap.opt()], outs=[out_ap.opt()]), inc="cc", n=1)


def emit_bcast_f32(ctx, row_bf, out_f32, ones, psum_f32):
    F = row_bf.shape[-1]
    ctx.wait("t", "v")
    ctx.wait("t", "s")
    for f in range(0, F, 512):
        w = min(512, F - f)
        at = ctx.mm(psum_f32[:, f:f + w], ones[:], row_bf[:, f:f + w],
                    start=True, stop=True, inc=True)
    ctx.wait("v", "t", at)
    ctx.cp(out_f32, psum_f32[:, 0:F])


SKIP_CC = False

def emit_L1a(ctx, stack, x_in, y_out):
    nc = ctx.nc
    E = stack.enter_context
    ident, ones = ctx.ident, ctx.ones
    wk_in = ctx.dr("wkT", [D, D], F32, kind="ExternalInput").ap()
    wv_in = ctx.dr("wvT", [D, D], F32, kind="ExternalInput").ap()
    wq_in = ctx.dr("wqT", [D, D], F32, kind="ExternalInput").ap()
    wg_in = ctx.dr("wgT", [D, D], F32, kind="ExternalInput").ap()
    wo_in = ctx.dr("woT", [D, D], F32, kind="ExternalInput").ap()
    w0_in = ctx.dr("w0", [D, D], F32, kind="ExternalInput").ap()
    a_in = ctx.dr("ainit", [D, 8], F32, kind="ExternalInput").ap()
    b_in = ctx.dr("binit", [8, D], F32, kind="ExternalInput").ap()
    lng_in = ctx.dr("lng", [1, D], F32, kind="ExternalInput").ap()
    lnb_in = ctx.dr("lnb", [1, D], F32, kind="ExternalInput").ap()
    loglr_in = ctx.dr("loglr", [1, D], F32, kind="ExternalInput").ap()
    lsg_in = ctx.dr("lsg", [1, D], F32, kind="ExternalInput").ap()
    strict_in = ctx.dr("strictLT", [128, 128], F32, kind="ExternalInput").ap()
    maskp_in = ctx.dr("maskprev", [8, 1], F32, kind="ExternalInput").ap()
    pos_in = ctx.dr("poscol", [128, NT], F32, kind="ExternalInput").ap()
    cs_pub = ctx.dr("cs_pub", [1, D], F32)
    cs_ag = ctx.dr("cs_ag", [NCORES, D], F32)
    xmid = ctx.dr("xmid", [TOK, D], F32)

    wk = E(ctx.sb("wk", [128, 8 * D], BF16))
    wv = E(ctx.sb("wv", [128, 8 * D], BF16))
    wq = E(ctx.sb("wq", [128, 8 * D], BF16))
    wg = E(ctx.sb("wg", [128, 8 * D], BF16))
    wo = E(ctx.sb("wo", [128, 8 * D], BF16))
    w0 = E(ctx.sb("w0_sb", [128, 8 * D], BF16))
    asb = E(ctx.sb("asb", [128, 8 * 8], BF16))
    bsb = E(ctx.sb("bsb", [8, D], BF16))
    strict = E(ctx.sb("strict", [128, 128], BF16))
    maskp = E(ctx.sb("maskp", [8, 1], BF16))
    poss = E(ctx.sb("poss", [128, NT], F32))
    rows = E(ctx.sb("rows", [1, D], F32))
    rowsbf = E(ctx.sb("rowsbf", [1, 2 * D], BF16))
    g_bc = E(ctx.sb("g_bc", [128, D], BF16))
    b_bc = E(ctx.sb("b_bc", [128, D], BF16))
    lr_bc = E(ctx.sb("lr_bc", [128, D], BF16))
    lsg_bc = E(ctx.sb("lsg_bc", [128, D], BF16))
    Qg = E(ctx.sb("Qg", [128, NT * D], BF16))
    xtile = E(ctx.sb("xtile", [128, D], F32))
    hf = E(ctx.sb("hf", [128, D], F32))
    scr2 = E(ctx.sb("scr2", [128, D], F32))
    hbf = E(ctx.sb("hbf", [128, D], BF16))
    hnbf = E(ctx.sb("hnbf", [128, D], BF16))
    hT = E(ctx.sb("hT", [128, D], BF16))
    hnT = E(ctx.sb("hnT", [128, D], BF16))
    kf = E(ctx.sb("kf", [128, D], F32))
    kbf = E(ctx.sb("kbf", [128, D], BF16))
    kT = E(ctx.sb("kT", [128, D], BF16))
    vf = E(ctx.sb("vf", [128, D], F32))
    qf = E(ctx.sb("qf", [128, D], F32))
    gatef = E(ctx.sb("gatef", [128, D], F32))
    gradbf = E(ctx.sb("gradbf", [128, D], BF16))
    y0bf = E(ctx.sb("y0bf", [128, D], BF16))
    t1 = E(ctx.sb("t1", [128, D], F32))
    ob = E(ctx.sb("ob", [128, D], BF16))
    obT = E(ctx.sb("obT", [128, D], BF16))
    kabf = E(ctx.sb("kabf", [128, 8], BF16))
    kaT = E(ctx.sb("kaT", [8, 128], BF16))
    onescol = E(ctx.sb("onescol", [128, 1], BF16))
    carry = E(ctx.sb("carry", [1, D], F32))
    ag8 = E(ctx.sb("ag8", [8, D], BF16))
    mcol = E(ctx.sb("mcol", [128, 8], F32))
    ptp = E(ctx.ps("ptp", [128, 128], BF16))
    pa = E(ctx.ps("pa", [128, 1024], F32))
    pb = E(ctx.ps("pb", [128, 1024], F32))
    pka = E(ctx.ps("pka", [128, 16], F32))
    pc1 = E(ctx.ps("pc1", [1, 1024], F32))

    x_t = x_in.rearrange("(n p) d -> n p d", p=128)
    y_t = y_out.rearrange("(n p) d -> n p d", p=128)
    xm_t = xmid.ap().rearrange("(n p) d -> n p d", p=128)
    for w_sb, w_ap in [(wk, wk_in), (wv, wv_in), (wq, wq_in), (wg, wg_in),
                       (wo, wo_in), (w0, w0_in)]:
        wt = w_ap.rearrange("(c p) d -> c p d", p=128)
        for c in range(8):
            ctx.dma("g", w_sb[:, c * D:(c + 1) * D], wt[c])
    at_ = a_in.rearrange("(c p) r -> c p r", p=128)
    for c in range(8):
        ctx.dma("g", asb[:, c * 8:(c + 1) * 8], at_[c])
    ctx.dma("g", bsb[:], b_in)
    ctx.dma("g", strict[:], strict_in)
    ctx.dma("g", maskp[:], maskp_in)
    ctx.dma("q", poss[:], pos_in)
    ctx.run("g", lambda E_: E_.memset(onescol[:], 1.0), inc="g")
    rb1 = rowsbf[:, 0:D]
    for (src_ap, dst_bc, is_lr) in [(lng_in, g_bc, False), (lnb_in, b_bc, False),
                                    (loglr_in, lr_bc, True), (lsg_in, lsg_bc, False)]:
        ctx.wait("q", "v")
        ctx.wait("q", "s")
        ctx.dma("q", rows[:], src_ap)
        ctx.wait("v", "qd")
        ctx.wait("s", "qd")
        if is_lr:
            ctx.act(rows[:], rows[:], AF.Exp)
            ctx.wait("v", "s")
            ctx.ts(rb1, rows[:], 1.0, OP.min)
        else:
            ctx.cp(rb1, rows[:])
        emit_bcast_f32(ctx, rb1, dst_bc[:], ones, pa)

    ctx.wait("t", "gd")
    ctx.run("v", lambda E_: E_.memset(rowsbf[:, D:2 * D], 0.0), inc="v")
    for i in range(NT):
        ctx.wait("q", "v")
        ctx.wait("q", "gd")
        ctx.dma("q", xtile[:], x_t[i])
        ctx.wait("v", "qd")
        ctx.wait("s", "qd")
        emit_ln(ctx, xtile[:], hbf[:], hf[:], mcol)
        ctx.ts(hf[:], hf[:], mcol[:, 3:4], OP.mult)
        ctx.cp(hbf[:], hf[:])
        emit_ln(ctx, hf[:], hnbf[:], scr2[:], mcol[:, 4:8],
                gain_bc=g_bc[:], bias_bc=b_bc[:])
        ctx.wait("t", "v")
        emit_transpose_tile8(ctx, hnbf[:], hnT[:], ident, ptp)
        emit_transpose_tile8(ctx, hbf[:], hT[:], ident, ptp)
        emit_proj(ctx, hnT[:], wk[:], pa, [512, 512])
        ctx.wait("s", "t")
        ctx.cp(kf[:], pa[:], eng="s")
        ctx.wait("v", "s")
        ctx.cp(kbf[:], kf[:])
        ctx.wait("t", "v")
        emit_proj(ctx, hnT[:], wv[:], pb, [512, 512])
        ctx.wait("s", "t")
        ctx.cp(vf[:], pb[:], eng="s")
        ctx.wait("t", "v")
        emit_transpose_tile8(ctx, kbf[:], kT[:], ident, ptp)
        ctx.wait("t", "v")
        emit_proj(ctx, kT[:], asb[:], pka, [8])
        ctx.wait("v", "t")
        ctx.cp(kabf[:], pka[:, 0:8])
        emit_transpose(ctx, kabf[:], kaT[:], ident, ptp)
        ctx.wait("t", "s")
        emit_proj(ctx, kT[:], w0[:], pb, [512, 512], close=False)
        ctx.wait("t", "v")
        for f in range(2):
            ctx.mm(pb[:, f * 512:(f + 1) * 512], kaT[:],
                   bsb[:, f * 512:(f + 1) * 512], start=False, stop=True,
                   inc=True)
        y0_at = ctx.cnt["t"]
        ctx.wait("v", "t", y0_at)
        ctx.tt(vf[:], pb[:], vf[:], OP.subtract)
        ctx.tt(t1[:], kf[:], vf[:], OP.mult)
        ctx.cp(gradbf[:], t1[:])
        ctx.cp(y0bf[:], pb[:])
        ctx.wait("t", "v")
        emit_proj(ctx, hnT[:], wq[:], pa, [512, 512])
        ctx.wait("s", "t")
        ctx.cp(qf[:], pa[:], eng="s")
        ctx.wait("t", "s")
        ctx.wait("t", "v")
        emit_proj(ctx, hT[:], wg[:], pb, [512, 512])
        ctx.wait("s", "t")
        ctx.act(gatef[:], pb[:], AF.Sigmoid)
        ctx.wait("t", "s")
        ctx.wait("t", "v")
        for f in range(2):
            ctx.mm(pa[:, f * 512:(f + 1) * 512], strict[:],
                   gradbf[:, f * 512:(f + 1) * 512], start=True, stop=False)
            if i > 0:
                ctx.mm(pa[:, f * 512:(f + 1) * 512], ones[:],
                       rowsbf[:, D + f * 512:D + (f + 1) * 512],
                       start=False, stop=True, inc=True)
            else:
                ctx.mm(pa[:, f * 512:(f + 1) * 512], ones[:],
                       rowsbf[:, D + f * 512:D + (f + 1) * 512],
                       start=False, stop=True, inc=True)
        sh_at = ctx.cnt["t"]
        ctx.wait("v", "t", sh_at)
        ctx.wait("v", "s")
        ctx.tt(t1[:], qf[:], lr_bc[:], OP.mult)
        ctx.ts(t1[:], t1[:], poss[:, i:i + 1], OP.mult)
        ctx.tt(Qg[:, i * D:(i + 1) * D], t1[:], gatef[:], OP.mult)
        ctx.tt(t1[:], t1[:], pa[:], OP.mult)
        ctx.tt(t1[:], y0bf[:], t1[:], OP.subtract)
        ctx.tt(ob[:], t1[:], gatef[:], OP.mult)
        ctx.wait("t", "v")
        for f in range(2):
            ctx.mm(pc1[0:1, f * 512:(f + 1) * 512], onescol[:],
                   gradbf[:, f * 512:(f + 1) * 512], start=True, stop=True,
                   inc=(f == 1))
        cs_at = ctx.cnt["t"]
        ctx.wait("v", "t", cs_at)
        if i == 0:
            ctx.cp(carry[:], pc1[0:1, :])
        else:
            ctx.tt(carry[:], carry[:], pc1[0:1, :], OP.add)
        ctx.cp(rowsbf[0:1, D:2 * D], carry[:])
        ctx.wait("t", "v")
        emit_transpose_tile8(ctx, ob[:], obT[:], ident, ptp)
        emit_proj(ctx, obT[:], wo[:], pb, [512, 512])
        ctx.wait("v", "t")
        ctx.tt(t1[:], pb[:], lsg_bc[:], OP.mult)
        ctx.tt(xtile[:], xtile[:], hf[:], OP.add)
        ctx.tt(xtile[:], xtile[:], t1[:], OP.add)
        ctx.wait("g", "v")
        ctx.dma("g", xm_t[i, :, :], xtile[:], inc="gd")

    ctx.wait("g", "v")
    ctx.dma("g", cs_pub.ap(), carry[:], inc="gd")
    ctx.wait("g", "gd")
    if not SKIP_CC:
        cc_at = emit_allgather(ctx, cs_pub.ap(), cs_ag.ap())
        ctx.wait("g", "cc", cc_at)
        ctx.dma("g", ag8[:], cs_ag.ap(), inc="gd")
    else:
        ctx.run("g", lambda E_: E_.memset(ag8[:], 0.0), inc="g")
    agd = ctx.cnt["gd"]
    ctx.wait("t", "gd", agd)
    for f in range(2):
        ctx.mm(pc1[0:1, f * 512:(f + 1) * 512], maskp[:],
               ag8[:, f * 512:(f + 1) * 512], start=True, stop=True,
               inc=(f == 1))
    ci_at = ctx.cnt["t"]
    ctx.wait("v", "t", ci_at)
    ctx.cp(rowsbf[0:1, D:2 * D], pc1[0:1, :])
    ctx.wait("t", "v")
    for f in range(2):
        cb_at = ctx.mm(pb[:, f * 512:(f + 1) * 512], ones[:],
                       rowsbf[:, D + f * 512:D + (f + 1) * 512],
                       start=True, stop=True, inc=True)

    ctx.wait("q", "gd")
    for i in range(NT):
        ctx.wait("q", "v")
        ctx.wait("q", "od")
        ctx.dma("q", xtile[:], xm_t[i, :, :])
        ctx.wait("v", "qd")
        ctx.wait("v", "t", cb_at)
        ctx.tt(ob[:], Qg[:, i * D:(i + 1) * D], pb[:], OP.mult)
        ctx.wait("t", "v")
        emit_transpose_tile8(ctx, ob[:], obT[:], ident, ptp)
        emit_proj(ctx, obT[:], wo[:], pa, [512, 512])
        ctx.wait("v", "t")
        ctx.tt(t1[:], pa[:], lsg_bc[:], OP.mult)
        ctx.tt(xtile[:], xtile[:], t1[:], OP.subtract)
        ctx.wait("g", "v")
        ctx.dma("g", y_t[i, :, :], xtile[:], inc="od")




def build_L1a():
    nc = bass.Bass("TRN2", target_bir_lowering=False, debug=False)
    x_in = nc.dram_tensor("x", [TOK, D], F32, kind="ExternalInput").ap()
    y_out = nc.dram_tensor("y", [TOK, D], F32, kind="ExternalOutput").ap()
    ctx = Ctx(nc)
    with ExitStack() as stack:
        ctx.alloc_sems(stack)
        ctx.ident, ctx.ones = emit_prelude(ctx, stack)
        ctx.scope = "a1_"
        emit_L1a(ctx, stack, x_in, y_out)
        ctx.wait("q", "od", ctx.cnt["od"])
        ctx.flush()
    return nc

def run_L1a(x_full, INP):
    nc = build_L1a()
    xs = _shard_x(x_full)
    T = lambda a: np.ascontiguousarray(np.asarray(a, np.float32).T)
    R1 = lambda a: np.ascontiguousarray(np.asarray(a, np.float32).reshape(1, -1))
    common = dict(
        wkT=T(INP["L1_attn_Wk"]), wvT=T(INP["L1_attn_Wv"]), wqT=T(INP["L1_attn_Wq"]),
        wgT=T(INP["L1_attn_Wg"]), woT=T(INP["L1_attn_Wo"]),
        w0=np.ascontiguousarray(INP["L1_attn_W0"], np.float32),
        ainit=np.ascontiguousarray(INP["L1_attn_A_init"], np.float32),
        binit=np.ascontiguousarray(INP["L1_attn_B_init"], np.float32),
        lng=R1(INP["L1_attn_ln_g"]), lnb=R1(INP["L1_attn_ln_b"]),
        loglr=R1(INP["L1_attn_log_lr"]), lsg=R1(INP["L1_attn_ls_g"]),
        strictLT=np.ascontiguousarray(np.triu(np.ones((128, 128)), 1), np.float32),
    )
    maps = []
    for r in range(NCORES):
        pos = (np.arange(TOK) + r * TOK + 1).astype(np.float64)
        ps = (1.0 / (1.0 + 0.1 * np.log(pos))).astype(np.float32)
        poscol = np.ascontiguousarray(ps.reshape(NT, 128).T)
        maskp = np.zeros((8, 1), np.float32)
        maskp[:r] = 1.0
        maps.append(dict(x=xs[r], poscol=poscol, maskprev=maskp, **common))
    res = run_bass_kernel_spmd(nc, _prefix_maps(maps, "a1_"), core_ids=list(range(NCORES)))
    return np.concatenate([res.results[r]["y"] for r in range(NCORES)], 0)



def emit_L0a(ctx, stack, x_in, y_out, dbg=False):
    nc = ctx.nc
    E = stack.enter_context
    ident, ones = ctx.ident, ctx.ones
    if dbg:
        d_ag = ctx.dr("d_ag", [NCORES * 128, 2048], F32, kind="ExternalOutput").ap()
        d_M = ctx.dr("d_M", [128, 2048], F32, kind="ExternalOutput").ap()
        d_sco = ctx.dr("d_sco", [128, 2048], F32, kind="ExternalOutput").ap()
    wq_in = ctx.dr("qkvuT", [D, 4 * D], F32, kind="ExternalInput").ap()
    qb_in = ctx.dr("qkvub", [1, 4 * D], F32, kind="ExternalInput").ap()
    wo_in = ctx.dr("outwT", [D, D], F32, kind="ExternalInput").ap()
    ob_in = ctx.dr("outb", [1, D], F32, kind="ExternalInput").ap()
    sl_in = ctx.dr("slope", [1, 8], F32, kind="ExternalInput").ap()
    th_in = ctx.dr("theta", [1, D], F32, kind="ExternalInput").ap()
    tk_in = ctx.dr("tokcol", [128, NT], F32, kind="ExternalInput").ap()
    trl_in = ctx.dr("trilT", [128, 128], F32, kind="ExternalInput").ap()
    wp_in = ctx.dr("wprev", [128, 8], F32, kind="ExternalInput").ap()
    s_pubh = [ctx.dr(f"s_pub{h}", [128, 256], F32) for h in range(8)]
    s_agh = [ctx.dr(f"s_ag{h}", [NCORES * 128, 256], F32) for h in range(8)]

    TWOPI = float(2 * np.pi)
    wq = E(ctx.sb("wq", [128, 8 * 4 * D], BF16))
    wo = E(ctx.sb("wo", [128, 8 * D], BF16))
    brow = E(ctx.sb("brow", [1, 4 * D], BF16))
    obrow = E(ctx.sb("obrow", [1, D], BF16))
    slr = E(ctx.sb("slr", [1, 8], F32))
    thr = E(ctx.sb("thr", [1, D], BF16))
    th_bc = E(ctx.sb("th_bc", [128, D], F32))
    es1_bc = E(ctx.sb("es1_bc", [128, 8], F32))
    tokc = E(ctx.sb("tokc", [128, NT], F32))
    tril = E(ctx.sb("tril", [128, 128], BF16))
    wprev = E(ctx.sb("wprev_sb", [128, 8], F32))
    hstore = E(ctx.sb("hstore", [128, NT * D], BF16))
    Sco = E(ctx.sb("Sco", [128, 2 * D], F32))
    Spv = E(ctx.sb("Spv", [128, 2 * D], F32))
    Sto = Sco
    Mbf = E(ctx.sb("Mbf", [128, 2 * D], BF16))
    blk = E(ctx.sb("blk", [128, 2 * D], BF16))
    xtile = E(ctx.sb("xtile", [128, D], F32))
    hf = E(ctx.sb("hf", [128, D], F32))
    hT = E(ctx.sb("hT", [128, D], BF16))
    ksil = E(ctx.sb("ksil", [128, D], F32))
    qsil = E(ctx.sb("qsil", [128, D], F32))
    uf = E(ctx.sb("uf", [128, D], F32))
    vbf = E(ctx.sb("vbf", [128, D], BF16))
    sint = E(ctx.sb("sint", [128, D], I32))
    tww = E(ctx.sb("tww", [128, D], F32))
    rr = E(ctx.sb("rr", [128, D], F32))
    sin_t, cos_t = rr, tww
    kc = E(ctx.sb("kc", [128, D], BF16))
    ks = E(ctx.sb("ks", [128, D], BF16))
    qc = E(ctx.sb("qc", [128, D], BF16))
    qs = E(ctx.sb("qs", [128, D], BF16))
    fT = E(ctx.sb("fT", [128, 4 * 128], BF16))
    sd = E(ctx.sb("sd", [128, 128], BF16))
    ao = E(ctx.sb("ao", [128, D], F32))
    obuf = E(ctx.sb("obuf", [128, D], BF16))
    obT = E(ctx.sb("obT", [128, D], BF16))
    mcol = E(ctx.sb("mcol", [128, 8], F32))
    ptp = E(ctx.ps("ptp", [128, 128], BF16))
    pq = E(ctx.ps("pq", [128, 2048], F32))
    pA = E(ctx.ps("pA", [128, 128], F32))
    pB = E(ctx.ps("pB", [128, 128], F32))

    x_t = x_in.rearrange("(n p) d -> n p d", p=128)
    y_t = y_out.rearrange("(n p) d -> n p d", p=128)
    wt = wq_in.rearrange("(c p) d -> c p d", p=128)
    for c in range(8):
        ctx.dma("g", wq[:, c * 4 * D:(c + 1) * 4 * D], wt[c])
    wot = wo_in.rearrange("(c p) d -> c p d", p=128)
    for c in range(8):
        ctx.dma("g", wo[:, c * D:(c + 1) * D], wot[c])
    ctx.dma("g", brow[:], qb_in)
    ctx.dma("g", obrow[:], ob_in)
    ctx.dma("g", thr[:], th_in)
    ctx.dma("q", slr[:], sl_in)
    ctx.dma("q", tokc[:], tk_in)
    ctx.dma("g", tril[:], trl_in)
    ctx.dma("q", wprev[:], wp_in)
    ctx.wait("s", "qd")
    ctx.wait("v", "qd")
    ctx.wait("v", "gd")
    ctx.act(slr[:], slr[:], AF.Exp)
    ctx.wait("v", "s")
    rb = E(ctx.sb("rb", [1, 128], BF16))
    ctx.ts(rb[:, 0:8], slr[:], -1.0, OP.add)
    emit_bcast_f32(ctx, rb[:, 0:8], es1_bc[:], ones, pq)
    emit_bcast_f32(ctx, thr[:], th_bc[:], ones, pq)
    ctx.run("v", lambda E_: E_.memset(Sco[:], 0.0), inc="v")
    ctx.run("v", lambda E_: E_.memset(Spv[:], 0.0), inc="v")

    def emit_trig(i):
        ctx.wait("v", "s")
        ctx.ts(tww[:], th_bc[:], tokc[:, i:i + 1], OP.mult)
        ctx.ts(rr[:], tww[:], 1.0 / TWOPI, OP.mult)
        ctx.cp(sint[:], rr[:])
        ctx.cp(rr[:], sint[:])
        ctx.stt(rr[:], rr[:], -TWOPI, tww[:], OP.mult, OP.add)
        ctx.ts(hf[:], rr[:], float(np.pi), OP.is_gt)
        ctx.stt(rr[:], hf[:], -TWOPI, rr[:], OP.mult, OP.add)
        ctx.ts(hf[:], rr[:], float(-np.pi), OP.is_lt)
        ctx.stt(rr[:], hf[:], TWOPI, rr[:], OP.mult, OP.add)
        ctx.ts(tww[:], rr[:], float(np.pi / 2), OP.add)
        ctx.ts(hf[:], tww[:], float(np.pi), OP.is_gt)
        ctx.stt(tww[:], hf[:], -TWOPI, tww[:], OP.mult, OP.add)
        ctx.wait("s", "v")
        ctx.act(sin_t[:], rr[:], AF.Sin)
        ctx.act(cos_t[:], tww[:], AF.Sin)
        ctx.wait("v", "s")

    def emit_qkvu(i, want_q):
        ctx.wait("t", "v")
        emit_transpose_tile8(ctx, hstore[:, i * D:(i + 1) * D], hT[:], ident, ptp)
        ctx.wait("t", "v")
        ctx.wait("t", "s")
        for f in range(4):
            ctx.mm(pq[:, f * 512:(f + 1) * 512], ones[:],
                   brow[:, D + f * 512:D + (f + 1) * 512], start=True, stop=False)
            for c in range(8):
                ctx.mm(pq[:, f * 512:(f + 1) * 512], hT[:, 128 * c:128 * (c + 1)],
                       wq[:, c * 4 * D + D + f * 512:c * 4 * D + D + (f + 1) * 512],
                       start=False, stop=(c == 7), inc=(c == 7))
        ctx.wait("s", "t")
        ctx.act(ksil[:], pq[:, 0:D], AF.Silu)
        ctx.wait("v", "s")
        ctx.cp(vbf[:], pq[:, D:2 * D])
        if want_q:
            ctx.wait("t", "v")
            ctx.wait("t", "s")
            for f in range(2):
                ctx.mm(pq[:, f * 512:(f + 1) * 512], ones[:],
                       brow[:, f * 512:(f + 1) * 512], start=True, stop=False)
                ctx.mm(pq[:, D + f * 512:D + (f + 1) * 512], ones[:],
                       brow[:, 3 * D + f * 512:3 * D + (f + 1) * 512],
                       start=True, stop=False)
                for c in range(8):
                    ctx.mm(pq[:, f * 512:(f + 1) * 512], hT[:, 128 * c:128 * (c + 1)],
                           wq[:, c * 4 * D + f * 512:c * 4 * D + (f + 1) * 512],
                           start=False, stop=(c == 7), inc=(c == 7))
                    ctx.mm(pq[:, D + f * 512:D + (f + 1) * 512],
                           hT[:, 128 * c:128 * (c + 1)],
                           wq[:, c * 4 * D + 3 * D + f * 512:c * 4 * D + 3 * D + (f + 1) * 512],
                           start=False, stop=(c == 7), inc=(c == 7))
            ctx.wait("s", "t")
            ctx.act(qsil[:], pq[:, 0:D], AF.Silu)
            ctx.cp(uf[:], pq[:, D:2 * D], eng="s")

    for i in range(NT):
        ctx.wait("q", "v")
        ctx.wait("q", "s")
        ctx.dma("q", xtile[:], x_t[i])
        ctx.wait("v", "qd")
        ctx.wait("s", "qd")
        emit_ln(ctx, xtile[:], hstore[:, i * D:(i + 1) * D], hf[:], mcol)
        emit_qkvu(i, want_q=False)
        emit_trig(i)
        ctx.tt(kc[:], ksil[:], cos_t[:], OP.mult)
        ctx.tt(ks[:], ksil[:], sin_t[:], OP.mult)
        for h in range(8):
            hs = slice(128 * h, 128 * (h + 1))
            ctx.wait("t", "v")
            ctx.mm(pA[:], kc[:, hs], vbf[:, hs], start=True, stop=True, inc=True)
            ctx.mm(pB[:], ks[:, hs], vbf[:, hs], start=True, stop=True, inc=True)
            ctx.wait("v", "t")
            ctx.tt(Sco[:, 256 * h:256 * h + 128], Sco[:, 256 * h:256 * h + 128],
                   pA[:], OP.add)
            ctx.tt(Sco[:, 256 * h + 128:256 * h + 256],
                   Sco[:, 256 * h + 128:256 * h + 256], pB[:], OP.add)

    ctx.wait("g", "v")
    for h in range(8):
        ctx.dma("g", s_pubh[h].ap(), Sco[:, 256 * h:256 * (h + 1)], inc="gd")
    ctx.wait("g", "gd")
    for h in range(8):
        cc_at = emit_allgather(ctx, s_pubh[h].ap(), s_agh[h].ap())
    ctx.wait("g", "cc", cc_at)
    ctx.wait("v", "gd")
    ctx.run("v", lambda E_: E_.memset(Sto[:], 0.0), inc="v")
    for r in range(NCORES):
        ctx.wait("g", "v")
        for h in range(8):
            ctx.dma("g", blk[:, 256 * h:256 * (h + 1)],
                    s_agh[h].ap().rearrange("(r p) d -> r p d", p=128)[r])
        ctx.wait("v", "gd")
        ctx.stt(Spv[:], blk[:], wprev[:, r:r + 1], Spv[:], OP.mult, OP.add)
        ctx.tt(Sto[:], Sto[:], blk[:], OP.add)
    if dbg:
        ctx.wait("g", "v")
        ctx.dma("g", d_sco, Sto[:], inc="gd")
    for h in range(8):
        ctx.stt(Spv[:, 256 * h:256 * (h + 1)], Spv[:, 256 * h:256 * (h + 1)],
                es1_bc[:, h:h + 1], Sto[:, 256 * h:256 * (h + 1)],
                OP.mult, OP.add)
    ctx.cp(Mbf[:], Spv[:])
    if dbg:
        ctx.wait("g", "v")
        ctx.dma("g", d_M, Spv[:], inc="gd")
        ctx.dma("g", d_ag, s_ag.ap(), inc="gd")
        ctx.wait("v", "gd")

    for i in range(NT):
        emit_qkvu(i, want_q=True)
        emit_trig(i)
        ctx.wait("v", "s")
        ctx.tt(kc[:], ksil[:], cos_t[:], OP.mult)
        ctx.tt(ks[:], ksil[:], sin_t[:], OP.mult)
        ctx.tt(qc[:], qsil[:], cos_t[:], OP.mult)
        ctx.tt(qs[:], qsil[:], sin_t[:], OP.mult)
        for h in range(8):
            hs = slice(128 * h, 128 * (h + 1))
            ctx.wait("t", "v")
            for j, src in enumerate((qc, qs, kc, ks)):
                emit_transpose(ctx, src[:, hs], fT[:, 128 * j:128 * (j + 1)],
                               ident, ptp)
            ctx.wait("t", "v")
            ctx.mm(pA[:], fT[:, 256:384], fT[:, 0:128], start=True, stop=False)
            ctx.mm(pA[:], fT[:, 384:512], fT[:, 128:256], start=False, stop=True,
                   inc=True)
            ctx.wait("v", "t")
            ctx.stt(sd[:], pA[:], es1_bc[:, h:h + 1], tril[:], OP.mult, OP.mult)
            ctx.wait("t", "v")
            ctx.mm(pB[:], sd[:], vbf[:, hs], start=True, stop=False)
            ctx.mm(pB[:], fT[:, 0:128], Mbf[:, 256 * h:256 * h + 128],
                   start=False, stop=False)
            ctx.mm(pB[:], fT[:, 128:256], Mbf[:, 256 * h + 128:256 * h + 256],
                   start=False, stop=True, inc=True)
            ctx.wait("v", "t")
            ctx.cp(ao[:, hs], pB[:])
            ctx.wait("t", "v")
            ctx.mm(pA[:], kc[:, hs], vbf[:, hs], start=True, stop=True, inc=True)
            ctx.wait("v", "t")
            ctx.stt(Spv[:, 256 * h:256 * h + 128], pA[:], es1_bc[:, h:h + 1],
                    Spv[:, 256 * h:256 * h + 128], OP.mult, OP.add)
            ctx.cp(Mbf[:, 256 * h:256 * h + 128], Spv[:, 256 * h:256 * h + 128])
            ctx.wait("t", "v")
            ctx.mm(pA[:], ks[:, hs], vbf[:, hs], start=True, stop=True, inc=True)
            ctx.wait("v", "t")
            ctx.stt(Spv[:, 256 * h + 128:256 * h + 256], pA[:], es1_bc[:, h:h + 1],
                    Spv[:, 256 * h + 128:256 * h + 256], OP.mult, OP.add)
            ctx.cp(Mbf[:, 256 * h + 128:256 * h + 256],
                   Spv[:, 256 * h + 128:256 * h + 256])
        ctx.wait("s", "v")
        ctx.act(obuf[:], ao[:], AF.Square, accum=mcol[:, 4:5])
        ctx.wait("v", "s")
        ctx.ts(mcol[:, 4:5], mcol[:, 4:5], 1.0 / D, OP.mult, 1e-6, OP.add)
        ctx.wait("s", "v")
        ctx.act(mcol[:, 5:6], mcol[:, 4:5], AF.Sqrt)
        ctx.wait("v", "s")
        ctx.run("v", lambda E_: E_.reciprocal(out=mcol[:, 6:7], in_=mcol[:, 5:6]),
                inc="v")
        ctx.ts(ao[:], ao[:], mcol[:, 6:7], OP.mult)
        ctx.tt(obuf[:], ao[:], uf[:], OP.mult)
        ctx.wait("t", "v")
        emit_transpose_tile8(ctx, obuf[:], obT[:], ident, ptp)
        ctx.wait("q", "v")
        ctx.wait("q", "od")
        ctx.dma("q", xtile[:], x_t[i])
        emit_proj(ctx, obT[:], wo[:], pq, [512, 512], bias=obrow, ones=ones)
        ctx.wait("v", "qd")
        ctx.wait("v", "t")
        ctx.tt(xtile[:], xtile[:], pq[:, 0:D], OP.add)
        ctx.wait("g", "v")
        ctx.dma("g", y_t[i, :, :], xtile[:], inc="od")




def build_L0a():
    nc = bass.Bass("TRN2", target_bir_lowering=False, debug=False)
    x_in = nc.dram_tensor("x", [TOK, D], F32, kind="ExternalInput").ap()
    y_out = nc.dram_tensor("y", [TOK, D], F32, kind="ExternalOutput").ap()
    ctx = Ctx(nc)
    with ExitStack() as stack:
        ctx.alloc_sems(stack)
        ctx.ident, ctx.ones = emit_prelude(ctx, stack)
        ctx.scope = "a0_"
        emit_L0a(ctx, stack, x_in, y_out)
        ctx.wait("q", "od", ctx.cnt["od"])
        ctx.flush()
    return nc

def run_L0a(x_full, INP):
    nc = build_L0a()
    xs = _shard_x(x_full)
    T = lambda a: np.ascontiguousarray(np.asarray(a, np.float32).T)
    R1 = lambda a: np.ascontiguousarray(np.asarray(a, np.float32).reshape(1, -1))
    common = dict(
        qkvuT=T(INP["L0_attn_qkvu_w"]), qkvub=R1(INP["L0_attn_qkvu_b"]),
        outwT=T(INP["L0_attn_out_w"]), outb=R1(INP["L0_attn_out_b"]),
        slope=R1(INP["L0_attn_slope_rate"]), theta=R1(INP["L0_attn_theta"]),
        trilT=np.ascontiguousarray(np.triu(np.ones((128, 128)), 0), np.float32),
    )
    maps = []
    for r in range(NCORES):
        tok = np.arange(TOK, dtype=np.float32) + r * TOK
        tokcol = np.ascontiguousarray(tok.reshape(NT, 128).T)
        wprev = np.zeros((128, 8), np.float32)
        wprev[:, :r] = 1.0
        maps.append(dict(x=xs[r], tokcol=tokcol, wprev=wprev, **common))
    res = run_bass_kernel_spmd(nc, _prefix_maps(maps, "a0_"), core_ids=list(range(NCORES)))
    return np.concatenate([res.results[r]["y"] for r in range(NCORES)], 0)



def emit_L2a(ctx, stack, x_in, y_out, dbg=False):
    nc = ctx.nc
    E = stack.enter_context
    ident, ones = ctx.ident, ctx.ones
    if dbg:
        d_q32 = ctx.dr("d_q32", [32, SEQ], F32, kind="ExternalOutput").ap()
        d_klat = ctx.dr("d_klat", [32, SEQ], F32, kind="ExternalOutput").ap()
        d_v65 = ctx.dr("d_v65", [128, 128 * 33], F32, kind="ExternalOutput").ap()
        d_part = ctx.dr("d_part", [128, D], F32, kind="ExternalOutput").ap()
        d_rs = ctx.dr("d_rs", [256, D], F32, kind="ExternalOutput").ap()
        d_pub = ctx.dr("d_pub", [NCORES * 256, D], F32, kind="ExternalOutput").ap()
        d_l = ctx.dr("d_l", [128, 1], F32, kind="ExternalOutput").ap()
    wqs_in = ctx.dr("wq_slice", [D, 128], F32, kind="ExternalInput").ap()
    wkd_in = ctx.dr("wkd_slice", [D, 32], F32, kind="ExternalInput").ap()
    wvd_in = ctx.dr("wvd_slice", [D, 32], F32, kind="ExternalInput").ap()
    wkup_in = ctx.dr("wkupT", [128, 32], F32, kind="ExternalInput").ap()
    wvup_in = ctx.dr("wvup", [32, 128], F32, kind="ExternalInput").ap()
    wos_in = ctx.dr("wo_slice", [128, D], F32, kind="ExternalInput").ap()
    m4_in = ctx.dr("mask4", [128, 4 * 512], F32, kind="ExternalInput").ap()
    trl_in = ctx.dr("trilT2", [128, 128], F32, kind="ExternalInput").ap()
    h_pub = [ctx.dr(f"h_pub{j}", [256, D], BF16) for j in range(8)]
    h_ag = [ctx.dr(f"h_ag{j}", [NCORES * 256, D], BF16) for j in range(8)]
    p_pub = [ctx.dr(f"p_pub{b}", [NCORES * 256, D], BF16) for b in range(8)]
    p_rs = [ctx.dr(f"p_rs{b}", [256, D], BF16) for b in range(8)]

    SCALE = float(128 ** -0.5)
    NGT = 128
    wqs = E(ctx.sb("wqs", [128, 8 * 128], BF16))
    wkd = E(ctx.sb("wkd", [128, 8 * 32], BF16))
    wvd = E(ctx.sb("wvd", [128, 8 * 32], BF16))
    wkup = E(ctx.sb("wkup_sb", [128, 32], BF16))
    wvup = E(ctx.sb("wvup_sb", [32, 128], BF16))
    wos = E(ctx.sb("wos", [128, D], BF16))
    wq32 = E(ctx.sb("wq32", [128, 8 * 32], BF16))
    mask4 = E(ctx.sb("mask4_sb", [128, 4 * 512], F32))
    tril2 = E(ctx.sb("tril2", [128, 128], F32))
    q33 = E(ctx.sb("q33", [33, SEQ], BF16))
    klat = E(ctx.sb("klat", [33, SEQ], BF16))

    v65 = E(ctx.sb("v65", [128, NGT * 33], BF16))
    xtile = E(ctx.sb("xtile", [128, D], F32))
    hf = E(ctx.sb("hf", [128, D], F32))
    hbf = E(ctx.sb("hbf", [128, D], BF16))
    hT = E(ctx.sb("hT", [128, D], BF16))
    vl = E(ctx.sb("vl", [32, 128], BF16))
    mrow = E(ctx.sb("mrow", [128, 8], F32))
    mbcol = E(ctx.sb("mbcol", [128, 1], BF16))
    Pbf = E(ctx.sb("Pbf", [128, 1024], BF16))
    y32 = E(ctx.sb("y32", [32, 128], BF16))
    y128 = E(ctx.sb("y128", [128, 128], BF16))
    lrow = E(ctx.sb("lrow", [1, 128], BF16))
    rcol = E(ctx.sb("rcol", [128, 1], F32))
    pout = E(ctx.sb("pout", [128, D], BF16))
    mcol = E(ctx.sb("mcol", [128, 8], F32))
    ptp = E(ctx.ps("ptp", [128, 128], BF16))
    pq3 = E(ctx.ps("pq3", [128, 1024], F32))
    psT = E(ctx.ps("psT", [128, 2048], F32))
    pov = E(ctx.ps("pov", [33, 128], F32))
    pwo = psT

    x_t = x_in.rearrange("(n p) d -> n p d", p=128)
    y_t = y_out.rearrange("(n p) d -> n p d", p=128)
    for w_sb, w_ap, wdt in [(wqs, wqs_in, 128), (wkd, wkd_in, 32), (wvd, wvd_in, 32)]:
        wt = w_ap.rearrange("(c p) d -> c p d", p=128)
        for c in range(8):
            ctx.dma("g", w_sb[:, c * wdt:(c + 1) * wdt], wt[c])
    ctx.dma("g", wkup[:], wkup_in)
    ctx.dma("g", wvup[:], wvup_in)
    ctx.dma("g", wos[:], wos_in)
    ctx.dma("q", mask4[:], m4_in)
    ctx.dma("q", tril2[:], trl_in)
    ctx.run("g", lambda E_: E_.memset(v65[:], 1.0), inc="g")
    ctx.run("g", lambda E_: E_.memset(klat[32:33, :], 1.0), inc="g")
    ctx.wait("v", "gd")
    ctx.wait("v", "qd")
    ctx.wait("t", "gd")
    for c in range(8):
        emit_transpose(ctx, wqs[:, 128 * c:128 * (c + 1)], hT[:, 0:128], ident, ptp)
        ctx.wait("t", "v")
        at = ctx.mm(pq3[:, 0:32], hT[:, 0:128], wkup[:], start=True, stop=True,
                    inc=True)
        ctx.wait("v", "t", at)
        ctx.cp(wq32[:, 32 * c:32 * (c + 1)], pq3[:, 0:32])

    for i in range(NT):
        ctx.wait("q", "v")
        ctx.wait("q", "s")
        ctx.dma("q", xtile[:], x_t[i])
        ctx.wait("v", "qd")
        ctx.wait("s", "qd")
        emit_ln(ctx, xtile[:], hbf[:], hf[:], mcol)
        ctx.wait("g", "v")
        ctx.dma("g", h_pub[i // 2].ap().rearrange("(a p) d -> a p d", p=128)[i % 2],
                hbf[:], inc="gd")
    ctx.wait("g", "gd")
    for j in range(8):
        cc_at = emit_allgather(ctx, h_pub[j].ap(), h_ag[j].ap())
    ctx.wait("q", "cc", cc_at)

    def hag_tile(n):
        r, j, a = n // 16, (n % 16) // 2, n % 2
        return h_ag[j].ap().rearrange("(b p) d -> b p d", p=128)[r * 2 + a]

    for n in range(NGT):
        ctx.wait("q", "v")
        ctx.wait("q", "t")
        ctx.dma("q", hbf[:], hag_tile(n))
        ctx.wait("t", "qd")
        ctx.wait("t", "v")
        emit_transpose_tile8(ctx, hbf[:], hT[:], ident, ptp)
        for (wsb, dst, scale) in [(wq32, q33, SCALE), (wkd, klat, None),
                                  (wvd, None, None)]:
            ctx.wait("t", "v")
            ctx.wait("t", "s")
            for c in range(8):
                ctx.mm(pq3[0:32, 0:128], wsb[:, 32 * c:32 * (c + 1)],
                       hT[:, 128 * c:128 * (c + 1)],
                       start=(c == 0), stop=(c == 7), inc=(c == 7))
            at = ctx.cnt["t"]
            if dst is not None:
                ctx.wait("s", "t", at)
                if scale is not None:
                    ctx.act(dst[0:32, 128 * n:128 * (n + 1)], pq3[0:32, 0:128],
                            AF.Copy, scale=scale)
                else:
                    ctx.cp(dst[0:32, 128 * n:128 * (n + 1)], pq3[0:32, 0:128],
                           eng="s")
            else:
                ctx.wait("v", "t", at)
                ctx.cp(vl[:], pq3[0:32, 0:128])
                ctx.wait("t", "v")
                tat = ctx.transpose(ptp[0:128, 0:32], vl[:], ident[0:32, 0:32])
                ctx.wait("v", "t", tat)
                ctx.cp(v65[:, 33 * n:33 * n + 32], ptp[0:128, 0:32])

    if dbg:
        ctx.wait("g", "s")
        ctx.wait("g", "v")
        ctx.dma("g", d_q32, q33[0:32, :], inc="gd")
        ctx.dma("g", d_klat, klat[0:32, :], inc="gd")
        ctx.dma("g", d_v65, v65[:], inc="gd")
    qrec = {}
    for qt in range(NGT):
        qsl = q33[0:32, 128 * qt:128 * (qt + 1)]
        nb = (qt + 4) // 4
        for b in range(nb):
            last = (b == nb - 1)
            pb = (b % 2) * 512
            if b >= 2:
                ctx.wait("t", "v", qrec["p1v", b - 2])
            elif qt > 0:
                ctx.wait("t", "v")
            at = ctx.mm(pq3[:, pb:pb + 512], qsl,
                        klat[0:32, 512 * b:512 * b + 512],
                        start=True, stop=True, inc=True)
            ctx.wait("v", "t", at)
            if last:
                off = qt - 4 * (nb - 1)
                ctx.tt(pq3[:, pb:pb + 512], pq3[:, pb:pb + 512],
                       mask4[:, 512 * off:512 * (off + 1)], OP.add)
            if b == 0:
                ctx.red(mrow[:, 1:2], pq3[:, pb:pb + 512], OP.max)
            else:
                ctx.red(mrow[:, 0:1], pq3[:, pb:pb + 512], OP.max)
                ctx.tt(mrow[:, 1:2], mrow[:, 1:2], mrow[:, 0:1], OP.max)
            qrec["p1v", b] = ctx.cnt["v"]
        ctx.ts(mrow[:, 2:3], mrow[:, 1:2], -1.0, OP.mult)
        ctx.cp(mbcol[:], mrow[:, 2:3])
        ctx.wait("t", "v")
        tat = ctx.transpose(ptp[0:1, 0:128], mbcol[:], ident[:])
        ctx.wait("v", "t", tat)
        ctx.cp(q33[32:33, 128 * qt:128 * (qt + 1)], ptp[0:1, 0:128])
        q33v = ctx.cnt["v"]
        first_pv = True
        for gi, kt0 in enumerate(range(0, qt + 1, 4)):
            kts = [kt for kt in range(kt0, min(kt0 + 4, qt + 1))]
            gb = (gi % 2) * 512
            if gi >= 2:
                ctx.wait("t", "s", qrec["exps", gi - 2])
                if ("trlv", gi - 2) in qrec:
                    ctx.wait("t", "v", qrec["trlv", gi - 2])
            else:
                ctx.wait("t", "s")
                ctx.wait("t", "v", q33v)
            for j, kt in enumerate(kts):
                ctx.mm(psT[:, gb + 128 * j:gb + 128 * (j + 1)],
                       klat[:, 128 * kt:128 * (kt + 1)],
                       q33[:, 128 * qt:128 * (qt + 1)],
                       start=True, stop=True, inc=(j == len(kts) - 1))
            at = ctx.cnt["t"]
            if kts[-1] == qt:
                j = len(kts) - 1
                ctx.wait("v", "t", at)
                ctx.tt(psT[:, gb + 128 * j:gb + 128 * (j + 1)],
                       psT[:, gb + 128 * j:gb + 128 * (j + 1)],
                       tril2[:], OP.add)
                qrec["trlv", gi] = ctx.cnt["v"]
                ctx.wait("s", "v", qrec["trlv", gi])
            ctx.wait("s", "t", at)
            if gi >= 2:
                ctx.wait("s", "t", qrec["pvt", gi - 2])
            ctx.act(Pbf[:, gb:gb + 512], psT[:, gb:gb + 512], AF.Exp)
            qrec["exps", gi] = ctx.cnt["s"]
            ctx.wait("t", "s", qrec["exps", gi])
            for j, kt in enumerate(kts):
                ctx.mm(pov[:], v65[:, 33 * kt:33 * (kt + 1)],
                       Pbf[:, gb + 128 * j:gb + 128 * (j + 1)],
                       start=first_pv, stop=(kt == qt), inc=(kt == qt))
                first_pv = False
            qrec["pvt", gi] = ctx.cnt["t"]
        pv_at = ctx.cnt["t"]
        ctx.wait("v", "t", pv_at)
        ctx.cp(y32[:], pov[0:32, :])
        ctx.cp(lrow[:], pov[32:33, :])
        ctx.wait("t", "v")
        at = ctx.mm(pq3[:, 0:128], wvup[:], y32[:],
                    start=True, stop=True, inc=True)
        ctx.wait("v", "t", at)
        ctx.cp(y128[:], pq3[:, 0:128])
        ctx.wait("t", "v")
        tat = ctx.transpose(ptp[0:128, 0:1], lrow[:], ident[0:1, 0:1])
        ctx.wait("v", "t", tat)
        ctx.cp(rcol[:], ptp[0:128, 0:1])
        ctx.run("v", lambda E_: E_.reciprocal(out=rcol[:], in_=rcol[:]), inc="v")
        ctx.wait("t", "v")
        for f in range(2):
            ctx.mm(pwo[:, 512 * f:512 * (f + 1)], y128[:],
                   wos[:, 512 * f:512 * (f + 1)], start=True, stop=True,
                   inc=(f == 1))
        at = ctx.cnt["t"]
        ctx.wait("v", "t", at)
        ctx.ts(pout[:], pwo[:, 0:1024], rcol[:], OP.mult)
        if dbg and qt == 5:
            ctx.wait("g", "v")
            ctx.cp(hf[:], pout[:])
            ctx.run("v", lambda E_: E_.reciprocal(out=mrow[:, 7:8], in_=rcol[:]), inc="v")
            ctx.wait("g", "v")
            ctx.dma("g", d_part, hf[:], inc="gd")
            ctx.dma("g", d_l, mrow[:, 7:8], inc="gd")
            ctx.wait("v", "gd")
        r_, b_, a_ = qt // 16, (qt % 16) // 2, qt % 2
        ctx.wait("g", "v")
        ctx.dma("g", p_pub[b_].ap().rearrange("(k p) d -> k p d", p=128)[r_ * 2 + a_],
                pout[:], inc="gd")

    ctx.wait("g", "gd")
    for b in range(8):
        cc2 = ctx.run("g", lambda E_, b=b: E_.collective_compute(
            "ReduceScatter", OP.add, replica_groups=[list(range(NCORES))],
            ins=[p_pub[b].ap().opt()], outs=[p_rs[b].ap().opt()]), inc="cc", n=1)
    ctx.wait("q", "cc", cc2)

    if dbg:
        ctx.wait("g", "cc")
        ctx.dma("g", d_rs, p_rs[0].ap(), inc="gd")
        ctx.dma("g", d_pub, p_pub[0].ap(), inc="gd")
    for i in range(NT):
        ctx.wait("q", "v")
        ctx.wait("q", "od")
        ctx.dma("q", xtile[:], x_t[i])
        ctx.dma("q", hbf[:], p_rs[i // 2].ap().rearrange("(a p) d -> a p d", p=128)[i % 2])
        ctx.wait("v", "qd")
        if dbg and i == 3:
            ctx.cp(hf[:], hbf[:])
            ctx.wait("g", "v")
            ctx.dma("g", d_part, hf[:], inc="gd")
            ctx.wait("v", "gd")
        ctx.tt(xtile[:], xtile[:], hbf[:], OP.add)
        ctx.wait("g", "v")
        ctx.dma("g", y_t[i, :, :], xtile[:], inc="od")




def build_L2a(dbg=False):
    nc = bass.Bass("TRN2", target_bir_lowering=False, debug=False)
    x_in = nc.dram_tensor("x", [TOK, D], F32, kind="ExternalInput").ap()
    y_out = nc.dram_tensor("y", [TOK, D], F32, kind="ExternalOutput").ap()
    ctx = Ctx(nc)
    with ExitStack() as stack:
        ctx.alloc_sems(stack)
        ctx.ident, ctx.ones = emit_prelude(ctx, stack)
        ctx.scope = "a2_"
        emit_L2a(ctx, stack, x_in, y_out, dbg)
        ctx.wait("q", "od", ctx.cnt["od"])
        ctx.flush()
    return nc

def run_L2a(x_full, INP):
    nc = build_L2a()
    xs = _shard_x(x_full)
    wq = np.asarray(INP["L2_attn_wq"], np.float32)
    wkd = np.asarray(INP["L2_attn_wk_down"], np.float32)
    wvd = np.asarray(INP["L2_attn_wv_down"], np.float32)
    wkup = np.asarray(INP["L2_attn_wk_up"], np.float32)
    wvup = np.asarray(INP["L2_attn_wv_up"], np.float32)
    wo = np.asarray(INP["L2_attn_wo"], np.float32)
    m4 = np.zeros((128, 4, 512), np.float32)
    qi = np.arange(128)[:, None]
    for o in range(4):
        m4[:, o, (o + 1) * 128:] = -1e30
        ki = np.arange(128)[None, :]
        m4[:, o, o * 128:(o + 1) * 128] = np.where(ki <= qi, 0.0, -1e30)
    trl = np.where(np.arange(128)[:, None] <= np.arange(128)[None, :], 0.0, -1e30)
    maps = []
    for r in range(NCORES):
        kv = r // 4
        maps.append(dict(
            x=xs[r],
            wq_slice=np.ascontiguousarray(wq[:, r * 128:(r + 1) * 128]),
            wkd_slice=np.ascontiguousarray(wkd[:, kv * 32:(kv + 1) * 32]),
            wvd_slice=np.ascontiguousarray(wvd[:, kv * 32:(kv + 1) * 32]),
            wkupT=np.ascontiguousarray(wkup.T),
            wvup=np.ascontiguousarray(wvup),
            wo_slice=np.ascontiguousarray(wo[r * 128:(r + 1) * 128, :]),
            mask4=np.ascontiguousarray(m4.reshape(128, 4 * 512)),
            trilT2=np.ascontiguousarray(trl, np.float32),
        ))
    res = run_bass_kernel_spmd(nc, _prefix_maps(maps, "a2_"), core_ids=list(range(NCORES)))
    return np.concatenate([res.results[r]["y"] for r in range(NCORES)], 0)



def emit_L3f(ctx, stack, x_in, y_out, dbg=False, nph=12):
    nc = ctx.nc
    E = stack.enter_context
    ident, ones = ctx.ident, ctx.ones
    if dbg:
        d_P = ctx.dr("d_P", [128, 512], F32, kind="ExternalOutput").ap()
        d_sc = ctx.dr("d_sc", [128, 512], F32, kind="ExternalOutput").ap()
    wqkv_in = ctx.dr("wqkv", [6, D, 3072], F32, kind="ExternalInput").ap()
    wo_in = ctx.dr("wol", [6, D, D], F32, kind="ExternalInput").ap()
    w1_in = ctx.dr("w1l", [6, D, 5460], F32, kind="ExternalInput").ap()
    b1_in = ctx.dr("b1l", [6, 5460], F32, kind="ExternalInput").ap()
    w2_in = ctx.dr("w2l", [6, 2730, D], F32, kind="ExternalInput").ap()
    b2_in = ctx.dr("b2l", [6, D], F32, kind="ExternalInput").ap()
    qs_in = ctx.dr("qsl", [6, 64], F32, kind="ExternalInput").ap()
    ks_in = ctx.dr("ksl", [6, 64], F32, kind="ExternalInput").ap()
    lng_in = ctx.dr("lngl", [6, D], F32, kind="ExternalInput").ap()
    lnb_in = ctx.dr("lnbl", [6, D], F32, kind="ExternalInput").ap()
    tcq_in = ctx.dr("tabcq", [128, 19 * 64], F32, kind="ExternalInput").ap()
    tsq_in = ctx.dr("tabsq", [128, 19 * 64], F32, kind="ExternalInput").ap()
    tck_in = ctx.dr("tabck", [128, 19 * 64], F32, kind="ExternalInput").ap()
    tsk_in = ctx.dr("tabsk", [128, 19 * 64], F32, kind="ExternalInput").ap()
    msk_in = ctx.dr("masks", [128, 4 * 1024], F32, kind="ExternalInput").ap()
    hidx_in = ctx.dr("hidx", [128, 1], U32, kind="ExternalInput").ap()
    hv_in = ctx.dr("hv", [1, 1], F32, kind="ExternalInput").ap()
    xd = [ctx.dr(f"xbuf{b}", [TOK, D], F32) for b in range(2)]
    xh = ctx.dr("xh0", [TOK, D], F32)
    hp = [ctx.dr(f"hpub{j}", [128, D], BF16) for j in range(3)]
    ha = [ctx.dr(f"hag{j}", [NCORES * 128, D], BF16) for j in range(3)]


    def barrier():
        for e in ENGS:
            for k in ("t", "v", "s", "g", "qd", "gd", "cc", "od", "wd"):
                ctx.wait(e, k)

    tabs = E(ctx.sb("tabs", [128, 4 * 19 * 64], BF16))
    masks = E(ctx.sb("masks_sb", [128, 4 * 1024], BF16))
    hidx = E(ctx.sb("hidx_sb", [128, 1], U32))
    rowst = E(ctx.sb("rowst", [1, D], F32))
    rowbf = E(ctx.sb("rowbf", [1, D], BF16))
    gbc = E(ctx.sb("gbc", [128, D], BF16))
    bbc = E(ctx.sb("bbc", [128, D], BF16))
    qsbc = E(ctx.sb("qsbc", [128, 64], F32))
    ksbc = E(ctx.sb("ksbc", [128, 64], F32))
    qsr = E(ctx.sb("qsr", [128, 64], F32))
    ksr = E(ctx.sb("ksr", [128, 64], F32))
    AB = E(ctx.sb("AB", [128, 4 * 64], F32))
    xa = E(ctx.sb("xa", [128, D], F32))
    sh = E(ctx.sb("shf", [128, D], F32))
    shbf = E(ctx.sb("shbf", [128, D], BF16))
    shT = E(ctx.sb("shT", [128, D], BF16))
    prowf = E(ctx.sb("prowf", [1, 512], F32))
    hal = E(ctx.sb("hal", [128, 3 * D], BF16))
    kf = E(ctx.sb("kf", [128, D], F32))
    rot = E(ctx.sb("rot", [128, D], F32))
    rn = E(ctx.sb("rn", [128, 16], F32))
    rope = E(ctx.sb("rope", [128, D], BF16))
    kTr = E(ctx.sb("kTr", [128, 8 * 4 * 128], BF16))
    vr = E(ctx.sb("vr", [128, 4 * D], BF16))
    qT = E(ctx.sb("qT", [128, 8 * 128], BF16))
    Pbf = E(ctx.sb("Pbf2", [128, 2 * 512], BF16))
    PT = E(ctx.sb("PT", [128, 2 * 512], BF16))
    ytile = E(ctx.sb("ytile", [128, D], BF16))
    yT = E(ctx.sb("yT", [128, D], BF16))
    mcol = E(ctx.sb("mcol2", [128, 16], F32))
    tf2 = E(ctx.sb("tf2", [128, 512], BF16))
    agbf = E(ctx.sb("agbf", [128, 2 * 512], BF16))
    agT = E(ctx.sb("agT2", [128, 512], BF16))
    ptpd = [E(ctx.ps(f"ptp2_{j}", [128, 512], BF16)) for j in range(2)]
    ptp = ptpd[0]
    psc = E(ctx.ps("psc", [128, 2048], F32))

    ctx.dma("g", tabs[:, 0:19 * 64], tcq_in)
    ctx.dma("g", tabs[:, 19 * 64:2 * 19 * 64], tsq_in)
    ctx.dma("g", tabs[:, 2 * 19 * 64:3 * 19 * 64], tck_in)
    ctx.dma("g", tabs[:, 3 * 19 * 64:4 * 19 * 64], tsk_in)
    ctx.dma("g", masks[:], msk_in)
    ctx.dma("q", hidx[:], hidx_in)
    hv = E(ctx.sb("hv_sb", [1, 1], F32))
    ctx.dma("q", hv[:], hv_in)

    def bcast_row(src_ap, dst_bc, wid):
        ctx.wait("q", "v")
        ctx.wait("q", "t")
        ctx.dma("q", rowst[:, 0:wid], src_ap)
        ctx.wait("v", "qd")
        ctx.cp(rowbf[:, 0:wid], rowst[:, 0:wid])
        emit_bcast_f32(ctx, rowbf[:, 0:wid], dst_bc, ones, psc)

    def emit_shift(e, src_dram_tile, from_halo, xbuf=None, preloaded=False,
                   prow_rd=None, prow_wr=None):
        if from_halo:
            ctx.wait("g", "v")
            ctx.wait("g", "t")
            ctx.wait("g", "s")
            ctx.dma("g", sh[:, 0:512], hal[:, (e * D):(e * D) + 512])
            ctx.dma("g", sh[1:128, 512:1024], hal[0:127, e * D + 512:e * D + 1024])
            ctx.dma("g", sh[0:1, 512:1024], prowf[:])
            ctx.wait("g", "gd")
            ctx.dma("g", prowf[:], hal[127:128, e * D + 512:e * D + 1024])
            ctx.wait("v", "gd")
            ctx.ts(prowf[:], prowf[:], hv[0:1, 0:1], OP.mult)
            ctx.wait("t", "gd")
            ctx.wait("s", "gd")
        else:
            xb = xa if xbuf is None else xbuf
            rd = prowf if prow_rd is None else prow_rd
            wr = prowf if prow_wr is None else prow_wr
            ctx.wait("q", "v")
            ctx.wait("q", "t")
            ctx.wait("q", "s")
            ctx.wait("q", "od")
            ctx.wait("q", "gd")
            ctx.wait("g", "v")
            if not preloaded:
                ctx.dma("q", xb[:], src_dram_tile)
            ctx.wait("q", "qd")
            ctx.dma("q", sh[:, 0:512], xb[:, 0:512])
            ctx.dma("q", sh[1:128, 512:1024], xb[0:127, 512:1024])
            ctx.dma("q", sh[0:1, 512:1024], rd[:])
            if rd is wr:
                ctx.wait("q", "qd")
            ctx.dma("q", wr[:], xb[127:128, 512:1024])
            ctx.wait("v", "qd")
            ctx.wait("t", "qd")
            ctx.wait("s", "qd")

    def l2norm_rope(tab_base, dst_bf):
        ctx.tt(rot[:], kf[:], kf[:], OP.mult)
        ctx.red(rn[:], rot.ap().rearrange("p (h d) -> p h d", d=64), OP.add)
        ctx.wait("s", "v")
        ctx.act(rn[:], rn[:], AF.Sqrt)
        ctx.wait("v", "s")
        ctx.ts(rn[:], rn[:], 1e-12, OP.max)
        ctx.run("v", lambda E_: E_.reciprocal(out=rn[:], in_=rn[:]), inc="v")
        v3 = kf.ap().rearrange("p (h a d) -> p h a d", a=2, d=32)
        r3 = rot.ap().rearrange("p (h a d) -> p h a d", a=2, d=32)
        ctx.ts(r3[:, :, 0, :], v3[:, :, 1, :], -1.0, OP.mult)
        ctx.cp(r3[:, :, 1, :], v3[:, :, 0, :])
        rnb = rn.ap().unsqueeze(2).broadcast_to([128, 16, 64])
        Ab = AB.ap()[:, tab_base:tab_base + 64].unsqueeze(1).broadcast_to([128, 16, 64])
        Bb = AB.ap()[:, tab_base + 64:tab_base + 128].unsqueeze(1).broadcast_to([128, 16, 64])
        k3 = kf.ap().rearrange("p (h d) -> p h d", d=64)
        ro3 = rot.ap().rearrange("p (h d) -> p h d", d=64)
        t3 = sh.ap().rearrange("p (h d) -> p h d", d=64)
        d3 = dst_bf.ap().rearrange("p (h d) -> p h d", d=64)
        ctx.tt(t3, k3, rnb, OP.mult)
        ctx.tt(t3, t3, Ab, OP.mult)
        ctx.tt(ro3, ro3, rnb, OP.mult)
        ctx.tt(ro3, ro3, Bb, OP.mult)
        ctx.tt(d3, t3, ro3, OP.add)

    ctx.run("g", lambda E_: E_.memset(kTr[:], 0.0), inc="g")
    ctx.run("g", lambda E_: E_.memset(vr[:], 0.0), inc="g")
    ctx.run("g", lambda E_: E_.memset(qT[:], 0.0), inc="g")
    xin_t = x_in.rearrange("(n p) d -> n p d", p=128)
    xh_t = xh.ap().rearrange("(n p) d -> n p d", p=128)
    for i in range(NT):
        ctx.wait("q", "v")
        ctx.wait("q", "s")
        ctx.wait("q", "gd")
        ctx.dma("q", xa[:], xin_t[i])
        ctx.wait("v", "qd")
        ctx.wait("s", "qd")
        emit_ln(ctx, xa[:], shbf[:], sh[:], mcol, out_f32=kf[:])
        ctx.wait("g", "v")
        ctx.dma("g", xh_t[i, :, :], kf[:], inc="gd")

    for ph in range(nph):
        li = ph // 2
        is_attn = (ph % 2 == 0)
        src = xh.ap() if ph == 0 else xd[(ph - 1) % 2].ap()
        dst = xd[ph % 2].ap()
        src_t = src.rearrange("(n p) d -> n p d", p=128)
        dst_t = dst.rearrange("(n p) d -> n p d", p=128)
        nhalo = 3 if is_attn else 1
        ctx.wait("g", "od")
        ctx.wait("g", "gd")
        ctx.wait("g", "v")
        for j in range(nhalo):
            ctx.dma("g", hp[j].ap(), src_t[NT - nhalo + j])
        ctx.wait("g", "gd")
        for j in range(nhalo):
            cc_at = emit_allgather(ctx, hp[j].ap(), ha[j].ap())
        ctx.wait("g", "cc", cc_at)
        ctx.run("g", lambda E_: E_.memset(hal[:], 0.0), inc="g")
        ctx.wait("g", "g")
        for j in range(nhalo):
            slot = 3 - nhalo + j
            ctx.run("g", lambda E_, j=j, slot=slot: E_.indirect_dma_start(
                out=hal[:, slot * D:(slot + 1) * D],
                out_offset=None, in_=ha[j].ap(),
                in_offset=bass.IndirectOffsetOnAxis(ap=hidx[:], axis=0)),
                inc="gd", n=16)
        ctx.wait("g", "gd")

        if is_attn:
          with ExitStack() as pstack:
            PE = pstack.enter_context
            wqkv = PE(nc.sbuf_tensor(f"wqkv_sb{ph}", [128, 8 * 3072], BF16))
            wosb = PE(nc.sbuf_tensor(f"wosb{ph}", [128, 8 * D], BF16))
            povd = [PE(nc.psum_tensor(f"pov{ph}_{j}", [128, 64], F32))
                    for j in range(2)]
            xa2 = PE(nc.sbuf_tensor(f"xa2_{ph}", [128, D], F32))
            xab = [xa, xa2]
            prowfs = [PE(nc.sbuf_tensor(f"prowf{ph}_{j}", [1, 512], F32))
                      for j in range(2)]
            wq_t = wqkv_in[li].rearrange("(c p) f -> c p f", p=128)
            for c in range(8):
                ctx.dma("g", wqkv[:, c * 3072:(c + 1) * 3072], wq_t[c])
            wo_t = wo_in[li].rearrange("(c p) f -> c p f", p=128)
            for c in range(8):
                ctx.dma("g", wosb[:, c * D:(c + 1) * D], wo_t[c])
            bcast_row(qs_in[li:li + 1, :], qsbc[:, 0:64], 64)
            bcast_row(ks_in[li:li + 1, :], ksbc[:, 0:64], 64)
            ctx.cp(qsr[:, 0:32], qsbc[:, 32:64])
            ctx.cp(qsr[:, 32:64], qsbc[:, 0:32])
            ctx.cp(ksr[:, 0:32], ksbc[:, 32:64])
            ctx.cp(ksr[:, 32:64], ksbc[:, 0:32])
            ctx.wait("t", "gd")
            ctx.wait("g", "v")
            ctx.dma("g", prowf[:], hal[127:128, 512:1024])
            ctx.wait("q", "gd")
            ctx.wait("v", "gd")
            ctx.ts(prowf[:], prowf[:], hv[0:1, 0:1], OP.mult)

            kTr3 = kTr.ap().rearrange("p (hp g) -> p hp g", g=512)

            def pair_transposes(src_bf, dst_of_round, vwait=None):
                for rnd in range(2):
                    pb = rnd % 2
                    ctx.wait("t", "v", vwait)
                    for j in range(4):
                        hp = rnd * 4 + j
                        t_at = ctx.transpose(ptpd[pb][:, 128 * j:128 * (j + 1)],
                                             src_bf[:, 128 * hp:128 * (hp + 1)],
                                             ident[0:128, 0:128])
                    ctx.wait("v", "t", t_at)
                    ctx.cp(dst_of_round(rnd),
                           ptpd[pb].ap().rearrange("p (a f) -> p a f", f=128))

            rec = {}
            for e in range(1, 19):
                toff = (e) * 64
                ctx.wait("v", "qd")
                ctx.wait("v", "gd")
                ctx.tt(AB[:, 0:64], tabs[:, toff:toff + 64], qsbc[:, 0:64], OP.mult)
                ctx.tt(AB[:, 64:128], tabs[:, 19 * 64 + toff:19 * 64 + toff + 64],
                       qsr[:, 0:64], OP.mult)
                ctx.tt(AB[:, 128:192], tabs[:, 2 * 19 * 64 + toff:2 * 19 * 64 + toff + 64],
                       ksbc[:, 0:64], OP.mult)
                ctx.tt(AB[:, 192:256], tabs[:, 3 * 19 * 64 + toff:3 * 19 * 64 + toff + 64],
                       ksr[:, 0:64], OP.mult)
                xcur = xa
                if e >= 3:
                    i0 = e - 3
                    ctx.wait("q", "v")
                    ctx.wait("q", "s")
                    ctx.wait("q", "od")
                    ctx.wait("q", "gd")
                    ctx.dma("q", xa[:], src_t[i0])
                    ctx.dma("q", sh[:, 0:512], src_t[i0][:, 0:512])
                    ctx.dma("q", sh[1:128, 512:1024], src_t[i0][0:127, 512:1024])
                    if i0 >= 1:
                        ctx.dma("q", sh[0:1, 512:1024],
                                src_t[i0 - 1][127:128, 512:1024])
                    else:
                        ctx.dma("q", sh[0:1, 512:1024], prowf[:])
                    ctx.wait("v", "qd")
                    ctx.wait("t", "qd")
                    ctx.wait("s", "qd")
                else:
                    emit_shift(e, None, from_halo=True)
                ctx.cp(shbf[:], sh[:])
                ctx.wait("t", "v")
                emit_transpose_tile8(ctx, shbf[:], shT[:], ident, ptpd[0])
                slot = e % 4
                ctx.wait("t", "v")
                ctx.wait("t", "s")
                for f in range(2):
                    for c in range(8):
                        ctx.mm(psc[:, 512 * f:512 * (f + 1)],
                               shT[:, 128 * c:128 * (c + 1)],
                               wqkv[:, c * 3072 + 1024 + 512 * f:c * 3072 + 1024 + 512 * (f + 1)],
                               start=(c == 0), stop=(c == 7), inc=(c == 7))
                kmm_t = ctx.cnt["t"]
                for f in range(2):
                    for c in range(8):
                        ctx.mm(psc[:, 1024 + 512 * f:1536 + 512 * f],
                               shT[:, 128 * c:128 * (c + 1)],
                               wqkv[:, c * 3072 + 2048 + 512 * f:c * 3072 + 2048 + 512 * (f + 1)],
                               start=(c == 0), stop=(c == 7), inc=(c == 7))
                vmm_t = ctx.cnt["t"]
                ctx.wait("v", "t", kmm_t)
                ctx.cp(kf[:], psc[:, 0:1024])
                kfcp_v = ctx.cnt["v"]
                if e >= 3:
                    ctx.wait("t", "v", kfcp_v)
                    for f in range(2):
                        for c in range(8):
                            ctx.mm(psc[:, 512 * f:512 * (f + 1)],
                                   shT[:, 128 * c:128 * (c + 1)],
                                   wqkv[:, c * 3072 + 512 * f:c * 3072 + 512 * (f + 1)],
                                   start=(c == 0), stop=(c == 7), inc=(c == 7))
                    qmm_t = ctx.cnt["t"]
                l2norm_rope(128, rope)
                ropek_v = ctx.cnt["v"]
                ctx.wait("v", "t", vmm_t)
                ctx.cp(vr[:, slot * D:(slot + 1) * D], psc[:, 1024:2048])
                if e >= 3:
                    ctx.wait("v", "t", qmm_t)
                    ctx.cp(kf[:], psc[:, 0:1024])
                    l2norm_rope(0, yT)
                pair_transposes(rope[:], lambda rnd: kTr3[
                    :, rnd * 4:rnd * 4 + 4, slot * 128:slot * 128 + 128],
                    vwait=ropek_v)
                if e < 3:
                    continue
                i = e - 3
                pair_transposes(yT[:], lambda rnd: qT.ap().rearrange(
                    "p (a f) -> p a f", f=128)[:, rnd * 4:rnd * 4 + 4, :],
                    vwait=ctx.cnt["v"])
                setup_v = ctx.cnt["v"]
                mslot = i if i < 2 else 2
                moff = mslot * 1024 + ((4 - (i % 4)) % 4) * 128
                for h in range(16):
                    b = h % 4
                    pb = h % 2
                    hb = (h % 2) * 64
                    hp_ = (h // 2) * 128
                    if h < 4:
                        ctx.wait("t", "v", setup_v)
                    else:
                        ctx.wait("t", "v", rec["vdone", h - 4])
                        ctx.wait("t", "s", rec["sdone", h - 4])
                    if h >= 2:
                        ctx.wait("t", "v", rec["scale", h - 2])
                        ctx.wait("t", "v", rec["ptev", h - 2])
                    sc_t = ctx.mm(psc[:, b * 512:(b + 1) * 512],
                                  qT[hb:hb + 64, hp_:hp_ + 128],
                                  kTr[hb:hb + 64, (h // 2) * 512:(h // 2) * 512 + 512],
                                  start=True, stop=True, inc=True)
                    ctx.wait("v", "t", sc_t)
                    ctx.tt(psc[:, b * 512:(b + 1) * 512],
                           psc[:, b * 512:(b + 1) * 512],
                           masks[:, moff:moff + 512], OP.add)
                    ctx.red(mcol[:, 4 + b:5 + b], psc[:, b * 512:(b + 1) * 512],
                            OP.max, negate=True)
                    rec["vdone", h] = ctx.cnt["v"]
                    ctx.wait("s", "v", rec["vdone", h])
                    if h >= 2:
                        ctx.wait("s", "t", rec["trdone", h - 2])
                    ctx.act(Pbf[:, pb * 512:(pb + 1) * 512],
                            psc[:, b * 512:(b + 1) * 512], AF.Exp,
                            bias=mcol[:, 4 + b:5 + b], accum=mcol[:, 8 + b:9 + b])
                    rec["sdone", h] = ctx.cnt["s"]
                    ctx.wait("v", "s", rec["sdone", h])
                    ctx.run("v", lambda E_, b=b: E_.reciprocal(
                        out=mcol[:, 12 + b:13 + b], in_=mcol[:, 8 + b:9 + b]), inc="v")
                    ctx.wait("t", "s", rec["sdone", h])
                    for s in range(4):
                        t_at = ctx.transpose(
                            ptpd[pb][:, 128 * s:128 * (s + 1)],
                            Pbf[:, pb * 512 + 128 * s:pb * 512 + 128 * (s + 1)],
                            ident[0:128, 0:128])
                    rec["trdone", h] = ctx.cnt["t"]
                    ctx.wait("v", "t", rec["trdone", h])
                    ctx.cp(PT[:, pb * 512:(pb + 1) * 512], ptpd[pb][:])
                    rec["ptev", h] = ctx.cnt["v"]
                    ctx.wait("t", "v", rec["ptev", h])
                    for s in range(4):
                        ctx.mm(povd[pb][:],
                               PT[:, pb * 512 + 128 * s:pb * 512 + 128 * (s + 1)],
                               vr[:, s * D + 64 * h:s * D + 64 * (h + 1)],
                               start=(s == 0), stop=(s == 3), inc=(s == 3))
                    pv_t = ctx.cnt["t"]
                    ctx.wait("v", "t", pv_t)
                    ctx.ts(ytile[:, 64 * h:64 * (h + 1)], povd[pb][:, 0:64],
                           mcol[:, 12 + b:13 + b], OP.mult)
                    rec["scale", h] = ctx.cnt["v"]
                ctx.wait("t", "v")
                ctx.wait("t", "s")
                emit_transpose_tile8(ctx, ytile[:], yT[:], ident, ptpd[0])
                ctx.wait("t", "v")
                for f in range(2):
                    for c in range(8):
                        ctx.mm(psc[:, 512 * f:512 * (f + 1)],
                               yT[:, 128 * c:128 * (c + 1)],
                               wosb[:, c * D + 512 * f:c * D + 512 * (f + 1)],
                               start=(c == 0), stop=(c == 7), inc=(c == 7))
                at = ctx.cnt["t"]
                ctx.wait("v", "t", at)
                ctx.tt(xcur[:], xcur[:], psc[:, 0:1024], OP.add)
                ctx.wait("g", "v")
                ctx.dma("g", dst_t[i], xcur[:], inc="od")

        else:
          with ExitStack() as pstack:
            PE = pstack.enter_context
            xnT_all = PE(nc.sbuf_tensor(f"xnT_all{ph}", [128, NT * D], BF16))
            yacc = PE(nc.sbuf_tensor(f"yacc{ph}", [128, NT * D], BF16))
            w1a = [PE(nc.sbuf_tensor(f"w1a{ph}_{j}", [128, 8 * 512], BF16))
                   for j in range(2)]
            w1g = [PE(nc.sbuf_tensor(f"w1g{ph}_{j}", [128, 8 * 512], BF16))
                   for j in range(2)]
            w2c = PE(nc.sbuf_tensor(f"w2c{ph}", [128, 4 * D], BF16))
            b1r = PE(nc.sbuf_tensor(f"b1r{ph}", [1, 5460], BF16))
            b2bc = PE(nc.sbuf_tensor(f"b2bc{ph}", [128, D], BF16))
            pw2 = PE(nc.psum_tensor(f"pw2_{ph}", [128, 1024], F32))
            bcast_row(lng_in[li:li + 1, :], gbc[:], D)
            bcast_row(lnb_in[li:li + 1, :], bbc[:], D)
            ctx.wait("g", "v")
            ctx.dma("g", b1r[:], b1_in[li:li + 1, :])
            ctx.wait("g", "gd")
            ctx.dma("g", prowf[:], hal[127:128, 2 * D + 512:3 * D])
            ctx.wait("v", "gd")
            ctx.ts(prowf[:], prowf[:], hv[0:1, 0:1], OP.mult)
            bcast_row(b2_in[li:li + 1, :], b2bc[:], D)
            ctx.wait("q", "gd")
            for i in range(NT):
                ctx.wait("q", "v")
                ctx.wait("q", "s")
                ctx.wait("q", "od")
                ctx.wait("q", "gd")
                ctx.dma("q", sh[:, 0:512], src_t[i][:, 0:512])
                ctx.dma("q", sh[1:128, 512:1024], src_t[i][0:127, 512:1024])
                if i >= 1:
                    ctx.dma("q", sh[0:1, 512:1024],
                            src_t[i - 1][127:128, 512:1024])
                else:
                    ctx.dma("q", sh[0:1, 512:1024], prowf[:])
                ctx.wait("v", "qd")
                ctx.wait("s", "qd")
                emit_ln(ctx, sh[:], shbf[:], kf[:], mcol, gain_bc=gbc[:],
                        bias_bc=bbc[:])
                ctx.wait("t", "v")
                emit_transpose_tile8(ctx, shbf[:], xnT_all[:, i * D:(i + 1) * D],
                                     ident, ptp)
                ctx.wait("v", "t")
                ctx.cp(yacc[:, i * D:(i + 1) * D], b2bc[:])
            fcs = [(0, 512), (512, 512), (1024, 512), (1536, 512), (2048, 512),
                   (2560, 170)]
            w1_t = w1_in[li].rearrange("(c p) f -> c p f", p=128)
            frec = {}

            def load_w1(ci):
                fcb, fcw = fcs[ci]
                cb = ci % 2
                if ci >= 2:
                    ctx.wait("g", "t", frec["w2t", ci - 2])
                for c in range(8):
                    ctx.dma("g", w1a[cb][:, c * 512:c * 512 + fcw],
                            w1_t[c, :, fcb:fcb + fcw])
                    ctx.dma("g", w1g[cb][:, c * 512:c * 512 + fcw],
                            w1_t[c, :, 2730 + fcb:2730 + fcb + fcw])

            def load_w2(ci):
                fcb, fcw = fcs[ci]
                if ci >= 1:
                    ctx.wait("g", "t", frec["w2t", ci - 1])
                for c in range((fcw + 127) // 128):
                    rr0 = fcb + c * 128
                    rw = min(128, fcb + fcw - rr0)
                    ctx.dma("g", w2c[0:rw, c * D:(c + 1) * D],
                            w2_in[li, rr0:rr0 + rw, :], inc="wd")

            ctx.wait("g", "t")
            load_w1(0)
            load_w2(0)
            for ci, (fcb, fcw) in enumerate(fcs):
                cb = ci % 2
                nw2 = (fcw + 127) // 128
                ctx.wait("t", "gd")
                if ci + 1 < len(fcs):
                    load_w1(ci + 1)
                if ci == 0:
                    ctx.wait("t", "v")
                    ctx.wait("t", "s")
                for i in range(NT):
                    gt = ci * NT + i
                    wb = (gt % 2) * 1024
                    if ("sg", gt - 2) in frec:
                        ctx.wait("t", "s", frec["sg", gt - 2])
                        ctx.wait("t", "v", frec["av", gt - 2])
                    ctx.mm(psc[:, wb:wb + fcw], ones[:], b1r[:, fcb:fcb + fcw],
                           start=True, stop=False)
                    ctx.mm(psc[:, wb + 512:wb + 512 + fcw], ones[:],
                           b1r[:, 2730 + fcb:2730 + fcb + fcw],
                           start=True, stop=False)
                    for c in range(8):
                        ctx.mm(psc[:, wb:wb + fcw],
                               xnT_all[:, i * D + 128 * c:i * D + 128 * (c + 1)],
                               w1a[cb][:, c * 512:c * 512 + fcw],
                               start=False, stop=False)
                        ctx.mm(psc[:, wb + 512:wb + 512 + fcw],
                               xnT_all[:, i * D + 128 * c:i * D + 128 * (c + 1)],
                               w1g[cb][:, c * 512:c * 512 + fcw],
                               start=False, stop=(c == 7), inc=(c == 7))
                    w1t = ctx.cnt["t"]
                    ctx.wait("s", "t", w1t)
                    ctx.act(tf2[:, 0:fcw], psc[:, wb + 512:wb + 512 + fcw],
                            AF.Gelu)
                    frec["sg", gt] = ctx.cnt["s"]
                    ab = (gt % 2) * 512
                    ctx.wait("v", "s", frec["sg", gt])
                    ctx.wait("v", "t", w1t)
                    ctx.tt(agbf[:, ab:ab + fcw], psc[:, wb:wb + fcw],
                           tf2[:, 0:fcw], OP.mult)
                    frec["av", gt] = ctx.cnt["v"]
                    ctx.wait("t", "v", frec["av", gt])
                    for c in range(nw2):
                        cw = min(128, fcw - c * 128)
                        ctx.transpose(ptpd[gt % 2][0:cw, 128 * c:128 * (c + 1)],
                                      agbf[:, ab + 128 * c:ab + 128 * c + cw],
                                      ident[0:128, 0:128])
                    trt = ctx.cnt["t"]
                    ctx.wait("v", "t", trt)
                    ctx.cp(agT[:, 0:128 * nw2], ptpd[gt % 2][:, 0:128 * nw2])
                    agv = ctx.cnt["v"]
                    ctx.wait("t", "v", agv)
                    ctx.wait("t", "wd")
                    for f in range(2):
                        for c in range(nw2):
                            cw = min(128, fcw - c * 128)
                            ctx.mm(pw2[:, 512 * f:512 * (f + 1)],
                                   agT[0:cw, 128 * c:128 * (c + 1)],
                                   w2c[0:cw, c * D + 512 * f:c * D + 512 * (f + 1)],
                                   start=(c == 0), stop=(c == nw2 - 1),
                                   inc=(c == nw2 - 1))
                    frec["w2t", ci] = ctx.cnt["t"]
                    ctx.wait("v", "t", frec["w2t", ci])
                    ctx.tt(yacc[:, i * D:(i + 1) * D], yacc[:, i * D:(i + 1) * D],
                           pw2[:, 0:1024], OP.add)
                    if i == NT - 1 and ci + 1 < len(fcs):
                        load_w2(ci + 1)
            for i in range(NT):
                ctx.wait("q", "v")
                ctx.wait("q", "od")
                ctx.dma("q", xa[:], src_t[i])
                ctx.wait("v", "qd")
                ctx.tt(xa[:], xa[:], yacc[:, i * D:(i + 1) * D], OP.add)
                ctx.wait("g", "v")
                ctx.dma("g", dst_t[i], xa[:], inc="od")

    barrier()
    xlast = xd[(nph - 1) % 2].ap().rearrange("(n p) d -> n p d", p=128)
    for i in range(NT):
        ctx.wait("q", "v")
        ctx.wait("q", "od")
        ctx.dma("q", xa[:], xin_t[i])
        ctx.dma("q", sh[:], xlast[i, :, :])
        ctx.wait("v", "qd")
        ctx.tt(xa[:], xa[:], sh[:], OP.add)
        ctx.wait("g", "v")
        ctx.dma("g", y_out.rearrange("(n p) d -> n p d", p=128)[i, :, :], xa[:],
                inc="od")




def build_L3f(dbg=False, nph=12):
    nc = bass.Bass("TRN2", target_bir_lowering=False, debug=False)
    x_in = nc.dram_tensor("x", [TOK, D], F32, kind="ExternalInput").ap()
    y_out = nc.dram_tensor("y", [TOK, D], F32, kind="ExternalOutput").ap()
    ctx = Ctx(nc)
    with ExitStack() as stack:
        ctx.alloc_sems(stack)
        ctx.ident, ctx.ones = emit_prelude(ctx, stack)
        ctx.scope = "f3_"
        emit_L3f(ctx, stack, x_in, y_out, dbg, nph)
        ctx.wait("q", "od", ctx.cnt["od"])
        ctx.flush()
    return nc

def run_L3f(x_full, INP, nc=None):
    nc = nc or build_L3f()
    xs = _shard_x(x_full)
    DIM_HEAD, WINDOW = 64, 256
    half = np.arange(0, DIM_HEAD, 2, dtype=np.float64)
    inv_freq = 1.0 / 10000.0 ** (half / DIM_HEAD)
    base_scale = (half + 0.4 * DIM_HEAD) / (1.4 * DIM_HEAD)

    def tables(pos):
        t = pos.astype(np.float64)
        freqs = t[:, None] * inv_freq[None, :]
        freqs = np.concatenate([freqs, freqs], -1)
        power = (t - 16384 // 2) / (WINDOW // 2)
        scale = base_scale[None, :] ** power[:, None]
        scale = np.concatenate([scale, scale], -1)
        cos, sin = np.cos(freqs), np.sin(freqs)
        return (cos * scale * 8.0, sin * scale * 8.0, cos / scale, sin / scale)

    def win_mask(i, first_block):
        m = np.full((128, 512), -1e30, np.float32)
        for p in range(128):
            qa = 384 + p
            lo = qa - 256
            if first_block and i < 2:
                lo = max(lo, 384 - 128 * i)
            a = np.arange(512)
            ok = (a >= lo) & (a <= qa)
            m[p, ok] = 0.0
        return m

    res_maps = []
    common = dict(
        wqkv=np.ascontiguousarray(INP["L3_ffn_Wqkv"], np.float32),
        wol=np.ascontiguousarray(INP["L3_ffn_Wo"], np.float32),
        w1l=np.ascontiguousarray(INP["L3_ffn_W1"], np.float32),
        b1l=np.ascontiguousarray(INP["L3_ffn_b1"], np.float32),
        w2l=np.ascontiguousarray(INP["L3_ffn_W2"], np.float32),
        b2l=np.ascontiguousarray(INP["L3_ffn_b2"], np.float32),
        qsl=np.ascontiguousarray(INP["L3_ffn_q_scale"], np.float32),
        ksl=np.ascontiguousarray(INP["L3_ffn_k_scale"], np.float32),
        lngl=np.ascontiguousarray(INP["L3_ffn_ln_g"], np.float32),
        lnbl=np.ascontiguousarray(INP["L3_ffn_ln_b"], np.float32),
    )
    for r in range(NCORES):
        base = r * TOK
        cq = np.zeros((128, 19 * 64), np.float32)
        sq = np.zeros((128, 19 * 64), np.float32)
        ck = np.zeros((128, 19 * 64), np.float32)
        sk = np.zeros((128, 19 * 64), np.float32)
        for e in range(19):
            pos = base - 384 + e * 128 + np.arange(128)
            pos = np.maximum(pos, 0)
            a, b, c, d = tables(pos)
            cq[:, e * 64:(e + 1) * 64] = a
            sq[:, e * 64:(e + 1) * 64] = b
            ck[:, e * 64:(e + 1) * 64] = c
            sk[:, e * 64:(e + 1) * 64] = d
        mk = np.zeros((4, 128, 512), np.float32)
        mk[0] = win_mask(0, r == 0)
        mk[1] = win_mask(1, r == 0)
        mk[2] = win_mask(2, False)
        mk[3] = win_mask(3, False)
        mkd = np.zeros((4, 128, 8, 128), np.float32)
        for t in range(8):
            mkd[:, :, t, :] = mk[:, :, (t % 4) * 128:(t % 4) * 128 + 128]
        hidx = np.full((128, 1), 0, np.uint32)
        if r > 0:
            hidx[:, 0] = (r - 1) * 128 + np.arange(128)
        res_maps.append(dict(x=xs[r], hv=np.full((1, 1), 0.0 if r == 0 else 1.0, np.float32),
                             tabcq=cq, tabsq=sq, tabck=ck, tabsk=sk,
                             masks=np.ascontiguousarray(mkd.transpose(1, 0, 2, 3).reshape(128, 4096)),
                             hidx=hidx, **common))
    res = run_bass_kernel_spmd(nc, _prefix_maps(res_maps, "f3_"), core_ids=list(range(NCORES)))
    return np.concatenate([res.results[r]["y"] for r in range(NCORES)], 0)




def _prefix_maps(maps, pre):
    return [{(k if k == "x" else pre + k): v for k, v in m.items()} for m in maps]


def _shard_x(x):
    x = np.ascontiguousarray(x.reshape(SEQ, D), dtype=np.float32)
    return [x[r * TOK:(r + 1) * TOK] for r in range(NCORES)]


def run_L0f(x_full, w, b):
    nc = build_L0f()
    xs = _shard_x(x_full)
    maps = [dict(x=xs[r], w=np.ascontiguousarray(w, np.float32),
                 b=np.ascontiguousarray(b.reshape(1, D), np.float32))
            for r in range(NCORES)]
    res = run_bass_kernel_spmd(nc, _prefix_maps(maps, "f0_"), core_ids=list(range(NCORES)))
    return np.concatenate([res.results[r]["y"] for r in range(NCORES)], 0)




def prep_L0f(INP):
    m = dict(w=np.ascontiguousarray(INP["L0_ffn_weight"], np.float32),
             b=np.ascontiguousarray(np.asarray(INP["L0_ffn_bias"]).reshape(1, D), np.float32))
    return [m for _ in range(NCORES)]


def prep_L0a(INP):
    T = lambda a: np.ascontiguousarray(np.asarray(a, np.float32).T)
    R1 = lambda a: np.ascontiguousarray(np.asarray(a, np.float32).reshape(1, -1))
    common = dict(
        qkvuT=T(INP["L0_attn_qkvu_w"]), qkvub=R1(INP["L0_attn_qkvu_b"]),
        outwT=T(INP["L0_attn_out_w"]), outb=R1(INP["L0_attn_out_b"]),
        slope=R1(INP["L0_attn_slope_rate"]), theta=R1(INP["L0_attn_theta"]),
        trilT=np.ascontiguousarray(np.triu(np.ones((128, 128)), 0), np.float32),
    )
    out = []
    for r in range(NCORES):
        tok = np.arange(TOK, dtype=np.float32) + r * TOK
        tokcol = np.ascontiguousarray(tok.reshape(NT, 128).T)
        wprev = np.zeros((128, 8), np.float32)
        wprev[:, :r] = 1.0
        out.append(dict(tokcol=tokcol, wprev=wprev, **common))
    return out


def prep_L1a(INP):
    T = lambda a: np.ascontiguousarray(np.asarray(a, np.float32).T)
    R1 = lambda a: np.ascontiguousarray(np.asarray(a, np.float32).reshape(1, -1))
    common = dict(
        wkT=T(INP["L1_attn_Wk"]), wvT=T(INP["L1_attn_Wv"]), wqT=T(INP["L1_attn_Wq"]),
        wgT=T(INP["L1_attn_Wg"]), woT=T(INP["L1_attn_Wo"]),
        w0=np.ascontiguousarray(INP["L1_attn_W0"], np.float32),
        ainit=np.ascontiguousarray(INP["L1_attn_A_init"], np.float32),
        binit=np.ascontiguousarray(INP["L1_attn_B_init"], np.float32),
        lng=R1(INP["L1_attn_ln_g"]), lnb=R1(INP["L1_attn_ln_b"]),
        loglr=R1(INP["L1_attn_log_lr"]), lsg=R1(INP["L1_attn_ls_g"]),
        strictLT=np.ascontiguousarray(np.triu(np.ones((128, 128)), 1), np.float32),
    )
    out = []
    for r in range(NCORES):
        pos = (np.arange(TOK) + r * TOK + 1).astype(np.float64)
        ps = (1.0 / (1.0 + 0.1 * np.log(pos))).astype(np.float32)
        poscol = np.ascontiguousarray(ps.reshape(NT, 128).T)
        maskp = np.zeros((8, 1), np.float32)
        maskp[:r] = 1.0
        out.append(dict(poscol=poscol, maskprev=maskp, **common))
    return out


def prep_L1f(INP):
    T = lambda a: np.ascontiguousarray(np.asarray(a, np.float32).T)
    m = dict(
        rmsw=np.ascontiguousarray(np.asarray(INP["L1_ffn_rms_w"]).reshape(1, D), np.float32),
        gwT=T(INP["L1_ffn_gate_w"]),
        w1=np.ascontiguousarray(INP["L1_ffn_W1"], np.float32),
        b1=np.ascontiguousarray(INP["L1_ffn_b1"], np.float32),
        w2=np.ascontiguousarray(INP["L1_ffn_W2"], np.float32),
        b2=np.ascontiguousarray(INP["L1_ffn_b2"], np.float32),
    )
    return [m for _ in range(NCORES)]


def prep_L2a(INP):
    wq = np.asarray(INP["L2_attn_wq"], np.float32)
    wkd = np.asarray(INP["L2_attn_wk_down"], np.float32)
    wvd = np.asarray(INP["L2_attn_wv_down"], np.float32)
    wkup = np.asarray(INP["L2_attn_wk_up"], np.float32)
    wvup = np.asarray(INP["L2_attn_wv_up"], np.float32)
    wo = np.asarray(INP["L2_attn_wo"], np.float32)
    m4 = np.zeros((128, 4, 512), np.float32)
    qi = np.arange(128)[:, None]
    for o in range(4):
        m4[:, o, (o + 1) * 128:] = -1e30
        ki = np.arange(128)[None, :]
        m4[:, o, o * 128:(o + 1) * 128] = np.where(ki <= qi, 0.0, -1e30)
    trl = np.where(np.arange(128)[:, None] <= np.arange(128)[None, :], 0.0,
                   -1e30).astype(np.float32)
    out = []
    for r in range(NCORES):
        kv = r // 4
        out.append(dict(
            wq_slice=np.ascontiguousarray(wq[:, r * 128:(r + 1) * 128]),
            wkd_slice=np.ascontiguousarray(wkd[:, kv * 32:(kv + 1) * 32]),
            wvd_slice=np.ascontiguousarray(wvd[:, kv * 32:(kv + 1) * 32]),
            wkupT=np.ascontiguousarray(wkup.T), wvup=np.ascontiguousarray(wvup),
            wo_slice=np.ascontiguousarray(wo[r * 128:(r + 1) * 128, :]),
            mask4=np.ascontiguousarray(m4.reshape(128, 2048)), trilT2=trl))
    return out


def prep_L2f(INP):
    T = lambda a: np.ascontiguousarray(np.asarray(a, np.float32).T)
    SC = lambda a, n: np.ascontiguousarray(np.asarray(a, np.float32).reshape(n, 128).T)
    scl = np.concatenate([SC(INP["L2_ffn_w1_s"], 32), SC(INP["L2_ffn_w2_s"], 32),
                          SC(INP["L2_ffn_h1_s"], 32), SC(INP["L2_ffn_h2_s"], 32),
                          SC(INP["L2_ffn_w3_s"], 8)], axis=1)
    m = dict(w1T=T(INP["L2_ffn_w1_w"]), w2T=T(INP["L2_ffn_w2_w"]),
             h1T=T(INP["L2_ffn_h1_w"]), h2T=T(INP["L2_ffn_h2_w"]),
             w3T=T(INP["L2_ffn_w3_w"]), scl=np.ascontiguousarray(scl))
    return [m for _ in range(NCORES)]


def prep_L3a(INP):
    T = lambda a: np.ascontiguousarray(np.asarray(a, np.float32).T)
    R = lambda a, s: np.ascontiguousarray(np.asarray(a, np.float32).reshape(s))
    m = dict(
        wvT=T(INP["L3_attn_Wv"]), w1T=T(INP["L3_attn_W1"]),
        w2T=T(INP["L3_attn_W2"]), woT=T(INP["L3_attn_Wo"]),
        sigT=T(INP["L3_attn_head_signatures"]),
        bv=R(INP["L3_attn_bv"], (1, D)), b1=R(INP["L3_attn_b1"], (1, D)),
        b2=R(INP["L3_attn_b2"], (1, 64)), bo=R(INP["L3_attn_bo"], (1, D)),
    )
    return [m for _ in range(NCORES)]


def prep_L3f(INP):
    DIM_HEAD, WINDOW = 64, 256
    half = np.arange(0, DIM_HEAD, 2, dtype=np.float64)
    inv_freq = 1.0 / 10000.0 ** (half / DIM_HEAD)
    base_scale = (half + 0.4 * DIM_HEAD) / (1.4 * DIM_HEAD)

    def tables(pos):
        t = pos.astype(np.float64)
        freqs = t[:, None] * inv_freq[None, :]
        freqs = np.concatenate([freqs, freqs], -1)
        power = (t - 16384 // 2) / (WINDOW // 2)
        scale = base_scale[None, :] ** power[:, None]
        scale = np.concatenate([scale, scale], -1)
        cos, sin = np.cos(freqs), np.sin(freqs)
        return (cos * scale * 8.0, sin * scale * 8.0, cos / scale, sin / scale)

    def win_mask(i, first_block):
        m = np.full((128, 512), -1e30, np.float32)
        for p in range(128):
            qa = 384 + p
            lo = qa - 256
            if first_block and i < 2:
                lo = max(lo, 384 - 128 * i)
            a = np.arange(512)
            ok = (a >= lo) & (a <= qa)
            m[p, ok] = 0.0
        return m

    common = dict(
        wqkv=np.ascontiguousarray(INP["L3_ffn_Wqkv"], np.float32),
        wol=np.ascontiguousarray(INP["L3_ffn_Wo"], np.float32),
        w1l=np.ascontiguousarray(INP["L3_ffn_W1"], np.float32),
        b1l=np.ascontiguousarray(INP["L3_ffn_b1"], np.float32),
        w2l=np.ascontiguousarray(INP["L3_ffn_W2"], np.float32),
        b2l=np.ascontiguousarray(INP["L3_ffn_b2"], np.float32),
        qsl=np.ascontiguousarray(INP["L3_ffn_q_scale"], np.float32),
        ksl=np.ascontiguousarray(INP["L3_ffn_k_scale"], np.float32),
        lngl=np.ascontiguousarray(INP["L3_ffn_ln_g"], np.float32),
        lnbl=np.ascontiguousarray(INP["L3_ffn_ln_b"], np.float32),
    )
    out = []
    for r in range(NCORES):
        base = r * TOK
        cq = np.zeros((128, 19 * 64), np.float32)
        sq = np.zeros((128, 19 * 64), np.float32)
        ck = np.zeros((128, 19 * 64), np.float32)
        sk = np.zeros((128, 19 * 64), np.float32)
        for e in range(19):
            pos = base - 384 + e * 128 + np.arange(128)
            pos = np.maximum(pos, 0)
            a, b, c, d = tables(pos)
            cq[:, e * 64:(e + 1) * 64] = a
            sq[:, e * 64:(e + 1) * 64] = b
            ck[:, e * 64:(e + 1) * 64] = c
            sk[:, e * 64:(e + 1) * 64] = d
        mk = np.zeros((4, 128, 512), np.float32)
        mk[0] = win_mask(0, r == 0)
        mk[1] = win_mask(1, r == 0)
        mk[2] = win_mask(2, False)
        mk[3] = win_mask(3, False)
        mkd = np.zeros((4, 128, 8, 128), np.float32)
        for t in range(8):
            mkd[:, :, t, :] = mk[:, :, (t % 4) * 128:(t % 4) * 128 + 128]
        hidx = np.full((128, 1), 0, np.uint32)
        if r > 0:
            hidx[:, 0] = (r - 1) * 128 + np.arange(128)
        out.append(dict(hv=np.full((1, 1), 0.0 if r == 0 else 1.0, np.float32),
                        tabcq=cq, tabsq=sq, tabck=ck, tabsk=sk,
                        masks=np.ascontiguousarray(mkd.transpose(1, 0, 2, 3).reshape(128, 4096)),
                        hidx=hidx, **common))
    return out


FUSE_PLAN = [("a0_", emit_L0a, prep_L0a), ("f0_", emit_L0f, prep_L0f),
             ("a1_", emit_L1a, prep_L1a), ("f1_", emit_L1f, prep_L1f),
             ("a2_", emit_L2a, prep_L2a), ("f2_", emit_L2f, prep_L2f),
             ("a3_", emit_L3a, prep_L3a), ("f3_", emit_L3f, prep_L3f)]


def build_fused():
    nc = bass.Bass("TRN2", target_bir_lowering=False, debug=False)
    x_in = nc.dram_tensor("x", [TOK, D], F32, kind="ExternalInput").ap()
    y_out = nc.dram_tensor("y", [TOK, D], F32, kind="ExternalOutput").ap()
    chain = [nc.dram_tensor(f"chain{b}", [TOK, D], F32) for b in range(2)]
    ctx = Ctx(nc)
    with ExitStack() as stack:
        ctx.alloc_sems(stack)
        ctx.ident, ctx.ones = emit_prelude(ctx, stack)
        nblk = len(FUSE_PLAN)
        for i, (pre, fn, _) in enumerate(FUSE_PLAN):
            src = x_in if i == 0 else chain[(i - 1) % 2].ap()
            dst = y_out if i == nblk - 1 else chain[i % 2].ap()
            ctx.scope = pre
            with ExitStack() as bstack:
                fn(ctx, bstack, src, dst)
                for e in ENGS:
                    for k in SEMS:
                        ctx.wait(e, k)
        ctx.wait("q", "od", ctx.cnt["od"])
        ctx.flush()
    return nc


_FUSED_CACHE = {}


def kernel_fused(**inputs):
    if "nc" not in _FUSED_CACHE:
        _FUSED_CACHE["nc"] = build_fused()
    nc = _FUSED_CACHE["nc"]
    x = np.ascontiguousarray(np.asarray(inputs["L0_attn_x"], np.float32).reshape(SEQ, D))
    xs = _shard_x(x)
    preps = [(pre, pf(inputs)) for (pre, _, pf) in FUSE_PLAN]
    maps = []
    for r in range(NCORES):
        m = {"x": xs[r]}
        for pre, pm in preps:
            for k, v in pm[r].items():
                m[pre + k] = v
        maps.append(m)
    res = run_bass_kernel_spmd(nc, maps, core_ids=list(range(NCORES)))
    y = np.concatenate([res.results[r]["y"] for r in range(NCORES)], 0)
    return y.reshape(1, SEQ, D)


def kernel(**inputs):
    _ = (inputs["L3_attn_Wq"], inputs["L3_attn_Wk"],
         inputs["L3_attn_bq"], inputs["L3_attn_bk"])
    return kernel_fused(**inputs)
